# version 38
# baseline (speedup 1.0000x reference)
import math
from contextlib import ExitStack

import numpy as np
import ml_dtypes

import concourse.bass as bass
import concourse.mybir as mybir
from concourse.bass_utils import run_bass_kernel_spmd

F32 = mybir.dt.float32
BF16 = mybir.dt.bfloat16
I32 = mybir.dt.int32
AF = mybir.ActivationFunctionType
ALU = mybir.AluOpType
AX = mybir.AxisListType
bfloat16 = ml_dtypes.bfloat16

B, T, D = 2, 4096, 4096
NCORES = 8
NCH = 32
TL = 1024
QG = 512
SC = 1.0 / math.sqrt(128.0)
NEG = -1.0e30
L0_NKC = (16, 32)
L1_NKC = (8, 8)



def _core_meta(c):
    b, cc = divmod(c, 4)
    return b, cc, cc, 7 - cc


def _rank_col(b, m):
    r = 4 * b + (m if m <= 3 else 7 - m)
    col = 0 if m <= 3 else 512
    return r, col


def _sched_pads_for_core(c):
    b, cc, m0, m1 = _core_meta(c)
    sched = np.zeros(128, np.int32)
    pads = np.zeros((128, 64), np.float32)
    idx = 0
    for s, (ms, nent, pcol0) in enumerate(((m0, 4, 0), (m1, 8, 16))):
        chunks = [ms] + list(range(ms - 1, -1, -1))
        for e in range(nent):
            if e < len(chunks):
                mm = chunks[e]
            else:
                mm = ms
                pads[:, pcol0 + 4 * e: pcol0 + 4 * e + 4] = NEG
            r, col = _rank_col(b, mm)
            sched[idx] = 128 * r; sched[idx + 1] = col; sched[idx + 2] = 2 * r + col // 512
            idx += 3
    assert idx == 36
    kbase, vbase = 36, 100
    for s, ms in enumerate((m0, m1)):
        for e, mm in enumerate((ms - 1, ms)):
            if mm < 0:
                mm = ms
                pads[:, 48 + 8 * s: 48 + 8 * s + 4] = NEG
            r, col = _rank_col(b, mm)
            sched[kbase] = r
            sched[kbase + 1] = col // 512
            kbase += 2
            sched[vbase] = 2 * r + col // 512
            vbase += 1
    assert kbase == 44 and vbase == 104
    return sched.reshape(1, 128), pads


def _mask_planes():
    dk = np.arange(128)[:, None]
    ql = np.arange(512)[None, :]
    planes = [(ql >= 128 * r + dk).astype(bfloat16) for r in range(4)]
    planes += [(128 * r + dk > ql).astype(bfloat16) for r in range(4)]
    return np.concatenate(planes, axis=1)


def _host_prep(inputs):
    f = np.float32
    asc = np.ascontiguousarray
    w = {}
    x = np.asarray(inputs["L0_attn_x"], f)
    xT = x.transpose(0, 2, 1)
    per_core_x, scheds, padss = [], [], []
    for c in range(NCORES):
        b, cc, m0, m1 = _core_meta(c)
        per_core_x.append(asc(np.concatenate(
            [xT[b][:, 512 * m0:512 * m0 + 512], xT[b][:, 512 * m1:512 * m1 + 512]],
            axis=1)))
        s, p = _sched_pads_for_core(c)
        scheds.append(s); padss.append(p)

    A0 = np.asarray(inputs["L0_ffn_A"], f)
    w["w_l0ar"] = asc(np.concatenate(
        [A0.reshape(64, D).T, np.asarray(inputs["L0_ffn_router_w"], f).T], axis=1))
    w["w_l0B"] = asc(np.asarray(inputs["L0_ffn_B"], f).transpose(0, 2, 1).reshape(64, D))
    w["w_l0qkv"] = asc(np.concatenate(
        [np.asarray(inputs["L0_attn_wq_w"], f).T,
         np.asarray(inputs["L0_attn_wk_w"], f).T,
         np.asarray(inputs["L0_attn_wv_w"], f).T], axis=1))
    w["b_l0qkv"] = asc(np.concatenate(
        [np.asarray(inputs["L0_attn_wq_b"], f),
         np.asarray(inputs["L0_attn_wk_b"], f),
         np.asarray(inputs["L0_attn_wv_b"], f)]).reshape(4352, 1))
    w["w_l0wo"] = asc(np.asarray(inputs["L0_attn_wo_w"], f).T)
    w["b_l0wo"] = asc(np.asarray(inputs["L0_attn_wo_b"], f).reshape(D, 1))
    w["w_l1qkv"] = asc(np.concatenate(
        [np.asarray(inputs["L1_attn_wq"], f).T,
         np.asarray(inputs["L1_attn_wk"], f).T,
         np.asarray(inputs["L1_attn_wv"], f).T], axis=1))
    w["w_l1wo"] = asc(np.asarray(inputs["L1_attn_wo"], f).T)
    w["w_l1base"] = asc(np.asarray(inputs["L1_ffn_base_w"], f).T)
    w["b_l1base"] = asc(np.asarray(inputs["L1_ffn_base_b"], f).reshape(D, 1))
    a1t = np.asarray(inputs["L1_ffn_A"], f).T
    w["w_l1ar"] = asc(np.concatenate(
        [a1t, a1t, np.asarray(inputs["L1_ffn_router_w"], f).T], axis=1))
    w["w_l1B"] = asc(np.asarray(inputs["L1_ffn_B"], f).transpose(0, 2, 1).reshape(1024, D))
    return w, per_core_x, scheds, padss



class KB:
    def __init__(self, nc, ctx):
        self.nc = nc
        self.ctx = ctx
        self.sems = {}
        self.sb = {}
        self.uid = 0

    def nm(self, base):
        self.uid += 1
        return f"{base}_{self.uid}"

    def sem(self, name):
        if name not in self.sems:
            h = self.ctx.enter_context(self.nc.semaphore(name))
            self.sems[name] = [h, 0]
        return name

    def inc(self, inst, name, k=1):
        s = self.sems[self.sem(name)]
        inst.then_inc(s[0], k)
        s[1] += k
        return inst

    def v(self, name):
        return self.sems[self.sem(name)][1]

    def wait(self, eng, name, val):
        eng.wait_ge(self.sems[self.sem(name)][0], val)

    def vbar(self, eng, sem="vs"):
        self.inc(eng.nop(), sem)
        self.wait(eng, sem, self.v(sem))

    def bar(self, eng, inst, sem="vs"):
        self.inc(inst, sem)
        self.wait(eng, sem, self.v(sem))
        return inst

    def alloc(self, name, shape, dtype):
        t = self.ctx.enter_context(self.nc.sbuf_tensor(name, shape, dtype))
        self.sb[name] = t
        return t

    def xnv(self, d, qg, w=QG):
        return self.sb["xn"][:, d * TL + qg * QG: d * TL + qg * QG + w]


def _chunked(ap2d, p=128):
    return ap2d.rearrange("(d p) t -> p d t", p=p)


def _const_tensors(nc):
    cst = {}
    cst["identf"] = nc.inline_tensor(np.eye(128, dtype=np.float32), "c_identf").ap()
    cst["identb"] = nc.inline_tensor(np.eye(128).astype(bfloat16), "c_identb").ap()
    cst["ones_colb"] = nc.inline_tensor(np.ones((128, 1), bfloat16), "c_onescb").ap()
    cst["ones_rowf"] = nc.inline_tensor(np.ones((1, 128), np.float32), "c_onesrf").ap()
    cst["ones_rowb"] = nc.inline_tensor(np.ones((1, 512), bfloat16), "c_onesrb").ap()
    cst["masks"] = nc.inline_tensor(_mask_planes(), "c_masks").ap()
    sel4 = np.zeros((4, 64), bfloat16)
    for e in range(4):
        sel4[e, e * 16:(e + 1) * 16] = 1
    cst["sel4"] = nc.inline_tensor(sel4, "c_sel4").ap()
    selp = np.zeros((16, 8 * 128), bfloat16)
    for p in range(8):
        selp[2 * p, 128 * p:128 * p + 64] = 1
        selp[2 * p + 1, 128 * p + 64:128 * p + 128] = 1
    cst["selp"] = nc.inline_tensor(selp, "c_selp").ap()
    return cst


def alloc_common(kb, attn=False):
    kb.alloc("identf", [128, 128], F32)
    kb.alloc("identb", [128, 128], BF16)
    kb.alloc("ones_colb", [128, 1], BF16)
    kb.alloc("ones_rowf", [1, 128], F32)
    kb.alloc("ones_rowb", [1, 512], BF16)
    kb.alloc("xn", [128, NCH * TL], BF16)
    kb.alloc("sq", [128, 2 * QG], BF16)
    kb.alloc("tb", [128, 2 * QG], BF16)
    kb.alloc("row_m", [1, QG], F32)
    kb.alloc("row_a", [1, QG], F32)
    kb.alloc("row_b", [1, QG], F32)
    kb.alloc("row_r", [1, QG], F32)
    kb.alloc("row_m1", [1, QG], F32)
    kb.alloc("row_a1", [1, QG], F32)
    kb.alloc("row_b1", [1, QG], F32)
    kb.alloc("row_r1", [1, QG], F32)
    kb.alloc("xi", [128, 3 * QG], F32)
    kb.alloc("xo", [128, 2 * QG], F32)
    kb.alloc("scr1", [128, 16], F32)
    kb.alloc("scr4", [128, 16], F32)
    kb.alloc("runq2", [1, TL], F32)
    kb.alloc("k2m", [1, 2], F32)
    kb.alloc("k2row", [1, 2], F32)
    if attn:
        kb.alloc("masks", [128, 8 * 512], BF16)
        kb.sb["sched"] = kb.ctx.enter_context(kb.nc.sbuf_tensor("sched_sb", [1, 128], I32))
        kb.sb["pads"] = kb.ctx.enter_context(kb.nc.sbuf_tensor("pads_sb", [128, 64], F32))


def emit_setup(kb, cst, ins, want):
    nc = kb.nc
    sb = kb.sb
    with nc.Block(kb.nm("setup")) as blk:
        @blk.gpsimd
        def _(g):
            def ld(dst, src, slow=False):
                kb.inc(g.dma_start(out=dst, in_=src,
                                   allow_slow_non_contiguous=slow), "ld", 16)
            ld(sb["identf"][:], cst["identf"])
            ld(sb["identb"][:], cst["identb"])
            ld(sb["ones_colb"][:], cst["ones_colb"])
            ld(sb["ones_rowf"][:], cst["ones_rowf"])
            ld(sb["ones_rowb"][:], cst["ones_rowb"])
            if "masks" in want:
                ld(sb["masks"][:], cst["masks"])
                ld(sb["sched"][:], ins["sched"])
                ld(sb["pads"][:], ins["pads"])
            if "l0ffn" in want:
                ld(sb["sel4"][:], cst["sel4"])
                ld(sb["l0ar"][:].rearrange("p (d t) -> p d t", t=68), _chunked(ins["w_l0ar"]))
                ld(sb["l0B"][:], ins["w_l0B"])
            if "l1ffn" in want:
                ld(sb["selp"][:], cst["selp"])
                ld(sb["l1ar"][:].rearrange("p (d t) -> p d t", t=144), _chunked(ins["w_l1ar"]))
                if "baseb_row" in sb:
                    ld(sb["baseb_row"][:], ins["b_l1base"].rearrange("o one -> one o"))
            if "l0qkvb" in want:
                ld(sb["l0qkvb"][:].rearrange("p (d one) -> p d one", one=1), _chunked(ins["b_l0qkv"]), slow=True)
                ld(sb["l0vb_row"][:], ins["b_l0qkv"][4224:4352, :].rearrange("o one -> one o"))
            if "wob" in want:
                ld(sb["wob"][:].rearrange("p (d one) -> p d one", one=1), _chunked(ins["wo_b"]), slow=True)
            if "rq" in want:
                ld(sb["runq2"][:], ins["rq_in"])

        @blk.vector
        def _(v):
            kb.wait(v, "ld", kb.v("ld"))
    nc.all_engine_barrier()


def emit_ln(kb, xsrc, pre=False):
    nc = kb.nc
    sb = kb.sb
    with ExitStack() as pc:
        ps_s = [pc.enter_context(nc.psum_tensor(kb.nm("lns"), [1, QG], F32))
                for _ in range(2)]
        ps_q = [pc.enter_context(nc.psum_tensor(kb.nm("lnq"), [1, QG], F32))
                for _ in range(2)]
        ps_bm = [pc.enter_context(nc.psum_tensor(kb.nm("lnbm"), [128, QG], F32))
                 for _ in range(2)]
        ps_br = [pc.enter_context(nc.psum_tensor(kb.nm("lnbr"), [128, QG], F32))
                 for _ in range(2)]
        with nc.Block(kb.nm("ln")) as blk:
            L = kb.v("ld"); Vb = kb.v("dv"); M = kb.v("mm")
            Fb = kb.v("fin"); Ab = kb.v("ac"); PB = kb.v("pb")

            HC = NCH // 2
            Lf = kb.v("lf")
            xnview = sb["xn"][:].rearrange("p (d t) -> p d t", t=TL)

            @blk.gpsimd
            def _(g):
                if pre:
                    for qg in range(2):
                        for d in range(HC):
                            kb.inc(g.dma_start(
                                out=xnview[:, d:d + 1, qg * QG:(qg + 1) * QG],
                                in_=_chunked(xsrc)[:, d:d + 1,
                                                   qg * QG:(qg + 1) * QG]),
                                "ld", 16)
                else:
                    for qg in range(2):
                        kb.inc(g.dma_start(
                            out=xnview[:, :, qg * QG:(qg + 1) * QG],
                            in_=_chunked(xsrc)[:, :, qg * QG:(qg + 1) * QG]),
                            "ld", 16)

            @blk.sync
            def _(sy):
                if pre:
                    for qg in range(2):
                        for d in range(HC, NCH):
                            kb.inc(sy.dma_start(
                                out=xnview[:, d:d + 1, qg * QG:(qg + 1) * QG],
                                in_=_chunked(xsrc)[:, d:d + 1,
                                                   qg * QG:(qg + 1) * QG]),
                                "lf", 16)
                kb.wait(sy, "dv", Vb + 4 * NCH)

            @blk.scalar
            def _(s):
                for qg in range(2):
                    rb = sb["row_b" if qg == 0 else "row_b1"]
                    rr = sb["row_r" if qg == 0 else "row_r1"]
                    ra = sb["row_a" if qg == 0 else "row_a1"]
                    kb.wait(s, "fin", Fb + qg + 1)
                    kb.bar(s, s.activation(ra[:], rb[:], AF.Ln), "ss")
                    kb.inc(s.activation(rr[:], ra[:], AF.Exp,
                                        scale=-0.5), "ac")
                for qg in range(2):
                    bmm = M + qg * (NCH + 2)
                    kb.wait(s, "mm", bmm + NCH + 2)
                    if qg == 1:
                        kb.wait(s, "dv", Vb + 3 * NCH)
                    kb.inc(s.copy(sb["tb"][:, 0:QG], ps_bm[qg][:]), "cb")
                    kb.wait(s, "cb", kb.v("cb"))
                    kb.inc(s.copy(sb["tb"][:, QG:2 * QG], ps_br[qg][:]), "pb")

            @blk.vector
            def _(v):
                for qg in range(2):
                    bdv = Vb + qg * 2 * NCH
                    bmm = M + qg * (NCH + 2)
                    rm = sb["row_m" if qg == 0 else "row_m1"]
                    rb = sb["row_b" if qg == 0 else "row_b1"]
                    ra = sb["row_a" if qg == 0 else "row_a1"]
                    for d in range(NCH):
                        if not pre:
                            kb.wait(v, "ld", L + (qg + 1) * 16)
                        elif d < HC:
                            kb.wait(v, "ld", L + min(qg * HC + d + 2, 2 * HC) * 16)
                        else:
                            kb.wait(v, "lf", Lf + min(qg * (NCH - HC) + d - HC + 2,
                                                      2 * (NCH - HC)) * 16)
                        if d >= 2 or qg > 0:
                            kb.wait(v, "mm", bmm + max(0, d - 1))
                        kb.inc(v.tensor_mul(sb["sq"][:, (d % 2) * QG:(d % 2 + 1) * QG],
                                            kb.xnv(d, qg), kb.xnv(d, qg)), "dv")
                    kb.wait(v, "mm", bmm + NCH)
                    kb.bar(v, v.tensor_scalar(rm[:], ps_s[qg][:], 1.0 / D, None,
                                              ALU.mult))
                    kb.bar(v, v.tensor_scalar(ra[:], ps_q[qg][:], 1.0 / D, None,
                                              ALU.mult))
                    kb.bar(v, v.tensor_mul(rb[:], rm[:], rm[:]))
                    kb.bar(v, v.tensor_sub(rb[:], ra[:], rb[:]))
                    kb.bar(v, v.tensor_scalar(rb[:], rb[:], 1e-5, None, ALU.add))
                    kb.inc(v.nop(), "fin")
                for qg in range(2):
                    kb.wait(v, "pb", PB + qg + 1)
                    for d in range(NCH):
                        v.tensor_sub(sb["sq"][:, (d % 2) * QG:(d % 2 + 1) * QG],
                                     kb.xnv(d, qg), sb["tb"][:, 0:QG])
                        kb.inc(v.tensor_mul(kb.xnv(d, qg),
                                            sb["sq"][:, (d % 2) * QG:(d % 2 + 1) * QG],
                                            sb["tb"][:, QG:2 * QG]), "dv")

            @blk.tensor
            def _(t):
                for qg in range(2):
                    bdv = Vb + qg * NCH
                    rm = sb["row_m" if qg == 0 else "row_m1"]
                    rr = sb["row_r" if qg == 0 else "row_r1"]
                    for d in range(NCH):
                        kb.wait(t, "dv", bdv + d + 1)
                        t.matmul(ps_s[qg][:], sb["ones_colb"][:], kb.xnv(d, qg),
                                 start=(d == 0), stop=(d == NCH - 1))
                        kb.inc(t.matmul(ps_q[qg][:], sb["ones_colb"][:],
                                        sb["sq"][:, (d % 2) * QG:(d % 2 + 1) * QG],
                                        start=(d == 0), stop=(d == NCH - 1)), "mm")
                    kb.wait(t, "fin", Fb + qg + 1)
                    kb.inc(t.matmul(ps_bm[qg][:], sb["ones_rowf"][:], rm[:],
                                    start=True, stop=True), "mm")
                    kb.wait(t, "ac", Ab + qg + 1)
                    kb.inc(t.matmul(ps_br[qg][:], sb["ones_rowf"][:], rr[:],
                                    start=True, stop=True), "mm")
    nc.all_engine_barrier()


def emit_ffn0(kb, xsrc, xdst, dbg=False, xbdst=None):
    nc = kb.nc
    sb = kb.sb
    for qg in range(2):
        with ExitStack() as pc:
            psA = pc.enter_context(nc.psum_tensor(kb.nm(f"f0A{qg}"), [68, QG], F32))
            psT = pc.enter_context(nc.psum_tensor(kb.nm(f"f0T{qg}"), [128, 4 * 68], F32))
            psGT = pc.enter_context(nc.psum_tensor(kb.nm(f"f0GT{qg}"), [4, QG], BF16))
            psG = pc.enter_context(nc.psum_tensor(kb.nm(f"f0G{qg}"), [64, QG], F32))
            psD = [pc.enter_context(nc.psum_tensor(kb.nm(f"f0D{i}{qg}"), [128, QG], F32))
                   for i in range(2)]
            with nc.Block(kb.nm(f"ffn0_{qg}")) as blk:
                M = kb.v("mm"); A = kb.v("ac"); V = kb.v("dv")
                L = kb.v("ld"); S = kb.v("st"); S2 = kb.v("s2")
                CB = kb.v("cb")

                @blk.tensor
                def _(t):
                    for d in range(NCH):
                        kb.inc(t.matmul(psA[:], sb["l0ar"][:, d * 68:(d + 1) * 68],
                                        kb.xnv(d, qg), start=(d == 0),
                                        stop=(d == NCH - 1)), "mm")
                    kb.wait(t, "ac", A + 1)
                    for qt in range(4):
                        kb.inc(t.transpose(psT[:, qt * 68:(qt + 1) * 68],
                                           sb["arf"][0:68, qt * 128:(qt + 1) * 128],
                                           sb["identf"][0:68, 0:68]), "mm")
                    kb.wait(t, "dv", V + 2)
                    for qt in range(4):
                        kb.inc(t.transpose(psGT[:, qt * 128:(qt + 1) * 128],
                                           sb["gates4"][:, qt * 4:(qt + 1) * 4],
                                           sb["identb"][:]), "mm")
                    kb.wait(t, "ac", A + 7)
                    kb.inc(t.matmul(psG[:], sb["sel4"][:], sb["gT"][:],
                                    start=True, stop=True), "mm")
                    kb.wait(t, "dv", V + 3)
                    for o in range(NCH):
                        if o >= 2:
                            kb.wait(t, "dv", V + 2 + o)
                        kb.inc(t.matmul(psD[o % 2][:],
                                        sb["l0B"][0:64, o * 128:(o + 1) * 128],
                                        sb["wmid"][0:64, :], start=True, stop=True),
                               "mm")

                @blk.scalar
                def _(s):
                    kb.wait(s, "mm", M + NCH)
                    kb.inc(s.copy(sb["arf"][0:68, :], psA[:]), "ac")
                    kb.inc(s.copy(sb["midb"][0:64, :], psA[0:64, :]), "ac")
                    kb.wait(s, "dv", V + 1)
                    kb.wait(s, "mm", M + NCH + 4)
                    for qt in range(4):
                        kb.inc(s.activation(sb["e16"][:, qt * 4:(qt + 1) * 4],
                                            psT[:, qt * 68 + 64: qt * 68 + 68],
                                            AF.Exp, bias=sb["nm4"][:, qt:qt + 1]),
                               "ac")
                    kb.wait(s, "mm", M + NCH + 8)
                    kb.inc(s.copy(sb["gT"][:], psGT[:]), "ac")
                    for o in range(NCH):
                        kb.wait(s, "dv", V + 4 + o)
                        kb.inc(s.dma_start(
                            out=_chunked(xdst)[:, o, qg * QG:(qg + 1) * QG],
                            in_=sb["xo"][:, (o % 2) * QG:(o % 2 + 1) * QG]), "st", 16)
                        if xbdst is not None:
                            if o >= 2:
                                kb.wait(s, "s2", S2 + (o - 1) * 16)
                            kb.inc(s.copy(
                                sb["sq"][:, (o % 2) * QG:(o % 2 + 1) * QG],
                                sb["xo"][:, (o % 2) * QG:(o % 2 + 1) * QG]), "cb")
                            kb.wait(s, "cb", kb.v("cb"))
                            kb.inc(s.dma_start(
                                out=_chunked(xbdst)[:, o, qg * QG:(qg + 1) * QG],
                                in_=sb["sq"][:, (o % 2) * QG:(o % 2 + 1) * QG]),
                                "s2", 16)
                    if xbdst is not None:
                        kb.wait(s, "s2", kb.v("s2"))

                @blk.vector
                def _(v):
                    kb.wait(v, "mm", M + NCH + 4)
                    for qt in range(4):
                        inst = v.reduce_max(sb["nm4"][:, qt:qt + 1],
                                            psT[:, qt * 68 + 64: qt * 68 + 68],
                                            axis=AX.X, negate=True)
                    kb.bar(v, inst)
                    kb.inc(v.nop(), "dv")
                    kb.wait(v, "ac", A + 6)
                    for qt in range(4):
                        e4 = sb["e16"][:, qt * 4:(qt + 1) * 4]
                        msk = sb["scr4"][:, 0:4]; el2 = sb["scr4"][:, 4:8]
                        num = sb["scr4"][:, 8:12]
                        m2 = sb["scr1"][:, 0:1]; zz = sb["scr1"][:, 2:4]
                        iz = sb["scr1"][:, 4:6]
                        kb.bar(v, v.tensor_scalar(msk[:], e4, 1.0, None, ALU.is_ge))
                        kb.bar(v, v.scalar_tensor_tensor(el2[:], msk[:], NEG, e4,
                                                         ALU.mult, ALU.add))
                        kb.bar(v, v.reduce_max(m2[:], el2[:], axis=AX.X))
                        kb.bar(v, v.tensor_scalar(msk[:], e4, m2[:], None, ALU.is_ge))
                        kb.bar(v, v.tensor_mul(num[:], msk[:], e4))
                        kb.bar(v, v.reduce_sum(zz[:, 0:1], num[:], axis=AX.X))
                        kb.bar(v, v.tensor_copy(zz[:, 1:2], zz[:, 0:1]))
                        kb.bar(v, v.reciprocal(iz[:], zz[:]))
                        ginst = v.tensor_scalar(sb["gates4"][:, qt * 4:(qt + 1) * 4],
                                                num[:], iz[:, 0:1], 2.0, ALU.mult,
                                                ALU.mult)
                        kb.bar(v, ginst)
                    kb.inc(v.nop(), "dv")
                    if dbg and qg == 0:
                        v.tensor_copy(kb.sb["gdbg"][:, 40:44], sb["scr4"][:, 0:4])
                        v.tensor_copy(kb.sb["gdbg"][:, 44:48], sb["scr4"][:, 4:8])
                        v.tensor_copy(kb.sb["gdbg"][:, 48:52], sb["scr4"][:, 8:12])
                        v.tensor_copy(kb.sb["gdbg"][:, 52:53], sb["scr1"][:, 0:1])
                        v.tensor_copy(kb.sb["gdbg"][:, 53:55], sb["scr1"][:, 2:4])
                        v.tensor_copy(kb.sb["gdbg"][:, 55:57], sb["scr1"][:, 4:6])
                        v.tensor_copy(kb.sb["gdbg"][:, 0:16], sb["e16"][:])
                        v.tensor_copy(kb.sb["gdbg"][:, 16:32], sb["gates4"][:])
                        v.tensor_copy(kb.sb["gdbg"][:, 32:36], psT[:, 64:68])
                        v.tensor_copy(kb.sb["gdbg"][:, 36:40], sb["nm4"][:])
                    kb.wait(v, "mm", M + NCH + 9)
                    kb.wait(v, "ac", A + 2)
                    kb.inc(v.tensor_mul(sb["wmid"][0:64, :], sb["midb"][0:64, :],
                                        psG[:]), "dv")
                    for o in range(NCH):
                        kb.wait(v, "mm", M + NCH + 9 + o + 1)
                        kb.wait(v, "ld", L + (o + 1) * 16)
                        if o >= 2:
                            kb.wait(v, "st", S + (o - 1) * 16)
                            if xbdst is not None:
                                kb.wait(v, "cb", CB + o - 1)
                        kb.inc(v.tensor_add(
                            sb["xo"][:, (o % 2) * QG:(o % 2 + 1) * QG],
                            sb["xi"][:, (o % 3) * QG:(o % 3 + 1) * QG],
                            psD[o % 2][:]), "dv")

                @blk.sync
                def _(sy):
                    for o in range(NCH):
                        if o >= 3:
                            kb.wait(sy, "dv", V + 4 + o - 3)
                        kb.inc(sy.dma_start(
                            out=sb["xi"][:, (o % 3) * QG:(o % 3 + 1) * QG],
                            in_=_chunked(xsrc)[:, o, qg * QG:(qg + 1) * QG]), "ld", 16)
                    kb.wait(sy, "st", S + NCH * 16)
        nc.all_engine_barrier()


def emit_qkv(kb, layer, ins, qdst, kdst, vdst, part="all"):
    nc = kb.nc
    sb = kb.sb
    wsrc = ins["w_l1qkv"] if layer == 1 else ins["w_l0qkv"]
    nkg = 4 if layer == 1 else 1
    kw = 256 if layer == 1 else 128
    kmo = 2 if layer == 1 else 1

    groups = []
    if part in ("all", "q"):
        groups += [("q", g, 512 * g, 512, 4, True) for g in range(8)]
    if part in ("all", "kv", "k"):
        if layer == 1:
            nkg, kw, kmo = 2, 512, 4
        for gk in range(nkg):
            groups.append(("k", gk, 4096 + kw * gk, kw, kmo, False))

    slots = []
    mm_d, mm_q2, ac_evac, dv_sq, dv_mg = {}, {}, {}, {}, {}
    group_end_mm = {}
    st_i = {}
    mm = ac = 0
    dv = 1
    isq_list = []
    pend_q2 = [None]
    for gi, (kind, gidx, c0, wdt, nmo, isq) in enumerate(groups):
        for qg in range(2):
            for mo in range(nmo):
                i = len(slots)
                mm += 1; mm_d[i] = mm
                if pend_q2[0] is not None:
                    mm += 1; mm_q2[pend_q2[0]] = mm
                    pend_q2[0] = None
                if isq:
                    dv += 1; dv_sq[i] = dv
                    dv += 1; dv_mg[i] = dv
                    isq_list.append(i)
                    pend_q2[0] = i
                ac += 1; ac_evac[i] = ac
                slots.append((gi, qg, mo, isq, kind, gidx, nmo))
        group_end_mm[gi] = mm
    if pend_q2[0] is not None:
        mm += 1; mm_q2[pend_q2[0]] = mm
        pend_q2[0] = None
    nst = 0
    for gi, g in enumerate(groups):
        for qg in range(2):
            nst += 1
            st_i[(gi, qg)] = nst

    with ExitStack() as pc:
        wbuf = [pc.enter_context(nc.sbuf_tensor(kb.nm("qkvw"), [128, NCH * 512], BF16))
                for _ in range(2)]
        stage = pc.enter_context(nc.sbuf_tensor(kb.nm("qkvstage"), [128, 4 * 512], BF16))
        q2b = pc.enter_context(nc.sbuf_tensor(kb.nm(f"q2b"), [128, 512], BF16))
        psP = [pc.enter_context(nc.psum_tensor(kb.nm(f"qkvp{i}"), [128, QG], F32))
               for i in range(4)]
        psq2 = [pc.enter_context(nc.psum_tensor(kb.nm(f"q2p{i}"), [1, QG], F32))
                for i in range(2)]
        with nc.Block(kb.nm("qkv")) as blk:
            Lw = kb.v("lw"); M = kb.v("mm"); A = kb.v("ac")
            V = kb.v("dv"); S = kb.v("st")

            def wview(gi, d, lo, w):
                return wbuf[gi % 2][:].rearrange("p (d t) -> p d t", t=512)[:, d, lo:lo + w]

            @blk.gpsimd
            def _(g):
                for gi, (kind, gidx, c0, wdt, nmo, isq) in enumerate(groups):
                    if gi >= 2:
                        kb.wait(g, "mm", M + group_end_mm[gi - 2])
                    src = _chunked(wsrc)[:, :, c0:c0 + wdt]
                    dst = wbuf[gi % 2][:].rearrange("p (d t) -> p d t", t=512)[:, :, 0:wdt]
                    for part in range(4):
                        kb.inc(g.dma_start(
                            out=dst[:, part * 8:(part + 1) * 8, :],
                            in_=src[:, part * 8:(part + 1) * 8, :]), "lw", 16)

            @blk.tensor
            def _(t):
                nq2 = [0]
                pend = [None]

                def flush_q2():
                    if pend[0] is None:
                        return
                    pend[0] = None
                    kb.wait(t, "dv", V + dv_sq[isq_list[nq2[0]]])
                    if nq2[0] >= 2:
                        kb.wait(t, "dv", V + dv_mg[isq_list[nq2[0] - 2]])
                    kb.inc(t.matmul(psq2[nq2[0] % 2][:], sb["ones_colb"][:],
                                    q2b[:], start=True, stop=True), "mm")
                    nq2[0] += 1

                for i, (gi, qg, mo, isq, kind, gidx, nmo) in enumerate(slots):
                    if qg == 0 and mo == 0:
                        kb.wait(t, "lw", Lw + 64 * (gi + 1))
                    if i >= 4:
                        kb.wait(t, "ac", A + ac_evac[i - 4])
                    for d in range(NCH):
                        inst = t.matmul(psP[i % 4][:], wview(gi, d, mo * 128, 128),
                                        kb.xnv(d, qg), start=(d == 0),
                                        stop=(d == NCH - 1))
                    kb.inc(inst, "mm")
                    flush_q2()
                    if isq:
                        pend[0] = i
                flush_q2()

            @blk.scalar
            def _(s):
                for i, (gi, qg, mo, isq, kind, gidx, nmo) in enumerate(slots):
                    kb.wait(s, "mm", M + mm_d[i])
                    prev = (gi, qg - 1) if qg == 1 else ((gi - 1, 1) if gi >= 1 else None)
                    if prev is not None:
                        kb.wait(s, "st", S + 16 * st_i[prev])
                    if layer == 0:
                        bias = sb["l0qkvb"][:, 4 * gidx + mo: 4 * gidx + mo + 1] \
                            if kind == "q" else sb["l0qkvb"][:, 32:33]
                    else:
                        bias = 0.0
                    kb.inc(s.activation(
                        stage[:, mo * 512:(mo + 1) * 512],
                        psP[i % 4][:], AF.Identity, bias=bias), "ac")

            if part in ("all", "q"):
                @blk.vector
                def _(v):
                    v.memset(sb["runq2"][:], 0.0)
                    kb.inc(v.nop(), "dv")
                    for n, i in enumerate(isq_list):
                        gi, qg, mo, isq, kind, gidx, nmo = slots[i]
                        kb.wait(v, "ac", A + ac_evac[i])
                        if n >= 1:
                            kb.wait(v, "mm", M + mm_q2[isq_list[n - 1]])
                        sl = stage[:, mo * 512:(mo + 1) * 512]
                        kb.inc(v.tensor_mul(q2b[:], sl, sl), "dv")
                        kb.wait(v, "mm", M + mm_q2[i])
                        kb.inc(v.tensor_tensor(
                            sb["runq2"][:, qg * QG:(qg + 1) * QG],
                            sb["runq2"][:, qg * QG:(qg + 1) * QG],
                            psq2[n % 2][:], ALU.max), "dv")

            @blk.sync
            def _(sy):
                for gi, (kind, gidx, c0, wdt, nmo, isq) in enumerate(groups):
                    for qg in range(2):
                        last = [i for i, sl in enumerate(slots)
                                if sl[0] == gi and sl[1] == qg][-1]
                        kb.wait(sy, "ac", A + ac_evac[last])
                        dst = qdst if kind == "q" else kdst
                        r0 = nmo * gidx
                        kb.inc(sy.dma_start(
                            out=_chunked(dst)[:, r0:r0 + nmo, qg * QG:(qg + 1) * QG],
                            in_=stage[:, 0:nmo * 512].rearrange(
                                "p (m t) -> p m t", t=512)), "st", 16)
                kb.wait(sy, "st", kb.v("st"))
    nc.all_engine_barrier()
    if part in ("q", "k"):
        return

    vw = 1024 if layer == 1 else 128
    vc0 = 4096 + (1024 if layer == 1 else 128)
    nvh = 2 if layer == 1 else 1
    vhw = 512 if layer == 1 else 128
    with ExitStack() as pc:
        vwb = pc.enter_context(nc.sbuf_tensor(kb.nm("vwb"), [128, NCH * vhw], BF16))
        vstage = pc.enter_context(nc.sbuf_tensor(kb.nm("vstage"), [128, 2 * 512], BF16))
        psV = [pc.enter_context(nc.psum_tensor(kb.nm("vps"), [128, 512], F32))
               for i in range(2)]
        with nc.Block(kb.nm("vpass")) as blk:
            Lw = kb.v("lw"); M = kb.v("mm"); A = kb.v("ac"); S = kb.v("st")

            @blk.gpsimd
            def _(g):
                for vh in range(nvh):
                    if vh >= 1:
                        kb.wait(g, "mm", M + 8 * vh)
                    src = _chunked(wsrc)[:, :, vc0 + vh * vhw:vc0 + (vh + 1) * vhw]
                    dst = vwb[:].rearrange("p (d t) -> p d t", t=vhw)
                    for part in range(4):
                        kb.inc(g.dma_start(
                            out=dst[:, part * 8:(part + 1) * 8, :],
                            in_=src[:, part * 8:(part + 1) * 8, :]), "lw", 16)

            @blk.tensor
            def _(t):
                n = 0
                for vh in range(nvh):
                    kb.wait(t, "lw", Lw + 64 * (vh + 1))
                    for tt in range(8):
                        n += 1
                        if n > 2:
                            kb.wait(t, "ac", A + n - 2)
                        for d in range(NCH):
                            inst = t.matmul(
                                psV[n % 2][:, 0:vhw],
                                kb.xnv(d, tt // 4)[:, (tt % 4) * 128:(tt % 4 + 1) * 128],
                                vwb[:].rearrange("p (d t) -> p d t", t=vhw)[:, d, :],
                                start=(d == 0),
                                stop=(d == NCH - 1 and layer == 1))
                        if layer == 0:
                            inst = t.matmul(psV[n % 2][:, 0:vhw],
                                            sb["ones_rowb"][:, 0:128],
                                            sb["l0vb_row"][:], start=False, stop=True)
                        kb.inc(inst, "mm")

            @blk.scalar
            def _(s):
                n = 0
                for vh in range(nvh):
                    for tt in range(8):
                        n += 1
                        kb.wait(s, "mm", M + n)
                        if n > 2:
                            kb.wait(s, "st", S + (n - 2) * 16)
                        kb.inc(s.copy(vstage[:, (n % 2) * 512:(n % 2) * 512 + vhw],
                                      psV[n % 2][:, 0:vhw]), "ac")

            @blk.sync
            def _(sy):
                n = 0
                for vh in range(nvh):
                    for tt in range(8):
                        n += 1
                        kb.wait(sy, "ac", A + n)
                        kb.inc(sy.dma_start(
                            out=vdst[tt * 128:(tt + 1) * 128, vh * vhw:(vh + 1) * vhw],
                            in_=vstage[:, (n % 2) * 512:(n % 2) * 512 + vhw]), "st", 16)
                kb.wait(sy, "st", kb.v("st"))
    nc.all_engine_barrier()


def emit_rq_out(kb, rq_out):
    nc = kb.nc
    with nc.Block(kb.nm("rqout")) as blk:
        @blk.sync
        def _(sy):
            kb.inc(sy.dma_start(out=rq_out, in_=kb.sb["runq2"][:]), "st", 16)
            kb.wait(sy, "st", kb.v("st"))
    nc.all_engine_barrier()



def emit_ffn1(kb, ins, xsrc, xdst, xbdst=None):
    nc = kb.nc
    sb = kb.sb
    whctx = ExitStack()
    sb["whb"] = whctx.enter_context(
        nc.sbuf_tensor(kb.nm("whb"), [128, 16 * QG], BF16))
    wbuf = [whctx.enter_context(nc.sbuf_tensor(kb.nm("f1w"), [128, NCH * 256],
                                               BF16)) for _ in range(2)]
    bbuf = [whctx.enter_context(nc.sbuf_tensor(kb.nm("f1b"), [128, 8 * 256],
                                               BF16)) for _ in range(2)]
    LwP = kb.v("lw")

    def load_group(g_, og):
        src_ = _chunked(ins["w_l1base"])[:, :, og * 256:(og + 1) * 256]
        dst_ = wbuf[og % 2][:].rearrange("p (d t) -> p d t", t=256)
        for p_ in range(4):
            kb.inc(g_.dma_start(out=dst_[:, p_ * 8:(p_ + 1) * 8, :],
                                in_=src_[:, p_ * 8:(p_ + 1) * 8, :]), "lw", 16)
        kb.inc(g_.dma_start(
            out=bbuf[og % 2][:].rearrange("p (d t) -> p d t", t=256),
            in_=_chunked(ins["w_l1B"])[:, :, og * 256:(og + 1) * 256]), "lw", 16)


    with ExitStack() as pc:
        psM = [pc.enter_context(nc.psum_tensor(kb.nm("f1M"), [128, QG], F32))
               for i in range(2)]
        psR = [pc.enter_context(nc.psum_tensor(kb.nm("f1R"), [16, QG], F32))
               for i in range(2)]
        psT = pc.enter_context(nc.psum_tensor(kb.nm("f1T"), [128, 4 * 16], F32))
        psCT = pc.enter_context(nc.psum_tensor(kb.nm("f1CT"), [16, QG], BF16))
        psC = pc.enter_context(nc.psum_tensor(kb.nm("f1C"), [128, QG], F32))
        mid2 = pc.enter_context(nc.sbuf_tensor(kb.nm("mid2"), [128, 2 * QG], BF16))
        e16t = pc.enter_context(nc.sbuf_tensor(kb.nm("e16t"), [128, 16], F32))
        nm1 = pc.enter_context(nc.sbuf_tensor(kb.nm("nm1"), [128, 4], F32))
        gall = pc.enter_context(nc.sbuf_tensor(kb.nm("gall"), [128, 64], BF16))
        cT = pc.enter_context(nc.sbuf_tensor(kb.nm("cT"), [16, QG], BF16))
        for qg in range(2):
            with nc.Block(kb.nm("f1a")) as blk:
                M = kb.v("mm"); A = kb.v("ac"); V = kb.v("dv")

                if qg == 0:
                    @blk.gpsimd
                    def _(g):
                        load_group(g, 0)
                        load_group(g, 1)

                @blk.tensor
                def _(t):
                    for d in range(NCH):
                        t.matmul(psM[qg][:], sb["l1ar"][:, d * 144:d * 144 + 128],
                                 kb.xnv(d, qg), start=(d == 0), stop=(d == NCH - 1))
                        kb.inc(t.matmul(psR[qg][:], sb["l1ar"][:, d * 144 + 128:(d + 1) * 144],
                                        kb.xnv(d, qg), start=(d == 0),
                                        stop=(d == NCH - 1)), "mm")
                    kb.wait(t, "ac", A + 1)
                    for qt in range(4):
                        kb.inc(t.transpose(psT[:, qt * 16:(qt + 1) * 16],
                                           sb["arf"][0:16, qt * 128:(qt + 1) * 128],
                                           sb["identf"][0:16, 0:16]), "mm")
                    kb.wait(t, "dv", V + 2)
                    for qt in range(4):
                        kb.inc(t.transpose(psCT[:, qt * 128:(qt + 1) * 128],
                                           gall[:, qt * 16:(qt + 1) * 16],
                                           sb["identb"][:]), "mm")
                    kb.wait(t, "ac", A + 3)
                    for p in range(8):
                        if p >= 1:
                            kb.wait(t, "dv", V + 2 + p)
                        kb.inc(t.matmul(psC[:], sb["selp"][:, 128 * p:128 * (p + 1)],
                                        cT[:], start=True, stop=True), "mm")

                @blk.scalar
                def _(s):
                    kb.wait(s, "mm", M + NCH)
                    kb.inc(s.copy(sb["arf"][0:16, :], psR[qg][:]), "ac")
                    kb.inc(s.copy(mid2[:, qg * QG:(qg + 1) * QG], psM[qg][:]), "ac")
                    kb.wait(s, "dv", V + 1)
                    kb.wait(s, "mm", M + NCH + 4)
                    GX0 = kb.v("gx")
                    for qt in range(4):
                        if qt >= 1:
                            kb.wait(s, "gx", GX0 + qt)
                        kb.inc(s.activation(e16t[:], psT[:, qt * 16:(qt + 1) * 16],
                                            AF.Exp, bias=nm1[:, qt:qt + 1]), "ex")
                    kb.wait(s, "mm", M + NCH + 8)
                    kb.inc(s.copy(cT[:], psCT[:]), "ac")

                @blk.vector
                def _(v):
                    kb.wait(v, "mm", M + NCH + 4)
                    for qt in range(4):
                        inst = v.reduce_max(nm1[:, qt:qt + 1],
                                            psT[:, qt * 16:(qt + 1) * 16],
                                            axis=AX.X, negate=True)
                    kb.bar(v, inst)
                    kb.inc(v.nop(), "dv")
                    EX = kb.v("ex") - 4
                    for qt in range(4):
                        kb.wait(v, "ex", EX + qt + 1)
                        e4 = e16t[:]
                        msk = sb["scr4"][:, 0:16]
                        m2 = sb["scr1"][:, 0:1]; zz0 = sb["scr1"][:, 2:4]
                        iz = sb["scr1"][:, 4:6]; ks = sb["scr1"][:, 6:8]
                        el2 = sb["scrB"][:, 0:16]; num = sb["scrB"][:, 16:32]
                        kb.bar(v, v.reduce_sum(zz0[:, 0:1], e4, axis=AX.X))
                        kb.bar(v, v.tensor_scalar(msk, e4, 1.0, None, ALU.is_ge))
                        kb.bar(v, v.scalar_tensor_tensor(el2, msk, NEG, e4,
                                                         ALU.mult, ALU.add))
                        kb.bar(v, v.reduce_max(m2, el2, axis=AX.X))
                        kb.bar(v, v.tensor_scalar(msk, e4, m2[:], None, ALU.is_ge))
                        kb.bar(v, v.tensor_mul(num, msk, e4))
                        kb.bar(v, v.reduce_sum(ks[:, 0:1], num, axis=AX.X))
                        kb.bar(v, v.scalar_tensor_tensor(ks[:, 1:2], zz0[:, 0:1], 1e-6,
                                                         ks[:, 0:1], ALU.mult, ALU.add))
                        kb.bar(v, v.tensor_copy(ks[:, 0:1], ks[:, 1:2]))
                        kb.bar(v, v.reciprocal(iz[:], ks[:, 0:2]))
                        ginst = v.tensor_scalar(gall[:, qt * 16:(qt + 1) * 16],
                                                num, iz[:, 0:1], 0.5,
                                                ALU.mult, ALU.mult)
                        kb.bar(v, ginst)
                        kb.inc(v.nop(), "gx")
                    kb.inc(v.nop(), "dv")
                    M2 = M + NCH + 8
                    for p in range(8):
                        kb.wait(v, "mm", M2 + 1 + p)
                        kb.inc(v.tensor_mul(
                            sb["whb"][:, (qg * 8 + p) * QG:(qg * 8 + p + 1) * QG],
                            mid2[:, qg * QG:(qg + 1) * QG], psC[:]), "dv")
            nc.all_engine_barrier()

    NOG = 16
    with ExitStack() as pc:
        bbr = pc.enter_context(nc.sbuf_tensor(kb.nm("bbr"), [1, D], BF16))
        psB2 = [pc.enter_context(nc.psum_tensor(kb.nm("f1ps"), [128, QG], F32))
                for i in range(4)]
        with nc.Block(kb.nm("f1base")) as blk:
            Lw = kb.v("lw"); M = kb.v("mm"); V = kb.v("dv")
            L = kb.v("ld"); S = kb.v("st"); LB = kb.v("lb")
            CB = kb.v("cb")

            @blk.gpsimd
            def _(g):
                kb.inc(g.dma_start(
                    out=bbr[:],
                    in_=ins["b_l1base"].rearrange("o one -> one o")), "lb", 16)
                for og in range(2, NOG):
                    kb.wait(g, "mm", M + (og - 1) * 4)
                    load_group(g, og)

            @blk.tensor
            def _(t):
                j = 0
                kb.wait(t, "lb", LB + 16)
                for og in range(NOG):
                    kb.wait(t, "lw", LwP + 80 * (og + 1))
                    for qg in range(2):
                        for mo in range(2):
                            j += 1
                            if j > 4:
                                kb.wait(t, "dv", V + j - 4)
                            for d in range(NCH):
                                t.matmul(psB2[j % 4][:],
                                         wbuf[og % 2][:].rearrange("p (d t) -> p d t", t=256)[:, d, mo * 128:(mo + 1) * 128],
                                         kb.xnv(d, qg),
                                         start=(d == 0), stop=False)
                            t.matmul(psB2[j % 4][:],
                                     bbr[0:1, og * 256 + mo * 128:
                                         og * 256 + (mo + 1) * 128],
                                     sb["ones_rowb"][:], start=False, stop=False)
                            for p in range(8):
                                inst = t.matmul(
                                    psB2[j % 4][:],
                                    bbuf[og % 2][:].rearrange("p (d t) -> p d t", t=256)[:, p, mo * 128:(mo + 1) * 128],
                                    sb["whb"][:, (qg * 8 + p) * QG:(qg * 8 + p + 1) * QG],
                                    start=False, stop=(p == 7))
                            kb.inc(inst, "mm")

            @blk.vector
            def _(v):
                j = 0
                for og in range(NOG):
                    for qg in range(2):
                        for mo in range(2):
                            j += 1
                            kb.wait(v, "mm", M + j)
                            kb.wait(v, "ld", L + j * 16)
                            if j > 2:
                                kb.wait(v, "st", S + (j - 2) * 16)
                                if xbdst is not None:
                                    kb.wait(v, "cb", CB + j - 2)
                            kb.inc(v.scalar_tensor_tensor(
                                sb["xo"][:, (j % 2) * QG:(j % 2 + 1) * QG],
                                psB2[j % 4][:], 0.5,
                                sb["xi"][:, (j % 3) * QG:(j % 3 + 1) * QG],
                                ALU.mult, ALU.add), "dv")

            @blk.scalar
            def _(sc):
                S2 = kb.v("s2")
                order = [(og, qg, mo) for og in range(NOG) for qg in range(2)
                         for mo in range(2)]
                for j in range(1, len(order) + 1):
                    og, qg, mo = order[j - 1]
                    kb.wait(sc, "dv", V + j)
                    kb.inc(sc.dma_start(
                        out=_chunked(xdst)[:, og * 2 + mo, qg * QG:(qg + 1) * QG],
                        in_=sb["xo"][:, (j % 2) * QG:((j % 2) + 1) * QG]), "st", 16)
                    if xbdst is not None:
                        if j >= 3:
                            kb.wait(sc, "s2", S2 + (j - 2) * 16)
                        kb.inc(sc.copy(
                            sb["sq"][:, (j % 2) * QG:(j % 2 + 1) * QG],
                            sb["xo"][:, (j % 2) * QG:((j % 2) + 1) * QG]), "cb")
                        kb.wait(sc, "cb", kb.v("cb"))
                        kb.inc(sc.dma_start(
                            out=_chunked(xbdst)[:, og * 2 + mo,
                                                qg * QG:(qg + 1) * QG],
                            in_=sb["sq"][:, (j % 2) * QG:(j % 2 + 1) * QG]),
                            "s2", 16)
                kb.wait(sc, "st", kb.v("st"))
                if xbdst is not None:
                    kb.wait(sc, "s2", kb.v("s2"))

            @blk.sync
            def _(sy):
                order = [(og, qg, mo) for og in range(NOG) for qg in range(2)
                         for mo in range(2)]
                for j in range(1, len(order) + 1):
                    og, qg, mo = order[j - 1]
                    if j > 3:
                        kb.wait(sy, "dv", V + j - 3)
                    kb.inc(sy.dma_start(
                        out=sb["xi"][:, (j % 3) * QG:((j % 3) + 1) * QG],
                        in_=_chunked(xsrc)[:, og * 2 + mo, qg * QG:(qg + 1) * QG]),
                        "ld", 16)
    whctx.close()
    nc.all_engine_barrier()


def build_launch_A():
    nc = bass.Bass("TRN2", target_bir_lowering=False, num_devices=NCORES)
    ins = {}
    ins["x"] = nc.dram_tensor("x", [D, TL], F32, kind="ExternalInput").ap()
    ins["w_l0ar"] = nc.dram_tensor("w_l0ar", [D, 68], F32, kind="ExternalInput").ap()
    ins["w_l0B"] = nc.dram_tensor("w_l0B", [64, D], F32, kind="ExternalInput").ap()
    ins["w_l0qkv"] = nc.dram_tensor("w_l0qkv", [D, 4352], F32, kind="ExternalInput").ap()
    ins["b_l0qkv"] = nc.dram_tensor("b_l0qkv", [4352, 1], F32, kind="ExternalInput").ap()
    x1 = nc.dram_tensor("x1", [D, TL], F32, kind="ExternalOutput").ap()
    q0 = nc.dram_tensor("q0", [D, TL], BF16, kind="ExternalOutput").ap()
    k0c = nc.dram_tensor("k0c", [128, TL], BF16, kind="ExternalOutput").ap()
    v0c = nc.dram_tensor("v0c", [TL, 128], BF16, kind="ExternalOutput").ap()
    rq0 = nc.dram_tensor("rq0", [1, TL], F32, kind="ExternalOutput").ap()
    cst = _const_tensors(nc)
    with ExitStack() as ctx:
        kb = KB(nc, ctx)
        alloc_common(kb)
        kb.alloc("sel4", [4, 64], BF16)
        kb.alloc("l0ar", [128, NCH * 68], BF16)
        kb.alloc("l0B", [64, D], BF16)
        kb.alloc("l0qkvb", [128, 34], F32)
        kb.alloc("l0vb_row", [1, 128], BF16)
        kb.alloc("arf", [68, QG], F32)
        kb.alloc("midb", [64, QG], BF16)
        kb.alloc("e16", [128, 16], F32)
        kb.alloc("nm4", [128, 4], F32)
        kb.alloc("gates4", [128, 16], BF16)
        kb.alloc("gT", [4, QG], BF16)
        kb.alloc("wmid", [64, QG], BF16)
        emit_setup(kb, cst, ins, {"l0ffn", "l0qkvb"})
        emit_ln(kb, ins["x"])
        emit_ffn0(kb, ins["x"], x1)
        emit_ln(kb, x1)
        emit_qkv(kb, 0, ins, q0, k0c, v0c)
        emit_rq_out(kb, rq0)
    return nc



def emit_attn_layer_pre(kb, layer, kg, vg):
    width = 8192 if layer == 1 else (2048, 4096)
    st = ExitStack()
    kb.sb["_attn_stack"] = st
    if layer == 1:
        for nm_ in ("kloc0", "kloc1", "vloc0", "vloc1"):
            kb.sb[nm_] = st.enter_context(
                kb.nc.sbuf_tensor(kb.nm(nm_), [128, 8192], BF16))
    else:
        kb.sb["kloc0"] = st.enter_context(kb.nc.sbuf_tensor(kb.nm("kloc0"), [128, 2048], BF16))
        kb.sb["vloc0"] = st.enter_context(kb.nc.sbuf_tensor(kb.nm("vloc0"), [128, 2048], BF16))
        kb.sb["kloc1"] = st.enter_context(kb.nc.sbuf_tensor(kb.nm("kloc1"), [128, 4096], BF16))
        kb.sb["vloc1"] = st.enter_context(kb.nc.sbuf_tensor(kb.nm("vloc1"), [128, 4096], BF16))
    emit_assemble(kb, layer, kg, vg)
    emit_k2(kb, layer)


def emit_attn_layer_post(kb):
    kb.sb.pop("_attn_stack").close()


def emit_assemble(kb, layer, kg, vg):
    nc = kb.nc
    sb = kb.sb
    u = kb.uid
    with nc.Block(kb.nm("asm")) as blk:
        if layer == 0:
            @blk.sync
            def _(sy):
                kb.wait(sy, "cc", kb.v("cc"))
                i = 0
                for s, nent in ((0, 4), (1, 8)):
                    base = 0 if s == 0 else 12
                    for e in range(nent):
                        i += 1
                        with sy.register(f"ka{u}_{i}") as r0, \
                             sy.register(f"kb{u}_{i}") as r1:
                            sy.reg_load(r0, sb["sched"][0:1, base + 3 * e: base + 3 * e + 1])
                            sy.reg_load(r1, sb["sched"][0:1, base + 3 * e + 1: base + 3 * e + 2])
                            krow = sy.snap(r0, donate=True, min_val=0, max_val=896)
                            kcol = sy.snap(r1, donate=True, min_val=0, max_val=512)
                            kb.inc(sy.dma_start(
                                out=sb[f"kloc{s}"][:, 512 * e: 512 * e + 512],
                                in_=kg[bass.ds(krow, 128), bass.ds(kcol, 512)]), "as", 16)
                kb.wait(sy, "as", kb.v("as"))

            @blk.gpsimd
            def _(g):
                kb.wait(g, "cc", kb.v("cc"))
                i = 0
                for s, nent in ((0, 4), (1, 8)):
                    base = 0 if s == 0 else 12
                    for e in range(nent):
                        i += 1
                        with g.register(f"va{u}_{i}") as r2:
                            g.reg_load(r2, sb["sched"][0:1, base + 3 * e + 2: base + 3 * e + 3])
                            vblk = g.snap(r2, donate=True, min_val=0, max_val=15)
                            kb.inc(g.dma_start(
                                out=sb[f"vloc{s}"][:, 512 * e: 512 * e + 512].rearrange(
                                    "p (one c h) -> p one c h", one=1, h=128),
                                in_=vg.rearrange("(r c p) h -> r p c h", c=4, p=128)[
                                    bass.ds(vblk, 1), :, :, :]), "av", 16)
                kb.wait(g, "av", kb.v("av"))
        else:
            @blk.scalar
            def _(sc):
                kb.wait(sc, "cc", kb.v("cc"))
                i = 0
                for s in range(2):
                    for e in range(2):
                        i += 1
                        kbase = 36 + (s * 2 + e) * 2
                        with sc.register(f"ra{u}_{i}") as r0, \
                             sc.register(f"rb{u}_{i}") as r1, \
                             sc.register(f"rc{u}_{i}") as r2:
                            sc.reg_load(r0, sb["sched"][0:1, kbase:kbase + 1])
                            sc.reg_load(r1, sb["sched"][0:1, kbase + 1:kbase + 2])
                            rr = sc.snap(r0, donate=True, min_val=0, max_val=7)
                            cb = sc.snap(r1, donate=True, min_val=0, max_val=1)
                            src = kg.rearrange("(r h p) (cb c) -> cb r p h c",
                                               h=8, p=128, c=512)[
                                bass.ds(cb, 1), bass.ds(rr, 1), :, :, :]
                            dst = sb[f"kloc{s}"][:].rearrange(
                                "p (h ec c) -> p h ec c", h=8, ec=2, c=512)[:, :, e:e + 1, :]
                            kb.inc(sc.dma_start(out=dst, in_=src), "as", 16)
                            sc.reg_load(r2, sb["sched"][0:1, 100 + s * 2 + e: 101 + s * 2 + e])
                            vblk = sc.snap(r2, donate=True, min_val=0, max_val=15)
                            dst = sb[f"vloc{s}"][:].rearrange(
                                "p (ec c h hd) -> p ec c h hd", ec=2, c=4, h=8)[:, e, :, :, :]
                            kb.inc(sc.dma_start(
                                out=dst,
                                in_=vg.rearrange("(r c p) (h hd) -> r p c h hd",
                                                 c=4, p=128, h=8)[
                                    bass.ds(vblk, 1), :, :, :, :]), "as", 16)
                kb.wait(sc, "as", kb.v("as"))
    nc.all_engine_barrier()


def emit_k2(kb, layer):
    nc = kb.nc
    sb = kb.sb
    slices = []
    if layer == 0:
        for j in range(8):
            slices.append(sb["kloc1"][:, 512 * j:512 * (j + 1)])
    else:
        for s in range(2):
            for j in range(16):
                slices.append(sb[f"kloc{s}"][:, 512 * j:512 * (j + 1)])
    with ExitStack() as pc:
        psk = [pc.enter_context(nc.psum_tensor(kb.nm("psk"), [1, 512], F32))
               for i in range(2)]
        k2b = pc.enter_context(nc.sbuf_tensor(kb.nm("k2b"), [128, 512], BF16))
        with nc.Block(kb.nm("k2")) as blk:
            M = kb.v("mm"); V = kb.v("dv")

            @blk.vector
            def _(v):
                kb.bar(v, v.memset(sb["row_m"][:], NEG))
                for j, sl in enumerate(slices):
                    if j >= 1:
                        kb.wait(v, "mm", M + j)
                    kb.inc(v.tensor_mul(k2b[:], sl, sl), "dv")
                    if j >= 1:
                        v.tensor_tensor(sb["row_m"][:], sb["row_m"][:],
                                        psk[(j - 1) % 2][:], ALU.max)
                kb.wait(v, "mm", M + len(slices))
                kb.bar(v, v.tensor_tensor(sb["row_m"][:], sb["row_m"][:],
                                          psk[(len(slices) - 1) % 2][:], ALU.max))
                kb.bar(v, v.reduce_max(sb["k2m"][:, 0:1], sb["row_m"][:],
                                       axis=AX.X))
                kb.bar(v, v.tensor_copy(sb["k2m"][:, 1:2], sb["k2m"][:, 0:1]))
                kb.inc(v.nop(), "dv")

            @blk.tensor
            def _(t):
                for j in range(len(slices)):
                    kb.wait(t, "dv", V + j + 1)
                    kb.inc(t.matmul(psk[j % 2][:], sb["ones_colb"][:], k2b[:],
                                    start=True, stop=True), "mm")
    nc.all_engine_barrier()


def emit_attn(kb, layer, qsrc):
    nc = kb.nc
    sb = kb.sb
    nkc = L1_NKC if layer == 1 else L0_NKC
    nmask = 8 if layer == 1 else 4
    padcol = (48, 56) if layer == 1 else (0, 16)
    with ExitStack() as pc:
        psS = [pc.enter_context(nc.psum_tensor(kb.nm("psS"), [128, QG], F32))
               for i in range(3)]
        psO = [pc.enter_context(nc.psum_tensor(kb.nm("psO"), [128, QG], F32))
               for i in range(3)]
        psZ = [pc.enter_context(nc.psum_tensor(kb.nm("psZ"), [1, QG], F32))]
        psB = pc.enter_context(nc.psum_tensor(kb.nm("psB"), [128, QG], F32))
        psU = psB
        qh = pc.enter_context(nc.sbuf_tensor(kb.nm("qh"), [128, 2 * QG], BF16))
        pt = pc.enter_context(nc.sbuf_tensor(kb.nm("pt"), [128, 3 * QG], BF16))
        zacc = sb["xi"][:].bitcast(BF16)
        izb = sb["xo"][:].bitcast(BF16)
        lnb = sb["xo"][:, QG:2 * QG]
        bcb = pc.enter_context(nc.sbuf_tensor(kb.nm("bcb"), [128, QG], BF16))
        bmat = pc.enter_context(nc.sbuf_tensor(kb.nm("bmat"), [128, 64], F32))
        kb.sb["_dbg_bmat"] = bmat
        ub = pc.enter_context(nc.sbuf_tensor(kb.nm("ub"), [128, 1], F32))
        uacc = pc.enter_context(nc.sbuf_tensor(kb.nm("uacc"), [1, 4], F32))
        kb.sb["_dbg_uacc"] = uacc

        with nc.Block(kb.nm("attn")) as blk:
            SM = kb.v("sm"); EX = kb.v("ex"); PT = kb.v("pt"); ZZ = kb.v("zz")
            IV = kb.v("iv"); BE = kb.v("be"); AO = kb.v("ao"); BZ = kb.v("bz")
            QR = kb.v("qr"); LQ = kb.v("lq"); PU = kb.v("pu"); FU = kb.v("fu")
            PV = kb.v("pv"); ZA = kb.v("za"); ZR = kb.v("zr")
            PF = kb.v("pf")

            iters = [(qg, h) for qg in range(2) for h in range(32)]
            csm, cex, cpt, czz, cz4 = [0], [0], [0], [0], [0]
            sm_at, ex_at, pt_at, zz_at, z4_at = [], [], [], [], []
            for (qg, h) in iters:
                sm_at.append(csm[0]); ex_at.append(cex[0])
                pt_at.append(cpt[0]); zz_at.append(czz[0])
                z4_at.append(cz4[0])
                csm[0] += nkc[qg]; cex[0] += nkc[qg]
                cpt[0] += min(nmask, nkc[qg]); czz[0] += nkc[qg]
                cz4[0] += nkc[qg] // 4

            @blk.sync
            def _(sy):
                for it, (qg, h) in enumerate(iters):
                    if it >= 2:
                        kb.wait(sy, "qr", QR + it - 1)
                    kb.inc(sy.dma_start(
                        out=qh[:, (it % 2) * QG:(it % 2 + 1) * QG],
                        in_=_chunked(qsrc)[:, h, qg * QG:(qg + 1) * QG]), "lq", 16)

            @blk.vector
            def _(v):
                for qg in range(2):
                    kb.bar(v, v.reduce_max(uacc[:, 0:1],
                                           sb["runq2"][:, qg * QG:(qg + 1) * QG],
                                           axis=AX.X))
                    kb.bar(v, v.tensor_copy(uacc[:, 1:2], uacc[:, 0:1]))
                    kb.bar(v, v.tensor_mul(uacc[:, 0:2], uacc[:, 0:2], sb["k2m"][:]))
                    kb.inc(v.nop(), "pu")
                    kb.wait(v, "fu", FU + qg + 1)
                    kb.bar(v, v.tensor_scalar(uacc[:, 2:4], uacc[:, 2:4], -SC, -1.0,
                                              ALU.mult, ALU.add))
                    kb.inc(v.nop(), "pu")
                    kb.wait(v, "bz", BZ + qg * 33 + 1)
                    kb.bar(v, v.tensor_scalar(bmat[:, 32 * qg:32 * qg + nkc[qg]],
                                              sb["pads"][:, padcol[qg]:padcol[qg] + nkc[qg]],
                                              psU[:, 0:1], None, ALU.add))
                    kb.inc(v.nop(), "pu")
                    for h in range(32):
                        it = qg * 32 + h
                        n = nkc[qg]
                        zsl = zacc[:, (it % 2) * QG:(it % 2 + 1) * QG]
                        for kc in range(n):
                            gz = zz_at[it] + kc
                            psl = pt[:, (gz % 3) * QG:(gz % 3 + 1) * QG]
                            kb.wait(v, "ex", EX + ex_at[it] + kc + 1)
                            if kc < min(nmask, n):
                                mvar = kc if (layer == 0 or kc >= 4) else (4 + kc)
                                if layer == 1 and kc >= 4:
                                    mvar = kc - 4
                                kb.inc(v.tensor_mul(
                                    psl, psl,
                                    sb["masks"][:, mvar * 512:(mvar + 1) * 512]),
                                    "pt")
                            if kc % 4 == 0:
                                if kc == 0:
                                    if it >= 2:
                                        kb.wait(v, "zz", ZZ + z4_at[it - 1])
                                else:
                                    kb.wait(v, "zz", ZZ + z4_at[it] + kc // 4)
                                kb.inc(v.tensor_copy(zsl, psl), "za")
                            else:
                                kb.inc(v.tensor_add(zsl, zsl, psl), "za")
                        kb.wait(v, "zz", ZZ + z4_at[it] + n // 4)
                        if it >= 2:
                            kb.wait(v, "bz", BZ + it + (1 if it >= 34 else 0))
                        kb.inc(v.tensor_copy(
                            izb[0:1, (it % 2) * QG:(it % 2 + 1) * QG],
                            psZ[0][:]), "zr")
                        eits = ([it - 1] if h >= 1 else []) + ([it] if h == 31 else [])
                        for eit in eits:
                            kb.wait(v, "be", BE + eit + 1)
                            kb.inc(v.tensor_mul(kb.xnv(eit - qg * 32, qg),
                                                psO[eit % 3][:], bcb[:]), "ao")

            @blk.scalar
            def _(s):
                for qg in range(2):
                    kb.wait(s, "pu", PU + qg * 3 + 1)
                    kb.bar(s, s.activation(uacc[:, 2:4], uacc[:, 0:2], AF.Ln), "ss")
                    kb.bar(s, s.activation(uacc[:, 2:4], uacc[:, 2:4], AF.Exp,
                                           scale=0.5), "ss")
                    kb.inc(s.nop(), "fu")
                    def bcbe(j):
                        kb.wait(s, "bz", BZ + j + 2 + (1 if j >= 32 else 0))
                        kb.inc(s.activation(lnb, psB[:], AF.Ln), "pf")
                        kb.wait(s, "ao", AO + max(0, j))
                        kb.inc(s.activation(bcb[:], lnb, AF.Exp, scale=-1.0),
                               "be")

                    for h in range(32):
                        it = qg * 32 + h
                        if it == qg * 32:
                            kb.wait(s, "pu", PU + qg * 3 + 3)
                        for kc in range(nkc[qg]):
                            gz = zz_at[it] + kc
                            kb.wait(s, "sm", SM + sm_at[it] + kc + 1)
                            if gz >= 3:
                                kb.wait(s, "pv", PV + gz - 2)
                                kb.wait(s, "za", ZA + gz - 2)
                            kb.inc(s.activation(
                                pt[:, (gz % 3) * QG:(gz % 3 + 1) * QG],
                                psS[(sm_at[it] + kc) % 3][:], AF.Exp,
                                bias=bmat[:, 32 * qg + kc:32 * qg + kc + 1],
                                scale=SC), "ex")
                        if h >= 1:
                            bcbe(it - 1)
                        if h == 31:
                            bcbe(it)

            @blk.tensor
            def _(t):
                pend_bc = [None]

                def flush_bc():
                    if pend_bc[0] is None:
                        return
                    it = pend_bc[0]
                    pend_bc[0] = None
                    kb.wait(t, "zr", ZR + it + 1)
                    if it >= 1:
                        kb.wait(t, "pf", PF + it)
                    kb.inc(t.matmul(psB[:], sb["ones_rowb"][:, 0:128],
                                    izb[0:1, (it % 2) * QG:(it % 2 + 1) * QG],
                                    start=True, stop=True), "bz")

                for qg in range(2):
                    flush_bc()
                    kb.wait(t, "pu", PU + qg * 3 + 2)
                    if qg == 1:
                        kb.wait(t, "pf", PF + 32)
                    kb.inc(t.matmul(psU[:, 0:1], sb["ones_rowf"][:], uacc[0:1, 3:4],
                                    start=True, stop=True), "bz")
                    for h in range(32):
                        it = qg * 32 + h
                        kb.wait(t, "lq", LQ + (it + 1) * 16)
                        n = nkc[qg]

                        def s_mm(kc, it=it, qg=qg, h=h):
                            g = sm_at[it] + kc
                            if ex_at[it] + kc - 2 > 0:
                                kb.wait(t, "ex", EX + ex_at[it] + kc - 2)
                            kb.inc(t.matmul(
                                psS[g % 3][:],
                                sb[f"kloc{qg}"][:, 128 * kc:128 * (kc + 1)]
                                if layer == 0 else
                                sb[f"kloc{qg}"][:, 1024 * (h // 4) + 128 * kc:
                                                1024 * (h // 4) + 128 * (kc + 1)],
                                qh[:, (it % 2) * QG:(it % 2 + 1) * QG],
                                start=True, stop=True), "sm")

                        def pv_mm(kc, it=it, qg=qg, h=h, n=n):
                            if kc < min(nmask, n):
                                kb.wait(t, "pt", PT + pt_at[it] + kc + 1)
                            else:
                                kb.wait(t, "ex", EX + ex_at[it] + kc + 1)
                            if kc == 0:
                                kb.wait(t, "ao", AO + max(0, it - 2))
                            vsl = (sb[f"vloc{qg}"][:, 128 * kc:128 * (kc + 1)]
                                   if layer == 0 else
                                   sb[f"vloc{qg}"][:, (kc * 8 + (h // 4)) * 128:
                                                   (kc * 8 + (h // 4) + 1) * 128])
                            gz = zz_at[it] + kc
                            kb.inc(t.matmul(psO[it % 3][:], vsl,
                                            pt[:, (gz % 3) * QG:(gz % 3 + 1) * QG],
                                            start=(kc == 0), stop=(kc == n - 1)), "pv")

                        def z_mm(kc, it=it, n=n):
                            kb.wait(t, "za", ZA + zz_at[it] + kc + 1)
                            if kc == 3:
                                kb.wait(t, "zr", ZR + it)
                            kb.inc(t.matmul(psZ[0][:], sb["ones_colb"][:],
                                            zacc[:, (it % 2) * QG:(it % 2 + 1) * QG],
                                            start=(kc == 3), stop=(kc == n - 1)),
                                   "zz")

                        s_mm(0)
                        s_mm(1)
                        s_mm(2)
                        for kc in range(n):
                            if kc + 3 < n:
                                s_mm(kc + 3)
                            if kc + 3 == n:
                                kb.inc(t.nop(), "qr")
                            pv_mm(kc)
                            if kc % 4 == 3:
                                z_mm(kc)
                            if kc == 3:
                                flush_bc()
                        pend_bc[0] = it
                    flush_bc() if qg == 1 else None
    nc.all_engine_barrier()


def emit_oproj(kb, layer, wsrc, xsrc, xdst, has_bias, xbdst=None):
    nc = kb.nc
    sb = kb.sb
    NOG = 16
    with ExitStack() as pc:
        wbuf = [pc.enter_context(nc.sbuf_tensor(kb.nm("wow"), [128, NCH * 256], BF16))
                for _ in range(2)]
        psO2 = [pc.enter_context(nc.psum_tensor(kb.nm("psO2"), [128, QG], F32))
                for i in range(4)]
        with nc.Block(kb.nm("oproj")) as blk:
            Lw = kb.v("lw"); M = kb.v("mm"); V = kb.v("dv")
            L = kb.v("ld"); S = kb.v("st"); CB = kb.v("cb")
            order = [(og, qg, mo) for og in range(NOG) for qg in range(2)
                     for mo in range(2)]

            @blk.gpsimd
            def _(g):
                for og in range(NOG):
                    if og >= 2:
                        kb.wait(g, "mm", M + (og - 1) * 4)
                    src = _chunked(wsrc)[:, :, og * 256:(og + 1) * 256]
                    dst = wbuf[og % 2][:].rearrange("p (d t) -> p d t", t=256)
                    for part in range(4):
                        kb.inc(g.dma_start(
                            out=dst[:, part * 8:(part + 1) * 8, :],
                            in_=src[:, part * 8:(part + 1) * 8, :]), "lw", 16)

            @blk.tensor
            def _(t):
                n = 0
                for og in range(NOG):
                    kb.wait(t, "lw", Lw + 64 * (og + 1))
                    for qg in range(2):
                        for mo in range(2):
                            n += 1
                            if n > 4:
                                kb.wait(t, "dv", V + n - 4)
                            for hch in range(NCH):
                                inst = t.matmul(
                                    psO2[n % 4][:],
                                    wbuf[og % 2][:].rearrange("p (d t) -> p d t", t=256)[:, hch, mo * 128:(mo + 1) * 128],
                                    kb.xnv(hch, qg),
                                    start=(hch == 0), stop=(hch == NCH - 1))
                            kb.inc(inst, "mm")

            @blk.vector
            def _(v):
                n = 0
                for og in range(NOG):
                    for qg in range(2):
                        for mo in range(2):
                            n += 1
                            oc = og * 2 + mo
                            kb.wait(v, "mm", M + n)
                            kb.wait(v, "ld", L + n * 16)
                            if n > 2:
                                kb.wait(v, "st", S + (n - 2) * 16)
                                if xbdst is not None:
                                    kb.wait(v, "cb", CB + n - 2)
                            bias = sb["wob"][:, oc:oc + 1] if has_bias else 0.0
                            kb.inc(v.scalar_tensor_tensor(
                                sb["xo"][:, (n % 2) * QG:(n % 2 + 1) * QG],
                                psO2[n % 4][:], bias,
                                sb["xi"][:, (n % 3) * QG:(n % 3 + 1) * QG],
                                ALU.add, ALU.add), "dv")

            @blk.scalar
            def _(sc):
                S2 = kb.v("s2")
                for j in range(1, len(order) + 1):
                    og, qg, mo = order[j - 1]
                    kb.wait(sc, "dv", V + j)
                    kb.inc(sc.dma_start(
                        out=_chunked(xdst)[:, og * 2 + mo, qg * QG:(qg + 1) * QG],
                        in_=sb["xo"][:, (j % 2) * QG:((j % 2) + 1) * QG]), "st", 16)
                    if xbdst is not None:
                        if j >= 3:
                            kb.wait(sc, "s2", S2 + (j - 2) * 16)
                        kb.inc(sc.copy(
                            sb["sq"][:, (j % 2) * QG:(j % 2 + 1) * QG],
                            sb["xo"][:, (j % 2) * QG:((j % 2) + 1) * QG]), "cb")
                        kb.wait(sc, "cb", kb.v("cb"))
                        kb.inc(sc.dma_start(
                            out=_chunked(xbdst)[:, og * 2 + mo,
                                                qg * QG:(qg + 1) * QG],
                            in_=sb["sq"][:, (j % 2) * QG:(j % 2 + 1) * QG]),
                            "s2", 16)
                kb.wait(sc, "st", kb.v("st"))
                if xbdst is not None:
                    kb.wait(sc, "s2", kb.v("s2"))

            @blk.sync
            def _(sy):
                for j in range(1, len(order) + 1):
                    og, qg, mo = order[j - 1]
                    if j > 3:
                        kb.wait(sy, "dv", V + j - 3)
                    kb.inc(sy.dma_start(
                        out=sb["xi"][:, (j % 3) * QG:((j % 3) + 1) * QG],
                        in_=_chunked(xsrc)[:, og * 2 + mo, qg * QG:(qg + 1) * QG]),
                        "ld", 16)
    nc.all_engine_barrier()


def build_launch_B(dbg=False):
    nc = bass.Bass("TRN2", target_bir_lowering=False, num_devices=NCORES)
    ins = {}
    ins["x1"] = nc.dram_tensor("x1", [D, TL], F32, kind="ExternalInput").ap()
    ins["q0"] = nc.dram_tensor("q0", [D, TL], BF16, kind="ExternalInput").ap()
    ins["k0g"] = nc.dram_tensor("k0g", [1024, TL], BF16, kind="ExternalInput").ap()
    ins["v0g"] = nc.dram_tensor("v0g", [8192, 128], BF16, kind="ExternalInput").ap()
    ins["rq_in"] = nc.dram_tensor("rq_in", [1, TL], F32, kind="ExternalInput").ap()
    ins["sched"] = nc.dram_tensor("sched", [1, 128], I32, kind="ExternalInput").ap()
    ins["pads"] = nc.dram_tensor("pads", [128, 64], F32, kind="ExternalInput").ap()
    ins["w_l0wo"] = nc.dram_tensor("w_l0wo", [D, D], F32, kind="ExternalInput").ap()
    ins["wo_b"] = nc.dram_tensor("wo_b", [D, 1], F32, kind="ExternalInput").ap()
    x2 = nc.dram_tensor("x2", [D, TL], F32, kind="ExternalOutput").ap()
    cst = _const_tensors(nc)
    with ExitStack() as ctx:
        kb = KB(nc, ctx)
        alloc_common(kb, attn=True)
        kb.alloc("wob", [128, 32], F32)
        emit_setup(kb, cst, ins, {"masks", "wob", "rq"})
        emit_attn_layer_pre(kb, 0, ins["k0g"], ins["v0g"])
        if dbg:
            dko = nc.dram_tensor("dko", [128, 6144], BF16, kind="ExternalOutput").ap()
            dvo = nc.dram_tensor("dvo", [128, 6144], BF16, kind="ExternalOutput").ap()
            dmisc = nc.dram_tensor("dmisc", [128, 72], F32, kind="ExternalOutput").ap()
        emit_attn(kb, 0, ins["q0"])
        emit_attn_layer_post(kb)
        if dbg:
            dao = nc.dram_tensor("dao", [D, TL], BF16, kind="ExternalOutput").ap()
            with nc.Block(kb.nm("dmp")) as blk:
                @blk.sync
                def _(sy):
                    kb.inc(sy.dma_start(out=dko[:, 0:2048], in_=kb.sb["kloc0"][:]), "st", 16)
                    kb.inc(sy.dma_start(out=dko[:, 2048:6144], in_=kb.sb["kloc1"][:]), "st", 16)
                    kb.inc(sy.dma_start(out=dvo[:, 0:2048], in_=kb.sb["vloc0"][:]), "st", 16)
                    kb.inc(sy.dma_start(out=dvo[:, 2048:6144], in_=kb.sb["vloc1"][:]), "st", 16)
                    kb.inc(sy.dma_start(out=dmisc[0:1, 0:2], in_=kb.sb["k2m"][:]), "st", 16)
                    kb.inc(sy.dma_start(out=dmisc[0:1, 2:4], in_=kb.sb["_dbg_uacc"][0:1, 2:4]), "st", 16)
                    kb.inc(sy.dma_start(out=dmisc[:, 4:68], in_=kb.sb["_dbg_bmat"][:]), "st", 16)
                    kb.inc(sy.dma_start(
                        out=_chunked(dao),
                        in_=kb.sb["xn"][:].rearrange("p (d t) -> p d t", t=TL)), "st", 16)
                    kb.wait(sy, "st", kb.v("st"))
        emit_oproj(kb, 0, ins["w_l0wo"], ins["x1"], x2, True)
    return nc


def _run_launch_B(w, x1s, q0s, k0g, v0g, rqs, scheds, padss, dbg=False):
    nc = build_launch_B(dbg)
    in_maps = []
    for c in range(NCORES):
        in_maps.append({
            "x1": x1s[c], "q0": q0s[c], "k0g": k0g, "v0g": v0g,
            "rq_in": rqs[c], "sched": scheds[c], "pads": padss[c],
            "w_l0wo": w["w_l0wo"], "wo_b": w["b_l0wo"],
        })
    res = run_bass_kernel_spmd(nc, in_maps, core_ids=list(range(NCORES)))
    return res.results


def build_launch_C():
    nc = bass.Bass("TRN2", target_bir_lowering=False, num_devices=NCORES)
    ins = {}
    ins["x2"] = nc.dram_tensor("x2", [D, TL], F32, kind="ExternalInput").ap()
    ins["w_l0ar"] = nc.dram_tensor("w_l0ar", [D, 68], F32, kind="ExternalInput").ap()
    ins["w_l0B"] = nc.dram_tensor("w_l0B", [64, D], F32, kind="ExternalInput").ap()
    ins["w_l1ar"] = nc.dram_tensor("w_l1ar", [D, 144], F32, kind="ExternalInput").ap()
    ins["w_l1B"] = nc.dram_tensor("w_l1B", [1024, D], F32, kind="ExternalInput").ap()
    ins["w_l1base"] = nc.dram_tensor("w_l1base", [D, D], F32, kind="ExternalInput").ap()
    ins["b_l1base"] = nc.dram_tensor("b_l1base", [D, 1], F32, kind="ExternalInput").ap()
    ins["w_l1qkv"] = nc.dram_tensor("w_l1qkv", [D, 6144], F32, kind="ExternalInput").ap()
    x3 = nc.dram_tensor("x3", [D, TL], F32, kind="ExternalOutput").ap()
    x4 = nc.dram_tensor("x4", [D, TL], F32, kind="ExternalOutput").ap()
    q1 = nc.dram_tensor("q1", [D, TL], BF16, kind="ExternalOutput").ap()
    k1c = nc.dram_tensor("k1c", [1024, TL], BF16, kind="ExternalOutput").ap()
    v1c = nc.dram_tensor("v1c", [TL, 1024], BF16, kind="ExternalOutput").ap()
    rq1 = nc.dram_tensor("rq1", [1, TL], F32, kind="ExternalOutput").ap()
    cst = _const_tensors(nc)
    with ExitStack() as ctx:
        kb = KB(nc, ctx)
        alloc_common(kb)
        kb.alloc("sel4", [4, 64], BF16)
        kb.alloc("selp", [16, 8 * 128], BF16)
        kb.alloc("l0ar", [128, NCH * 68], BF16)
        kb.alloc("l0B", [64, D], BF16)
        kb.alloc("l1ar", [128, NCH * 144], BF16)
        kb.alloc("baseb_row", [1, D], BF16)
        kb.alloc("arf", [68, QG], F32)
        kb.alloc("midb", [64, QG], BF16)
        kb.alloc("e16", [128, 16], F32)
        kb.alloc("nm4", [128, 4], F32)
        kb.alloc("gates4", [128, 16], BF16)
        kb.alloc("gT", [4, QG], BF16)
        kb.alloc("wmid", [64, QG], BF16)
        kb.alloc("scrB", [128, 32], F32)
        emit_setup(kb, cst, ins, {"l0ffn", "l1ffn"})
        emit_ln(kb, ins["x2"])
        emit_ffn0(kb, ins["x2"], x3)
        emit_ln(kb, x3)
        emit_ffn1(kb, ins, x3, x4)
        emit_ln(kb, x4)
        emit_qkv(kb, 1, ins, q1, k1c, v1c)
        emit_rq_out(kb, rq1)
    return nc


def _run_launch_C(w, x2s):
    nc = build_launch_C()
    in_maps = []
    for c in range(NCORES):
        in_maps.append({
            "x2": x2s[c],
            "w_l0ar": w["w_l0ar"], "w_l0B": w["w_l0B"],
            "w_l1ar": w["w_l1ar"], "w_l1B": w["w_l1B"],
            "w_l1base": w["w_l1base"], "b_l1base": w["b_l1base"],
            "w_l1qkv": w["w_l1qkv"],
        })
    res = run_bass_kernel_spmd(nc, in_maps, core_ids=list(range(NCORES)))
    return res.results


def build_launch_D():
    nc = bass.Bass("TRN2", target_bir_lowering=False, num_devices=NCORES)
    ins = {}
    ins["x4"] = nc.dram_tensor("x4", [D, TL], F32, kind="ExternalInput").ap()
    ins["q1"] = nc.dram_tensor("q1", [D, TL], BF16, kind="ExternalInput").ap()
    ins["k1g"] = nc.dram_tensor("k1g", [8192, TL], BF16, kind="ExternalInput").ap()
    ins["v1g"] = nc.dram_tensor("v1g", [8192, 1024], BF16, kind="ExternalInput").ap()
    ins["rq_in"] = nc.dram_tensor("rq_in", [1, TL], F32, kind="ExternalInput").ap()
    ins["sched"] = nc.dram_tensor("sched", [1, 128], I32, kind="ExternalInput").ap()
    ins["pads"] = nc.dram_tensor("pads", [128, 64], F32, kind="ExternalInput").ap()
    ins["w_l1wo"] = nc.dram_tensor("w_l1wo", [D, D], F32, kind="ExternalInput").ap()
    x5 = nc.dram_tensor("x5", [D, TL], F32, kind="ExternalOutput").ap()
    cst = _const_tensors(nc)
    with ExitStack() as ctx:
        kb = KB(nc, ctx)
        alloc_common(kb, attn=True)
        emit_setup(kb, cst, ins, {"masks", "rq"})
        emit_attn_layer_pre(kb, 1, ins["k1g"], ins["v1g"])
        emit_attn(kb, 1, ins["q1"])
        emit_attn_layer_post(kb)
        emit_oproj(kb, 1, ins["w_l1wo"], ins["x4"], x5, False)
    return nc


def _run_launch_D(w, x4s, q1s, k1g, v1g, rqs, scheds, padss):
    nc = build_launch_D()
    in_maps = []
    for c in range(NCORES):
        in_maps.append({
            "x4": x4s[c], "q1": q1s[c], "k1g": k1g, "v1g": v1g,
            "rq_in": rqs[c], "sched": scheds[c], "pads": padss[c],
            "w_l1wo": w["w_l1wo"],
        })
    res = run_bass_kernel_spmd(nc, in_maps, core_ids=list(range(NCORES)))
    return res.results


def build_launch_E():
    nc = bass.Bass("TRN2", target_bir_lowering=False, num_devices=NCORES)
    ins = {}
    ins["x5"] = nc.dram_tensor("x5", [D, TL], F32, kind="ExternalInput").ap()
    ins["w_l1ar"] = nc.dram_tensor("w_l1ar", [D, 144], F32, kind="ExternalInput").ap()
    ins["w_l1B"] = nc.dram_tensor("w_l1B", [1024, D], F32, kind="ExternalInput").ap()
    ins["w_l1base"] = nc.dram_tensor("w_l1base", [D, D], F32, kind="ExternalInput").ap()
    ins["b_l1base"] = nc.dram_tensor("b_l1base", [D, 1], F32, kind="ExternalInput").ap()
    xout = nc.dram_tensor("xout", [D, TL], F32, kind="ExternalOutput").ap()
    cst = _const_tensors(nc)
    with ExitStack() as ctx:
        kb = KB(nc, ctx)
        alloc_common(kb)
        kb.alloc("selp", [16, 8 * 128], BF16)
        kb.alloc("l1ar", [128, NCH * 144], BF16)
        kb.alloc("baseb_row", [1, D], BF16)
        kb.alloc("arf", [68, QG], F32)
        kb.alloc("scrB", [128, 32], F32)
        emit_setup(kb, cst, ins, {"l1ffn"})
        emit_ln(kb, ins["x5"])
        emit_ffn1(kb, ins, ins["x5"], xout)
    return nc


def _run_launch_E(w, x5s):
    nc = build_launch_E()
    in_maps = []
    for c in range(NCORES):
        in_maps.append({
            "x5": x5s[c],
            "w_l1ar": w["w_l1ar"], "w_l1B": w["w_l1B"],
            "w_l1base": w["w_l1base"], "b_l1base": w["b_l1base"],
        })
    res = run_bass_kernel_spmd(nc, in_maps, core_ids=list(range(NCORES)))
    return res.results


def build_dbg(stage):
    nc = bass.Bass("TRN2", target_bir_lowering=False, num_devices=NCORES)
    ins = {}
    ins["x"] = nc.dram_tensor("x", [D, TL], F32, kind="ExternalInput").ap()
    ins["w_l0ar"] = nc.dram_tensor("w_l0ar", [D, 68], F32, kind="ExternalInput").ap()
    ins["w_l0B"] = nc.dram_tensor("w_l0B", [64, D], F32, kind="ExternalInput").ap()
    ins["w_l0qkv"] = nc.dram_tensor("w_l0qkv", [D, 4352], F32, kind="ExternalInput").ap()
    ins["b_l0qkv"] = nc.dram_tensor("b_l0qkv", [4352, 1], F32, kind="ExternalInput").ap()
    xn_out = nc.dram_tensor("xn_out", [D, TL], BF16, kind="ExternalOutput").ap()
    gdbg = nc.dram_tensor("gdbg", [128, 64], F32, kind="ExternalOutput").ap()
    x1 = nc.dram_tensor("x1", [D, TL], F32, kind="ExternalOutput").ap()
    cst = _const_tensors(nc)
    with ExitStack() as ctx:
        kb = KB(nc, ctx)
        alloc_common(kb)
        kb.alloc("sel4", [4, 64], BF16)
        kb.alloc("l0ar", [128, NCH * 68], BF16)
        kb.alloc("l0B", [64, D], BF16)
        kb.alloc("l0qkvb", [128, 34], F32)
        kb.alloc("l0vb_row", [1, 128], BF16)
        kb.alloc("arf", [68, QG], F32)
        kb.alloc("midb", [64, QG], BF16)
        kb.alloc("e16", [128, 16], F32)
        kb.alloc("nm4", [128, 4], F32)
        kb.alloc("gates4", [128, 16], BF16)
        kb.alloc("gT", [4, QG], BF16)
        kb.alloc("wmid", [64, QG], BF16)
        kb.alloc("gdbg", [128, 64], F32) if False else kb.sb.update({"gdbg": ctx.enter_context(nc.sbuf_tensor("gdbg_sb", [128, 64], F32))})
        emit_setup(kb, cst, ins, {"l0ffn", "l0qkvb"})
        emit_ln(kb, ins["x"])
        if stage != "ln1":
            emit_ffn0(kb, ins["x"], x1, dbg=True)
        with nc.Block(kb.nm("dump")) as blk:
            @blk.sync
            def _(sy):
                kb.inc(sy.dma_start(
                    out=_chunked(xn_out),
                    in_=kb.sb["xn"][:].rearrange("p (d t) -> p d t", t=TL)), "st", 16)
                kb.inc(sy.dma_start(out=gdbg, in_=kb.sb["gdbg"][:]), "st", 16)
                kb.wait(sy, "st", kb.v("st"))
    return nc


def _run_dbg(w, per_core_x, stage):
    nc = build_dbg(stage)
    in_maps = []
    for c in range(NCORES):
        in_maps.append({
            "x": per_core_x[c],
            "w_l0ar": w["w_l0ar"], "w_l0B": w["w_l0B"],
            "w_l0qkv": w["w_l0qkv"], "b_l0qkv": w["b_l0qkv"],
        })
    res = run_bass_kernel_spmd(nc, in_maps, core_ids=list(range(NCORES)))
    return res.results


def _run_launch_A(w, per_core_x):
    nc = build_launch_A()
    in_maps = []
    for c in range(NCORES):
        in_maps.append({
            "x": per_core_x[c],
            "w_l0ar": w["w_l0ar"], "w_l0B": w["w_l0B"],
            "w_l0qkv": w["w_l0qkv"], "b_l0qkv": w["b_l0qkv"],
        })
    res = run_bass_kernel_spmd(nc, in_maps, core_ids=list(range(NCORES)))
    return res.results


def emit_gather(kb, layer, ccki, ccko, ccvi, ccvo, wait=True, which="both"):
    nc = kb.nc
    with nc.Block(kb.nm("gather")) as blk:
        @blk.gpsimd
        def _(g):
            if which in ("both", "k"):
                kb.inc(g.collective_compute(
                    "AllGather", mybir.AluOpType.bypass,
                    replica_groups=[list(range(NCORES))],
                    ins=[ccki], outs=[ccko]), "cc")
            if which in ("both", "v"):
                kb.inc(g.collective_compute(
                    "AllGather", mybir.AluOpType.bypass,
                    replica_groups=[list(range(NCORES))],
                    ins=[ccvi], outs=[ccvo]), "cc")
            if wait:
                kb.wait(g, "cc", kb.v("cc"))
    if wait:
        nc.all_engine_barrier()


def build_fused(upto=99):
    nc = bass.Bass("TRN2", target_bir_lowering=False, num_devices=NCORES)
    ins = {}
    ins["x"] = nc.dram_tensor("x", [D, TL], F32, kind="ExternalInput").ap()
    for name, shape in (("w_l0ar", [D, 68]), ("w_l0B", [64, D]),
                        ("w_l0qkv", [D, 4352]), ("b_l0qkv", [4352, 1]),
                        ("w_l0wo", [D, D]), ("wo_b", [D, 1]),
                        ("w_l1qkv", [D, 6144]), ("w_l1wo", [D, D]),
                        ("w_l1base", [D, D]), ("b_l1base", [D, 1]),
                        ("w_l1ar", [D, 144]), ("w_l1B", [1024, D]),
                        ("sched", [1, 128]), ("pads", [128, 64])):
        dt = I32 if name == "sched" else F32
        ins[name] = nc.dram_tensor(name, shape, dt, kind="ExternalInput").ap()
    xout = nc.dram_tensor("xout", [D, TL], F32, kind="ExternalOutput").ap()
    xscr = nc.dram_tensor("xscr", [D, TL], F32).ap()
    xbscr = nc.dram_tensor("xbscr", [D, TL], BF16).ap()
    q_scr = nc.dram_tensor("q_scr", [D, TL], BF16).ap()
    ck0i = nc.dram_tensor("ck0i", [128, TL], BF16).ap()
    cv0i = nc.dram_tensor("cv0i", [TL, 128], BF16).ap()
    ck0o = nc.dram_tensor("ck0o", [1024, TL], BF16).ap()
    cv0o = nc.dram_tensor("cv0o", [8192, 128], BF16).ap()
    ck1i = nc.dram_tensor("ck1i", [1024, TL], BF16).ap()
    cv1i = nc.dram_tensor("cv1i", [TL, 1024], BF16).ap()
    ck1o = nc.dram_tensor("ck1o", [8192, TL], BF16).ap()
    cv1o = nc.dram_tensor("cv1o", [8192, 1024], BF16).ap()
    cst = _const_tensors(nc)
    with ExitStack() as ctx:
        kb = KB(nc, ctx)
        alloc_common(kb, attn=True)
        kb.alloc("selp", [16, 8 * 128], BF16)
        kb.alloc("l1ar", [128, NCH * 144], BF16)
        kb.alloc("arf", [68, QG], F32)
        kb.alloc("midb", [64, QG], BF16)
        kb.alloc("e16", [128, 16], F32)
        kb.alloc("nm4", [128, 4], F32)
        kb.alloc("gates4", [128, 16], BF16)
        kb.alloc("gT", [4, QG], BF16)
        kb.alloc("wmid", [64, QG], BF16)
        kb.alloc("scrB", [128, 32], F32)
        l0ctx = ExitStack()
        for nm_, shp, dt_ in (("sel4", [4, 64], BF16),
                              ("l0ar", [128, NCH * 68], BF16),
                              ("l0B", [64, D], BF16),
                              ("l0qkvb", [128, 34], F32),
                              ("l0vb_row", [1, 128], BF16),
                              ("wob", [128, 32], F32)):
            kb.sb[nm_] = l0ctx.enter_context(kb.nc.sbuf_tensor(kb.nm(nm_), shp, dt_))
        emit_setup(kb, cst, ins, {"masks", "l0ffn", "l1ffn", "l0qkvb", "wob"})
        done = False

        def fin(stage):
            nonlocal done
            if done or upto > stage:
                return upto <= stage and done
            with nc.Block(kb.nm("fin")) as blk:
                @blk.sync
                def _(sy):
                    kb.inc(sy.dma_start(out=xout, in_=xscr), "st", 16)
                    kb.wait(sy, "st", kb.v("st"))
            done = True
            return True
        emit_ln(kb, ins["x"])
        emit_ffn0(kb, ins["x"], xscr, xbdst=xbscr)
        if not fin(2):
            emit_ln(kb, xbscr, pre=True)
            emit_qkv(kb, 0, ins, q_scr, ck0i, cv0i, part="k")
            emit_gather(kb, 0, ck0i, ck0o, cv0i, cv0o, wait=False, which="k")
            emit_qkv(kb, 0, ins, q_scr, ck0i, cv0i, part="v")
        if not fin(3):
            emit_gather(kb, 0, ck0i, ck0o, cv0i, cv0o, wait=False, which="v")
            emit_qkv(kb, 0, ins, q_scr, ck0i, cv0i, part="q")
        if not fin(4):
            emit_attn_layer_pre(kb, 0, ck0o, cv0o)
            emit_attn(kb, 0, q_scr)
            emit_attn_layer_post(kb)
        if not fin(5):
            emit_oproj(kb, 0, ins["w_l0wo"], xscr, xscr, True, xbdst=xbscr)
        if not fin(6):
            emit_ln(kb, xbscr, pre=True)
            emit_ffn0(kb, xscr, xscr, xbdst=xbscr)
            l0ctx.close()
        if not fin(7):
            emit_ln(kb, xbscr, pre=True)
            emit_ffn1(kb, ins, xscr, xscr, xbdst=xbscr)
        if not fin(8):
            emit_ln(kb, xbscr, pre=True)
            emit_qkv(kb, 1, ins, q_scr, ck1i, cv1i, part="k")
            emit_gather(kb, 1, ck1i, ck1o, cv1i, cv1o, wait=False, which="k")
            emit_qkv(kb, 1, ins, q_scr, ck1i, cv1i, part="v")
        if not fin(9):
            emit_gather(kb, 1, ck1i, ck1o, cv1i, cv1o, wait=False, which="v")
            emit_qkv(kb, 1, ins, q_scr, ck1i, cv1i, part="q")
        if not fin(10):
            emit_attn_layer_pre(kb, 1, ck1o, cv1o)
            emit_attn(kb, 1, q_scr)
            emit_attn_layer_post(kb)
        if not fin(11):
            emit_oproj(kb, 1, ins["w_l1wo"], xscr, xscr, False, xbdst=xbscr)
        if not fin(12):
            emit_ln(kb, xbscr, pre=True)
            emit_ffn1(kb, ins, xscr, xout)
        if upto <= 6:
            l0ctx.close()
    return nc


def kernel_fused(upto=99, **inputs):
    w, pcx, scheds, padss = _host_prep(inputs)
    nc = build_fused(upto)
    in_maps = []
    for c in range(NCORES):
        m = {"x": pcx[c], "sched": scheds[c], "pads": padss[c]}
        for k in ("w_l0ar", "w_l0B", "w_l0qkv", "b_l0qkv", "w_l0wo",
                  "w_l1qkv", "w_l1wo", "w_l1base", "b_l1base", "w_l1ar", "w_l1B"):
            m[k] = w[k]
        m["wo_b"] = w["b_l0wo"]
        in_maps.append(m)
    res = run_bass_kernel_spmd(nc, in_maps, core_ids=list(range(NCORES)))
    return _reassemble([r["xout"] for r in res.results])


def _reassemble(x6s):
    y = np.zeros((B, T, D), np.float32)
    for c in range(NCORES):
        b, cc, m0, m1 = _core_meta(c)
        xc = np.asarray(x6s[c], np.float32)
        y[b, 512 * m0:512 * m0 + 512, :] = xc[:, 0:512].T
        y[b, 512 * m1:512 * m1 + 512, :] = xc[:, 512:1024].T
    return y


def kernel(**inputs):
    return kernel_fused(**inputs)


def _kernel_unfused(**inputs):
    w, pcx, scheds, padss = _host_prep(inputs)
    asc = np.ascontiguousarray
    resA = _run_launch_A(w, pcx)
    x1s = [asc(np.asarray(r["x1"], np.float32)) for r in resA]
    q0s = [asc(np.asarray(r["q0"])) for r in resA]
    rq0 = [asc(np.asarray(r["rq0"], np.float32)) for r in resA]
    k0g = asc(np.concatenate([np.asarray(r["k0c"]) for r in resA], axis=0))
    v0g = asc(np.concatenate([np.asarray(r["v0c"]) for r in resA], axis=0))
    resB = _run_launch_B(w, x1s, q0s, k0g, v0g, rq0, scheds, padss)
    x2s = [asc(np.asarray(r["x2"], np.float32)) for r in resB]
    resC = _run_launch_C(w, x2s)
    x4s = [asc(np.asarray(r["x4"], np.float32)) for r in resC]
    q1s = [asc(np.asarray(r["q1"])) for r in resC]
    rq1 = [asc(np.asarray(r["rq1"], np.float32)) for r in resC]
    k1g = asc(np.concatenate([np.asarray(r["k1c"]) for r in resC], axis=0))
    v1g = asc(np.concatenate([np.asarray(r["v1c"]) for r in resC], axis=0))
    resD = _run_launch_D(w, x4s, q1s, k1g, v1g, rq1, scheds, padss)
    x5s = [asc(np.asarray(r["x5"], np.float32)) for r in resD]
    resE = _run_launch_E(w, x5s)
    return _reassemble([r["xout"] for r in resE])


# version 39 (final)
# speedup vs baseline: 1.0748x; 1.0748x over previous
import math
from contextlib import ExitStack

import numpy as np
import ml_dtypes

import concourse.bass as bass
import concourse.mybir as mybir
from concourse.bass_utils import run_bass_kernel_spmd

F32 = mybir.dt.float32
BF16 = mybir.dt.bfloat16
I32 = mybir.dt.int32
AF = mybir.ActivationFunctionType
ALU = mybir.AluOpType
AX = mybir.AxisListType
bfloat16 = ml_dtypes.bfloat16

B, T, D = 2, 4096, 4096
NCORES = 8
NCH = 32
TL = 1024
QG = 512
SC = 1.0 / math.sqrt(128.0)
NEG = -1.0e30
L0_NKC = (16, 32)
L1_NKC = (8, 8)



def _core_meta(c):
    b, cc = divmod(c, 4)
    return b, cc, cc, 7 - cc


def _rank_col(b, m):
    r = 4 * b + (m if m <= 3 else 7 - m)
    col = 0 if m <= 3 else 512
    return r, col


def _sched_pads_for_core(c):
    b, cc, m0, m1 = _core_meta(c)
    sched = np.zeros(128, np.int32)
    pads = np.zeros((128, 64), np.float32)
    idx = 0
    for s, (ms, nent, pcol0) in enumerate(((m0, 4, 0), (m1, 8, 16))):
        chunks = [ms] + list(range(ms - 1, -1, -1))
        for e in range(nent):
            if e < len(chunks):
                mm = chunks[e]
            else:
                mm = ms
                pads[:, pcol0 + 4 * e: pcol0 + 4 * e + 4] = NEG
            r, col = _rank_col(b, mm)
            sched[idx] = 128 * r; sched[idx + 1] = col; sched[idx + 2] = 2 * r + col // 512
            idx += 3
    assert idx == 36
    kbase, vbase = 36, 100
    for s, ms in enumerate((m0, m1)):
        for e, mm in enumerate((ms - 1, ms)):
            if mm < 0:
                mm = ms
                pads[:, 48 + 8 * s: 48 + 8 * s + 4] = NEG
            r, col = _rank_col(b, mm)
            sched[kbase] = r
            sched[kbase + 1] = col // 512
            kbase += 2
            sched[vbase] = 2 * r + col // 512
            vbase += 1
    assert kbase == 44 and vbase == 104
    return sched.reshape(1, 128), pads


def _mask_planes():
    dk = np.arange(128)[:, None]
    ql = np.arange(512)[None, :]
    planes = [(ql >= 128 * r + dk).astype(bfloat16) for r in range(4)]
    planes += [(128 * r + dk > ql).astype(bfloat16) for r in range(4)]
    return np.concatenate(planes, axis=1)


def _host_prep(inputs):
    f = np.float32
    asc = np.ascontiguousarray
    w = {}
    x = np.asarray(inputs["L0_attn_x"], f)
    xT = x.transpose(0, 2, 1)
    per_core_x, scheds, padss = [], [], []
    for c in range(NCORES):
        b, cc, m0, m1 = _core_meta(c)
        per_core_x.append(asc(np.concatenate(
            [xT[b][:, 512 * m0:512 * m0 + 512], xT[b][:, 512 * m1:512 * m1 + 512]],
            axis=1)))
        s, p = _sched_pads_for_core(c)
        scheds.append(s); padss.append(p)

    A0 = np.asarray(inputs["L0_ffn_A"], f)
    w["w_l0ar"] = asc(np.concatenate(
        [A0.reshape(64, D).T, np.asarray(inputs["L0_ffn_router_w"], f).T], axis=1))
    w["w_l0B"] = asc(np.asarray(inputs["L0_ffn_B"], f).transpose(0, 2, 1).reshape(64, D))
    w["w_l0qkv"] = asc(np.concatenate(
        [np.asarray(inputs["L0_attn_wq_w"], f).T,
         np.asarray(inputs["L0_attn_wk_w"], f).T,
         np.asarray(inputs["L0_attn_wv_w"], f).T], axis=1))
    w["b_l0qkv"] = asc(np.concatenate(
        [np.asarray(inputs["L0_attn_wq_b"], f),
         np.asarray(inputs["L0_attn_wk_b"], f),
         np.asarray(inputs["L0_attn_wv_b"], f)]).reshape(4352, 1))
    w["w_l0wo"] = asc(np.asarray(inputs["L0_attn_wo_w"], f).T)
    w["b_l0wo"] = asc(np.asarray(inputs["L0_attn_wo_b"], f).reshape(D, 1))
    w["w_l1qkv"] = asc(np.concatenate(
        [np.asarray(inputs["L1_attn_wq"], f).T,
         np.asarray(inputs["L1_attn_wk"], f).T,
         np.asarray(inputs["L1_attn_wv"], f).T], axis=1))
    w["w_l1wo"] = asc(np.asarray(inputs["L1_attn_wo"], f).T)
    w["w_l1base"] = asc(np.asarray(inputs["L1_ffn_base_w"], f).T)
    w["b_l1base"] = asc(np.asarray(inputs["L1_ffn_base_b"], f).reshape(D, 1))
    a1t = np.asarray(inputs["L1_ffn_A"], f).T
    w["w_l1ar"] = asc(np.concatenate(
        [a1t, a1t, np.asarray(inputs["L1_ffn_router_w"], f).T], axis=1))
    w["w_l1B"] = asc(np.asarray(inputs["L1_ffn_B"], f).transpose(0, 2, 1).reshape(1024, D))
    return w, per_core_x, scheds, padss



class KB:
    def __init__(self, nc, ctx):
        self.nc = nc
        self.ctx = ctx
        self.sems = {}
        self.sb = {}
        self.uid = 0

    def nm(self, base):
        self.uid += 1
        return f"{base}_{self.uid}"

    def sem(self, name):
        if name not in self.sems:
            h = self.ctx.enter_context(self.nc.semaphore(name))
            self.sems[name] = [h, 0]
        return name

    def inc(self, inst, name, k=1):
        s = self.sems[self.sem(name)]
        inst.then_inc(s[0], k)
        s[1] += k
        return inst

    def v(self, name):
        return self.sems[self.sem(name)][1]

    def wait(self, eng, name, val):
        eng.wait_ge(self.sems[self.sem(name)][0], val)

    def vbar(self, eng, sem="vs"):
        self.inc(eng.nop(), sem)
        self.wait(eng, sem, self.v(sem))

    def bar(self, eng, inst, sem="vs"):
        self.inc(inst, sem)
        self.wait(eng, sem, self.v(sem))
        return inst

    def alloc(self, name, shape, dtype):
        t = self.ctx.enter_context(self.nc.sbuf_tensor(name, shape, dtype))
        self.sb[name] = t
        return t

    def xnv(self, d, qg, w=QG):
        return self.sb["xn"][:, d * TL + qg * QG: d * TL + qg * QG + w]


def _chunked(ap2d, p=128):
    return ap2d.rearrange("(d p) t -> p d t", p=p)


def _const_tensors(nc):
    cst = {}
    cst["identf"] = nc.inline_tensor(np.eye(128, dtype=np.float32), "c_identf").ap()
    cst["identb"] = nc.inline_tensor(np.eye(128).astype(bfloat16), "c_identb").ap()
    cst["ones_colb"] = nc.inline_tensor(np.ones((128, 1), bfloat16), "c_onescb").ap()
    cst["ones_rowf"] = nc.inline_tensor(np.ones((1, 128), np.float32), "c_onesrf").ap()
    cst["ones_rowb"] = nc.inline_tensor(np.ones((1, 512), bfloat16), "c_onesrb").ap()
    cst["masks"] = nc.inline_tensor(_mask_planes(), "c_masks").ap()
    sel4 = np.zeros((4, 64), bfloat16)
    for e in range(4):
        sel4[e, e * 16:(e + 1) * 16] = 1
    cst["sel4"] = nc.inline_tensor(sel4, "c_sel4").ap()
    selp = np.zeros((16, 8 * 128), bfloat16)
    for p in range(8):
        selp[2 * p, 128 * p:128 * p + 64] = 1
        selp[2 * p + 1, 128 * p + 64:128 * p + 128] = 1
    cst["selp"] = nc.inline_tensor(selp, "c_selp").ap()
    return cst


def alloc_common(kb, attn=False):
    kb.alloc("identf", [128, 128], F32)
    kb.alloc("identb", [128, 128], BF16)
    kb.alloc("ones_colb", [128, 1], BF16)
    kb.alloc("ones_rowf", [1, 128], F32)
    kb.alloc("ones_rowb", [1, 512], BF16)
    kb.alloc("xn", [128, NCH * TL], BF16)
    kb.alloc("sq", [128, 2 * QG], BF16)
    kb.alloc("tb", [128, 2 * QG], BF16)
    kb.alloc("row_m", [1, QG], F32)
    kb.alloc("row_a", [1, QG], F32)
    kb.alloc("row_b", [1, QG], F32)
    kb.alloc("row_r", [1, QG], F32)
    kb.alloc("row_m1", [1, QG], F32)
    kb.alloc("row_a1", [1, QG], F32)
    kb.alloc("row_b1", [1, QG], F32)
    kb.alloc("row_r1", [1, QG], F32)
    kb.alloc("xi", [128, 3 * QG], F32)
    kb.alloc("xo", [128, 2 * QG], F32)
    kb.alloc("scr1", [128, 16], F32)
    kb.alloc("scr4", [128, 16], F32)
    kb.alloc("runq2", [1, TL], F32)
    kb.alloc("k2m", [1, 2], F32)
    kb.alloc("k2row", [1, 2], F32)
    if attn:
        kb.alloc("masks", [128, 8 * 512], BF16)
        kb.sb["sched"] = kb.ctx.enter_context(kb.nc.sbuf_tensor("sched_sb", [1, 128], I32))
        kb.sb["pads"] = kb.ctx.enter_context(kb.nc.sbuf_tensor("pads_sb", [128, 64], F32))


def emit_setup(kb, cst, ins, want):
    nc = kb.nc
    sb = kb.sb
    with nc.Block(kb.nm("setup")) as blk:
        @blk.gpsimd
        def _(g):
            def ld(dst, src, slow=False):
                kb.inc(g.dma_start(out=dst, in_=src,
                                   allow_slow_non_contiguous=slow), "ld", 16)
            ld(sb["identf"][:], cst["identf"])
            ld(sb["identb"][:], cst["identb"])
            ld(sb["ones_colb"][:], cst["ones_colb"])
            ld(sb["ones_rowf"][:], cst["ones_rowf"])
            ld(sb["ones_rowb"][:], cst["ones_rowb"])
            if "masks" in want:
                ld(sb["masks"][:], cst["masks"])
                ld(sb["sched"][:], ins["sched"])
                ld(sb["pads"][:], ins["pads"])
            if "l0ffn" in want:
                ld(sb["sel4"][:], cst["sel4"])
                ld(sb["l0ar"][:].rearrange("p (d t) -> p d t", t=68), _chunked(ins["w_l0ar"]))
                ld(sb["l0B"][:], ins["w_l0B"])
            if "l1ffn" in want:
                ld(sb["selp"][:], cst["selp"])
                ld(sb["l1ar"][:].rearrange("p (d t) -> p d t", t=144), _chunked(ins["w_l1ar"]))
                if "baseb_row" in sb:
                    ld(sb["baseb_row"][:], ins["b_l1base"].rearrange("o one -> one o"))
            if "l0qkvb" in want:
                ld(sb["l0qkvb"][:].rearrange("p (d one) -> p d one", one=1), _chunked(ins["b_l0qkv"]), slow=True)
                ld(sb["l0vb_row"][:], ins["b_l0qkv"][4224:4352, :].rearrange("o one -> one o"))
            if "wob" in want:
                ld(sb["wob"][:].rearrange("p (d one) -> p d one", one=1), _chunked(ins["wo_b"]), slow=True)
            if "rq" in want:
                ld(sb["runq2"][:], ins["rq_in"])

        @blk.vector
        def _(v):
            kb.wait(v, "ld", kb.v("ld"))
    nc.all_engine_barrier()


def emit_ln(kb, xsrc, pre=False):
    nc = kb.nc
    sb = kb.sb
    with ExitStack() as pc:
        ps_s = [pc.enter_context(nc.psum_tensor(kb.nm("lns"), [1, QG], F32))
                for _ in range(2)]
        ps_q = [pc.enter_context(nc.psum_tensor(kb.nm("lnq"), [1, QG], F32))
                for _ in range(2)]
        ps_bm = [pc.enter_context(nc.psum_tensor(kb.nm("lnbm"), [128, QG], F32))
                 for _ in range(2)]
        ps_br = [pc.enter_context(nc.psum_tensor(kb.nm("lnbr"), [128, QG], F32))
                 for _ in range(2)]
        with nc.Block(kb.nm("ln")) as blk:
            L = kb.v("ld"); Vb = kb.v("dv"); M = kb.v("mm")
            Fb = kb.v("fin"); Ab = kb.v("ac"); PB = kb.v("pb")

            HC = NCH // 2
            Lf = kb.v("lf")
            xnview = sb["xn"][:].rearrange("p (d t) -> p d t", t=TL)

            @blk.gpsimd
            def _(g):
                if pre:
                    for qg in range(2):
                        for d in range(HC):
                            kb.inc(g.dma_start(
                                out=xnview[:, d:d + 1, qg * QG:(qg + 1) * QG],
                                in_=_chunked(xsrc)[:, d:d + 1,
                                                   qg * QG:(qg + 1) * QG]),
                                "ld", 16)
                else:
                    for qg in range(2):
                        kb.inc(g.dma_start(
                            out=xnview[:, :, qg * QG:(qg + 1) * QG],
                            in_=_chunked(xsrc)[:, :, qg * QG:(qg + 1) * QG]),
                            "ld", 16)

            @blk.sync
            def _(sy):
                if pre:
                    for qg in range(2):
                        for d in range(HC, NCH):
                            kb.inc(sy.dma_start(
                                out=xnview[:, d:d + 1, qg * QG:(qg + 1) * QG],
                                in_=_chunked(xsrc)[:, d:d + 1,
                                                   qg * QG:(qg + 1) * QG]),
                                "lf", 16)
                kb.wait(sy, "dv", Vb + 4 * NCH)

            @blk.scalar
            def _(s):
                for qg in range(2):
                    rb = sb["row_b" if qg == 0 else "row_b1"]
                    rr = sb["row_r" if qg == 0 else "row_r1"]
                    ra = sb["row_a" if qg == 0 else "row_a1"]
                    kb.wait(s, "fin", Fb + qg + 1)
                    kb.bar(s, s.activation(ra[:], rb[:], AF.Ln), "ss")
                    kb.inc(s.activation(rr[:], ra[:], AF.Exp,
                                        scale=-0.5), "ac")
                for qg in range(2):
                    bmm = M + qg * (NCH + 2)
                    kb.wait(s, "mm", bmm + NCH + 2)
                    if qg == 1:
                        kb.wait(s, "dv", Vb + 3 * NCH)
                    kb.inc(s.copy(sb["tb"][:, 0:QG], ps_bm[qg][:]), "cb")
                    kb.wait(s, "cb", kb.v("cb"))
                    kb.inc(s.copy(sb["tb"][:, QG:2 * QG], ps_br[qg][:]), "pb")

            @blk.vector
            def _(v):
                for qg in range(2):
                    bdv = Vb + qg * 2 * NCH
                    bmm = M + qg * (NCH + 2)
                    rm = sb["row_m" if qg == 0 else "row_m1"]
                    rb = sb["row_b" if qg == 0 else "row_b1"]
                    ra = sb["row_a" if qg == 0 else "row_a1"]
                    for d in range(NCH):
                        if not pre:
                            kb.wait(v, "ld", L + (qg + 1) * 16)
                        elif d < HC:
                            kb.wait(v, "ld", L + min(qg * HC + d + 2, 2 * HC) * 16)
                        else:
                            kb.wait(v, "lf", Lf + min(qg * (NCH - HC) + d - HC + 2,
                                                      2 * (NCH - HC)) * 16)
                        if d >= 2 or qg > 0:
                            kb.wait(v, "mm", bmm + max(0, d - 1))
                        kb.inc(v.tensor_mul(sb["sq"][:, (d % 2) * QG:(d % 2 + 1) * QG],
                                            kb.xnv(d, qg), kb.xnv(d, qg)), "dv")
                    kb.wait(v, "mm", bmm + NCH)
                    kb.bar(v, v.tensor_scalar(rm[:], ps_s[qg][:], 1.0 / D, None,
                                              ALU.mult))
                    kb.bar(v, v.tensor_scalar(ra[:], ps_q[qg][:], 1.0 / D, None,
                                              ALU.mult))
                    kb.bar(v, v.tensor_mul(rb[:], rm[:], rm[:]))
                    kb.bar(v, v.tensor_sub(rb[:], ra[:], rb[:]))
                    kb.bar(v, v.tensor_scalar(rb[:], rb[:], 1e-5, None, ALU.add))
                    kb.inc(v.nop(), "fin")
                for qg in range(2):
                    kb.wait(v, "pb", PB + qg + 1)
                    for d in range(NCH):
                        v.tensor_sub(sb["sq"][:, (d % 2) * QG:(d % 2 + 1) * QG],
                                     kb.xnv(d, qg), sb["tb"][:, 0:QG])
                        kb.inc(v.tensor_mul(kb.xnv(d, qg),
                                            sb["sq"][:, (d % 2) * QG:(d % 2 + 1) * QG],
                                            sb["tb"][:, QG:2 * QG]), "dv")

            @blk.tensor
            def _(t):
                for qg in range(2):
                    bdv = Vb + qg * NCH
                    rm = sb["row_m" if qg == 0 else "row_m1"]
                    rr = sb["row_r" if qg == 0 else "row_r1"]
                    for d in range(NCH):
                        kb.wait(t, "dv", bdv + d + 1)
                        t.matmul(ps_s[qg][:], sb["ones_colb"][:], kb.xnv(d, qg),
                                 start=(d == 0), stop=(d == NCH - 1))
                        kb.inc(t.matmul(ps_q[qg][:], sb["ones_colb"][:],
                                        sb["sq"][:, (d % 2) * QG:(d % 2 + 1) * QG],
                                        start=(d == 0), stop=(d == NCH - 1)), "mm")
                    kb.wait(t, "fin", Fb + qg + 1)
                    kb.inc(t.matmul(ps_bm[qg][:], sb["ones_rowf"][:], rm[:],
                                    start=True, stop=True), "mm")
                    kb.wait(t, "ac", Ab + qg + 1)
                    kb.inc(t.matmul(ps_br[qg][:], sb["ones_rowf"][:], rr[:],
                                    start=True, stop=True), "mm")
    nc.all_engine_barrier()


def emit_ffn0(kb, xsrc, xdst, dbg=False, xbdst=None):
    nc = kb.nc
    sb = kb.sb
    for qg in range(2):
        with ExitStack() as pc:
            psA = pc.enter_context(nc.psum_tensor(kb.nm(f"f0A{qg}"), [68, QG], F32))
            psT = pc.enter_context(nc.psum_tensor(kb.nm(f"f0T{qg}"), [128, 4 * 68], F32))
            psGT = pc.enter_context(nc.psum_tensor(kb.nm(f"f0GT{qg}"), [4, QG], BF16))
            psG = pc.enter_context(nc.psum_tensor(kb.nm(f"f0G{qg}"), [64, QG], F32))
            psD = [pc.enter_context(nc.psum_tensor(kb.nm(f"f0D{i}{qg}"), [128, QG], F32))
                   for i in range(2)]
            with nc.Block(kb.nm(f"ffn0_{qg}")) as blk:
                M = kb.v("mm"); A = kb.v("ac"); V = kb.v("dv")
                L = kb.v("ld"); S = kb.v("st"); S2 = kb.v("s2")
                CB = kb.v("cb")

                @blk.tensor
                def _(t):
                    for d in range(NCH):
                        kb.inc(t.matmul(psA[:], sb["l0ar"][:, d * 68:(d + 1) * 68],
                                        kb.xnv(d, qg), start=(d == 0),
                                        stop=(d == NCH - 1)), "mm")
                    kb.wait(t, "ac", A + 1)
                    for qt in range(4):
                        kb.inc(t.transpose(psT[:, qt * 68:(qt + 1) * 68],
                                           sb["arf"][0:68, qt * 128:(qt + 1) * 128],
                                           sb["identf"][0:68, 0:68]), "mm")
                    kb.wait(t, "dv", V + 2)
                    for qt in range(4):
                        kb.inc(t.transpose(psGT[:, qt * 128:(qt + 1) * 128],
                                           sb["gates4"][:, qt * 4:(qt + 1) * 4],
                                           sb["identb"][:]), "mm")
                    kb.wait(t, "ac", A + 7)
                    kb.inc(t.matmul(psG[:], sb["sel4"][:], sb["gT"][:],
                                    start=True, stop=True), "mm")
                    kb.wait(t, "dv", V + 3)
                    for o in range(NCH):
                        if o >= 2:
                            kb.wait(t, "dv", V + 2 + o)
                        kb.inc(t.matmul(psD[o % 2][:],
                                        sb["l0B"][0:64, o * 128:(o + 1) * 128],
                                        sb["wmid"][0:64, :], start=True, stop=True),
                               "mm")

                @blk.scalar
                def _(s):
                    kb.wait(s, "mm", M + NCH)
                    kb.inc(s.copy(sb["arf"][0:68, :], psA[:]), "ac")
                    kb.inc(s.copy(sb["midb"][0:64, :], psA[0:64, :]), "ac")
                    kb.wait(s, "dv", V + 1)
                    kb.wait(s, "mm", M + NCH + 4)
                    for qt in range(4):
                        kb.inc(s.activation(sb["e16"][:, qt * 4:(qt + 1) * 4],
                                            psT[:, qt * 68 + 64: qt * 68 + 68],
                                            AF.Exp, bias=sb["nm4"][:, qt:qt + 1]),
                               "ac")
                    kb.wait(s, "mm", M + NCH + 8)
                    kb.inc(s.copy(sb["gT"][:], psGT[:]), "ac")
                    for o in range(NCH):
                        kb.wait(s, "dv", V + 4 + o)
                        kb.inc(s.dma_start(
                            out=_chunked(xdst)[:, o, qg * QG:(qg + 1) * QG],
                            in_=sb["xo"][:, (o % 2) * QG:(o % 2 + 1) * QG]), "st", 16)
                        if xbdst is not None:
                            if o >= 2:
                                kb.wait(s, "s2", S2 + (o - 1) * 16)
                            kb.inc(s.copy(
                                sb["sq"][:, (o % 2) * QG:(o % 2 + 1) * QG],
                                sb["xo"][:, (o % 2) * QG:(o % 2 + 1) * QG]), "cb")
                            kb.wait(s, "cb", kb.v("cb"))
                            kb.inc(s.dma_start(
                                out=_chunked(xbdst)[:, o, qg * QG:(qg + 1) * QG],
                                in_=sb["sq"][:, (o % 2) * QG:(o % 2 + 1) * QG]),
                                "s2", 16)
                    if xbdst is not None:
                        kb.wait(s, "s2", kb.v("s2"))

                @blk.vector
                def _(v):
                    kb.wait(v, "mm", M + NCH + 4)
                    for qt in range(4):
                        inst = v.reduce_max(sb["nm4"][:, qt:qt + 1],
                                            psT[:, qt * 68 + 64: qt * 68 + 68],
                                            axis=AX.X, negate=True)
                    kb.bar(v, inst)
                    kb.inc(v.nop(), "dv")
                    kb.wait(v, "ac", A + 6)
                    for qt in range(4):
                        e4 = sb["e16"][:, qt * 4:(qt + 1) * 4]
                        msk = sb["scr4"][:, 0:4]; el2 = sb["scr4"][:, 4:8]
                        num = sb["scr4"][:, 8:12]
                        m2 = sb["scr1"][:, 0:1]; zz = sb["scr1"][:, 2:4]
                        iz = sb["scr1"][:, 4:6]
                        kb.bar(v, v.tensor_scalar(msk[:], e4, 1.0, None, ALU.is_ge))
                        kb.bar(v, v.scalar_tensor_tensor(el2[:], msk[:], NEG, e4,
                                                         ALU.mult, ALU.add))
                        kb.bar(v, v.reduce_max(m2[:], el2[:], axis=AX.X))
                        kb.bar(v, v.tensor_scalar(msk[:], e4, m2[:], None, ALU.is_ge))
                        kb.bar(v, v.tensor_mul(num[:], msk[:], e4))
                        kb.bar(v, v.reduce_sum(zz[:, 0:1], num[:], axis=AX.X))
                        kb.bar(v, v.tensor_copy(zz[:, 1:2], zz[:, 0:1]))
                        kb.bar(v, v.reciprocal(iz[:], zz[:]))
                        ginst = v.tensor_scalar(sb["gates4"][:, qt * 4:(qt + 1) * 4],
                                                num[:], iz[:, 0:1], 2.0, ALU.mult,
                                                ALU.mult)
                        kb.bar(v, ginst)
                    kb.inc(v.nop(), "dv")
                    if dbg and qg == 0:
                        v.tensor_copy(kb.sb["gdbg"][:, 40:44], sb["scr4"][:, 0:4])
                        v.tensor_copy(kb.sb["gdbg"][:, 44:48], sb["scr4"][:, 4:8])
                        v.tensor_copy(kb.sb["gdbg"][:, 48:52], sb["scr4"][:, 8:12])
                        v.tensor_copy(kb.sb["gdbg"][:, 52:53], sb["scr1"][:, 0:1])
                        v.tensor_copy(kb.sb["gdbg"][:, 53:55], sb["scr1"][:, 2:4])
                        v.tensor_copy(kb.sb["gdbg"][:, 55:57], sb["scr1"][:, 4:6])
                        v.tensor_copy(kb.sb["gdbg"][:, 0:16], sb["e16"][:])
                        v.tensor_copy(kb.sb["gdbg"][:, 16:32], sb["gates4"][:])
                        v.tensor_copy(kb.sb["gdbg"][:, 32:36], psT[:, 64:68])
                        v.tensor_copy(kb.sb["gdbg"][:, 36:40], sb["nm4"][:])
                    kb.wait(v, "mm", M + NCH + 9)
                    kb.wait(v, "ac", A + 2)
                    kb.inc(v.tensor_mul(sb["wmid"][0:64, :], sb["midb"][0:64, :],
                                        psG[:]), "dv")
                    for o in range(NCH):
                        kb.wait(v, "mm", M + NCH + 9 + o + 1)
                        kb.wait(v, "ld", L + (o + 1) * 16)
                        if o >= 2:
                            kb.wait(v, "st", S + (o - 1) * 16)
                            if xbdst is not None:
                                kb.wait(v, "cb", CB + o - 1)
                        kb.inc(v.tensor_add(
                            sb["xo"][:, (o % 2) * QG:(o % 2 + 1) * QG],
                            sb["xi"][:, (o % 3) * QG:(o % 3 + 1) * QG],
                            psD[o % 2][:]), "dv")

                @blk.sync
                def _(sy):
                    for o in range(NCH):
                        if o >= 3:
                            kb.wait(sy, "dv", V + 4 + o - 3)
                        kb.inc(sy.dma_start(
                            out=sb["xi"][:, (o % 3) * QG:(o % 3 + 1) * QG],
                            in_=_chunked(xsrc)[:, o, qg * QG:(qg + 1) * QG]), "ld", 16)
                    kb.wait(sy, "st", S + NCH * 16)
        nc.all_engine_barrier()


def emit_qkv(kb, layer, ins, qdst, kdst, vdst, part="all"):
    nc = kb.nc
    sb = kb.sb
    wsrc = ins["w_l1qkv"] if layer == 1 else ins["w_l0qkv"]
    nkg = 4 if layer == 1 else 1
    kw = 256 if layer == 1 else 128
    kmo = 2 if layer == 1 else 1

    groups = []
    if part in ("all", "q"):
        groups += [("q", g, 512 * g, 512, 4, True) for g in range(8)]
    if part in ("all", "kv", "k"):
        if layer == 1:
            nkg, kw, kmo = 2, 512, 4
        for gk in range(nkg):
            groups.append(("k", gk, 4096 + kw * gk, kw, kmo, False))

    slots = []
    mm_d, mm_q2, ac_evac, dv_sq, dv_mg = {}, {}, {}, {}, {}
    group_end_mm = {}
    st_i = {}
    mm = ac = 0
    dv = 1
    isq_list = []
    pend_q2 = [None]
    for gi, (kind, gidx, c0, wdt, nmo, isq) in enumerate(groups):
        for qg in range(2):
            for mo in range(nmo):
                i = len(slots)
                mm += 1; mm_d[i] = mm
                if pend_q2[0] is not None:
                    mm += 1; mm_q2[pend_q2[0]] = mm
                    pend_q2[0] = None
                if isq:
                    dv += 1; dv_sq[i] = dv
                    dv += 1; dv_mg[i] = dv
                    isq_list.append(i)
                    pend_q2[0] = i
                ac += 1; ac_evac[i] = ac
                slots.append((gi, qg, mo, isq, kind, gidx, nmo))
        group_end_mm[gi] = mm
    if pend_q2[0] is not None:
        mm += 1; mm_q2[pend_q2[0]] = mm
        pend_q2[0] = None
    nst = 0
    for gi, g in enumerate(groups):
        for qg in range(2):
            nst += 1
            st_i[(gi, qg)] = nst

    with ExitStack() as pc:
        wbuf = [pc.enter_context(nc.sbuf_tensor(kb.nm("qkvw"), [128, NCH * 512], BF16))
                for _ in range(2)]
        stage = pc.enter_context(nc.sbuf_tensor(kb.nm("qkvstage"), [128, 4 * 512], BF16))
        q2b = pc.enter_context(nc.sbuf_tensor(kb.nm(f"q2b"), [128, 512], BF16))
        psP = [pc.enter_context(nc.psum_tensor(kb.nm(f"qkvp{i}"), [128, QG], F32))
               for i in range(4)]
        psq2 = [pc.enter_context(nc.psum_tensor(kb.nm(f"q2p{i}"), [1, QG], F32))
                for i in range(2)]
        with nc.Block(kb.nm("qkv")) as blk:
            Lw = kb.v("lw"); M = kb.v("mm"); A = kb.v("ac")
            V = kb.v("dv"); S = kb.v("st")

            def wview(gi, d, lo, w):
                return wbuf[gi % 2][:].rearrange("p (d t) -> p d t", t=512)[:, d, lo:lo + w]

            @blk.gpsimd
            def _(g):
                for gi, (kind, gidx, c0, wdt, nmo, isq) in enumerate(groups):
                    if gi >= 2:
                        kb.wait(g, "mm", M + group_end_mm[gi - 2])
                    src = _chunked(wsrc)[:, :, c0:c0 + wdt]
                    dst = wbuf[gi % 2][:].rearrange("p (d t) -> p d t", t=512)[:, :, 0:wdt]
                    for part in range(4):
                        kb.inc(g.dma_start(
                            out=dst[:, part * 8:(part + 1) * 8, :],
                            in_=src[:, part * 8:(part + 1) * 8, :]), "lw", 16)

            @blk.tensor
            def _(t):
                nq2 = [0]
                pend = [None]

                def flush_q2():
                    if pend[0] is None:
                        return
                    pend[0] = None
                    kb.wait(t, "dv", V + dv_sq[isq_list[nq2[0]]])
                    if nq2[0] >= 2:
                        kb.wait(t, "dv", V + dv_mg[isq_list[nq2[0] - 2]])
                    kb.inc(t.matmul(psq2[nq2[0] % 2][:], sb["ones_colb"][:],
                                    q2b[:], start=True, stop=True), "mm")
                    nq2[0] += 1

                for i, (gi, qg, mo, isq, kind, gidx, nmo) in enumerate(slots):
                    if qg == 0 and mo == 0:
                        kb.wait(t, "lw", Lw + 64 * (gi + 1))
                    if i >= 4:
                        kb.wait(t, "ac", A + ac_evac[i - 4])
                    for d in range(NCH):
                        inst = t.matmul(psP[i % 4][:], wview(gi, d, mo * 128, 128),
                                        kb.xnv(d, qg), start=(d == 0),
                                        stop=(d == NCH - 1))
                    kb.inc(inst, "mm")
                    flush_q2()
                    if isq:
                        pend[0] = i
                flush_q2()

            @blk.scalar
            def _(s):
                for i, (gi, qg, mo, isq, kind, gidx, nmo) in enumerate(slots):
                    kb.wait(s, "mm", M + mm_d[i])
                    prev = (gi, qg - 1) if qg == 1 else ((gi - 1, 1) if gi >= 1 else None)
                    if prev is not None:
                        kb.wait(s, "st", S + 16 * st_i[prev])
                    if layer == 0:
                        bias = sb["l0qkvb"][:, 4 * gidx + mo: 4 * gidx + mo + 1] \
                            if kind == "q" else sb["l0qkvb"][:, 32:33]
                    else:
                        bias = 0.0
                    kb.inc(s.activation(
                        stage[:, mo * 512:(mo + 1) * 512],
                        psP[i % 4][:], AF.Identity, bias=bias), "ac")

            if part in ("all", "q"):
                @blk.vector
                def _(v):
                    v.memset(sb["runq2"][:], 0.0)
                    kb.inc(v.nop(), "dv")
                    for n, i in enumerate(isq_list):
                        gi, qg, mo, isq, kind, gidx, nmo = slots[i]
                        kb.wait(v, "ac", A + ac_evac[i])
                        if n >= 1:
                            kb.wait(v, "mm", M + mm_q2[isq_list[n - 1]])
                        sl = stage[:, mo * 512:(mo + 1) * 512]
                        kb.inc(v.tensor_mul(q2b[:], sl, sl), "dv")
                        kb.wait(v, "mm", M + mm_q2[i])
                        kb.inc(v.tensor_tensor(
                            sb["runq2"][:, qg * QG:(qg + 1) * QG],
                            sb["runq2"][:, qg * QG:(qg + 1) * QG],
                            psq2[n % 2][:], ALU.max), "dv")

            @blk.sync
            def _(sy):
                for gi, (kind, gidx, c0, wdt, nmo, isq) in enumerate(groups):
                    for qg in range(2):
                        last = [i for i, sl in enumerate(slots)
                                if sl[0] == gi and sl[1] == qg][-1]
                        kb.wait(sy, "ac", A + ac_evac[last])
                        dst = qdst if kind == "q" else kdst
                        r0 = nmo * gidx
                        kb.inc(sy.dma_start(
                            out=_chunked(dst)[:, r0:r0 + nmo, qg * QG:(qg + 1) * QG],
                            in_=stage[:, 0:nmo * 512].rearrange(
                                "p (m t) -> p m t", t=512)), "st", 16)
                kb.wait(sy, "st", kb.v("st"))
    nc.all_engine_barrier()
    if part in ("q", "k"):
        return

    vw = 1024 if layer == 1 else 128
    vc0 = 4096 + (1024 if layer == 1 else 128)
    nvh = 2 if layer == 1 else 1
    vhw = 512 if layer == 1 else 128
    with ExitStack() as pc:
        vwb = pc.enter_context(nc.sbuf_tensor(kb.nm("vwb"), [128, NCH * vhw], BF16))
        vstage = pc.enter_context(nc.sbuf_tensor(kb.nm("vstage"), [128, 2 * 512], BF16))
        psV = [pc.enter_context(nc.psum_tensor(kb.nm("vps"), [128, 512], F32))
               for i in range(2)]
        with nc.Block(kb.nm("vpass")) as blk:
            Lw = kb.v("lw"); M = kb.v("mm"); A = kb.v("ac"); S = kb.v("st")

            @blk.gpsimd
            def _(g):
                for vh in range(nvh):
                    if vh >= 1:
                        kb.wait(g, "mm", M + 8 * vh)
                    src = _chunked(wsrc)[:, :, vc0 + vh * vhw:vc0 + (vh + 1) * vhw]
                    dst = vwb[:].rearrange("p (d t) -> p d t", t=vhw)
                    for part in range(4):
                        kb.inc(g.dma_start(
                            out=dst[:, part * 8:(part + 1) * 8, :],
                            in_=src[:, part * 8:(part + 1) * 8, :]), "lw", 16)

            @blk.tensor
            def _(t):
                n = 0
                for vh in range(nvh):
                    kb.wait(t, "lw", Lw + 64 * (vh + 1))
                    for tt in range(8):
                        n += 1
                        if n > 2:
                            kb.wait(t, "ac", A + n - 2)
                        for d in range(NCH):
                            inst = t.matmul(
                                psV[n % 2][:, 0:vhw],
                                kb.xnv(d, tt // 4)[:, (tt % 4) * 128:(tt % 4 + 1) * 128],
                                vwb[:].rearrange("p (d t) -> p d t", t=vhw)[:, d, :],
                                start=(d == 0),
                                stop=(d == NCH - 1 and layer == 1))
                        if layer == 0:
                            inst = t.matmul(psV[n % 2][:, 0:vhw],
                                            sb["ones_rowb"][:, 0:128],
                                            sb["l0vb_row"][:], start=False, stop=True)
                        kb.inc(inst, "mm")

            @blk.scalar
            def _(s):
                n = 0
                for vh in range(nvh):
                    for tt in range(8):
                        n += 1
                        kb.wait(s, "mm", M + n)
                        if n > 2:
                            kb.wait(s, "st", S + (n - 2) * 16)
                        kb.inc(s.copy(vstage[:, (n % 2) * 512:(n % 2) * 512 + vhw],
                                      psV[n % 2][:, 0:vhw]), "ac")

            @blk.sync
            def _(sy):
                n = 0
                for vh in range(nvh):
                    for tt in range(8):
                        n += 1
                        kb.wait(sy, "ac", A + n)
                        kb.inc(sy.dma_start(
                            out=vdst[tt * 128:(tt + 1) * 128, vh * vhw:(vh + 1) * vhw],
                            in_=vstage[:, (n % 2) * 512:(n % 2) * 512 + vhw]), "st", 16)
                kb.wait(sy, "st", kb.v("st"))
    nc.all_engine_barrier()


def emit_rq_out(kb, rq_out):
    nc = kb.nc
    with nc.Block(kb.nm("rqout")) as blk:
        @blk.sync
        def _(sy):
            kb.inc(sy.dma_start(out=rq_out, in_=kb.sb["runq2"][:]), "st", 16)
            kb.wait(sy, "st", kb.v("st"))
    nc.all_engine_barrier()



def emit_ffn1(kb, ins, xsrc, xdst, xbdst=None):
    nc = kb.nc
    sb = kb.sb
    whctx = ExitStack()
    sb["whb"] = whctx.enter_context(
        nc.sbuf_tensor(kb.nm("whb"), [128, 16 * QG], BF16))
    wbuf = [whctx.enter_context(nc.sbuf_tensor(kb.nm("f1w"), [128, NCH * 256],
                                               BF16)) for _ in range(2)]
    bbuf = [whctx.enter_context(nc.sbuf_tensor(kb.nm("f1b"), [128, 8 * 256],
                                               BF16)) for _ in range(2)]
    LwP = kb.v("lw")

    def load_group(g_, og):
        src_ = _chunked(ins["w_l1base"])[:, :, og * 256:(og + 1) * 256]
        dst_ = wbuf[og % 2][:].rearrange("p (d t) -> p d t", t=256)
        for p_ in range(4):
            kb.inc(g_.dma_start(out=dst_[:, p_ * 8:(p_ + 1) * 8, :],
                                in_=src_[:, p_ * 8:(p_ + 1) * 8, :]), "lw", 16)
        kb.inc(g_.dma_start(
            out=bbuf[og % 2][:].rearrange("p (d t) -> p d t", t=256),
            in_=_chunked(ins["w_l1B"])[:, :, og * 256:(og + 1) * 256]), "lw", 16)


    with ExitStack() as pc:
        psM = [pc.enter_context(nc.psum_tensor(kb.nm("f1M"), [128, QG], F32))
               for i in range(2)]
        psR = [pc.enter_context(nc.psum_tensor(kb.nm("f1R"), [16, QG], F32))
               for i in range(2)]
        psT = pc.enter_context(nc.psum_tensor(kb.nm("f1T"), [128, 4 * 16], F32))
        psCT = pc.enter_context(nc.psum_tensor(kb.nm("f1CT"), [16, QG], BF16))
        psC = pc.enter_context(nc.psum_tensor(kb.nm("f1C"), [128, QG], F32))
        mid2 = pc.enter_context(nc.sbuf_tensor(kb.nm("mid2"), [128, 2 * QG], BF16))
        e16t = pc.enter_context(nc.sbuf_tensor(kb.nm("e16t"), [128, 16], F32))
        nm1 = pc.enter_context(nc.sbuf_tensor(kb.nm("nm1"), [128, 4], F32))
        gall = pc.enter_context(nc.sbuf_tensor(kb.nm("gall"), [128, 64], BF16))
        cT = pc.enter_context(nc.sbuf_tensor(kb.nm("cT"), [16, QG], BF16))
        for qg in range(2):
            with nc.Block(kb.nm("f1a")) as blk:
                M = kb.v("mm"); A = kb.v("ac"); V = kb.v("dv")

                if qg == 0:
                    @blk.gpsimd
                    def _(g):
                        load_group(g, 0)
                        load_group(g, 1)

                @blk.tensor
                def _(t):
                    for d in range(NCH):
                        t.matmul(psM[qg][:], sb["l1ar"][:, d * 144:d * 144 + 128],
                                 kb.xnv(d, qg), start=(d == 0), stop=(d == NCH - 1))
                        kb.inc(t.matmul(psR[qg][:], sb["l1ar"][:, d * 144 + 128:(d + 1) * 144],
                                        kb.xnv(d, qg), start=(d == 0),
                                        stop=(d == NCH - 1)), "mm")
                    kb.wait(t, "ac", A + 1)
                    for qt in range(4):
                        kb.inc(t.transpose(psT[:, qt * 16:(qt + 1) * 16],
                                           sb["arf"][0:16, qt * 128:(qt + 1) * 128],
                                           sb["identf"][0:16, 0:16]), "mm")
                    kb.wait(t, "dv", V + 2)
                    for qt in range(4):
                        kb.inc(t.transpose(psCT[:, qt * 128:(qt + 1) * 128],
                                           gall[:, qt * 16:(qt + 1) * 16],
                                           sb["identb"][:]), "mm")
                    kb.wait(t, "ac", A + 3)
                    for p in range(8):
                        if p >= 1:
                            kb.wait(t, "dv", V + 2 + p)
                        kb.inc(t.matmul(psC[:], sb["selp"][:, 128 * p:128 * (p + 1)],
                                        cT[:], start=True, stop=True), "mm")

                @blk.scalar
                def _(s):
                    kb.wait(s, "mm", M + NCH)
                    kb.inc(s.copy(sb["arf"][0:16, :], psR[qg][:]), "ac")
                    kb.inc(s.copy(mid2[:, qg * QG:(qg + 1) * QG], psM[qg][:]), "ac")
                    kb.wait(s, "dv", V + 1)
                    kb.wait(s, "mm", M + NCH + 4)
                    GX0 = kb.v("gx")
                    for qt in range(4):
                        if qt >= 1:
                            kb.wait(s, "gx", GX0 + qt)
                        kb.inc(s.activation(e16t[:], psT[:, qt * 16:(qt + 1) * 16],
                                            AF.Exp, bias=nm1[:, qt:qt + 1]), "ex")
                    kb.wait(s, "mm", M + NCH + 8)
                    kb.inc(s.copy(cT[:], psCT[:]), "ac")

                @blk.vector
                def _(v):
                    kb.wait(v, "mm", M + NCH + 4)
                    for qt in range(4):
                        inst = v.reduce_max(nm1[:, qt:qt + 1],
                                            psT[:, qt * 16:(qt + 1) * 16],
                                            axis=AX.X, negate=True)
                    kb.bar(v, inst)
                    kb.inc(v.nop(), "dv")
                    EX = kb.v("ex") - 4
                    for qt in range(4):
                        kb.wait(v, "ex", EX + qt + 1)
                        e4 = e16t[:]
                        msk = sb["scr4"][:, 0:16]
                        m2 = sb["scr1"][:, 0:1]; zz0 = sb["scr1"][:, 2:4]
                        iz = sb["scr1"][:, 4:6]; ks = sb["scr1"][:, 6:8]
                        el2 = sb["scrB"][:, 0:16]; num = sb["scrB"][:, 16:32]
                        kb.bar(v, v.reduce_sum(zz0[:, 0:1], e4, axis=AX.X))
                        kb.bar(v, v.tensor_scalar(msk, e4, 1.0, None, ALU.is_ge))
                        kb.bar(v, v.scalar_tensor_tensor(el2, msk, NEG, e4,
                                                         ALU.mult, ALU.add))
                        kb.bar(v, v.reduce_max(m2, el2, axis=AX.X))
                        kb.bar(v, v.tensor_scalar(msk, e4, m2[:], None, ALU.is_ge))
                        kb.bar(v, v.tensor_mul(num, msk, e4))
                        kb.bar(v, v.reduce_sum(ks[:, 0:1], num, axis=AX.X))
                        kb.bar(v, v.scalar_tensor_tensor(ks[:, 1:2], zz0[:, 0:1], 1e-6,
                                                         ks[:, 0:1], ALU.mult, ALU.add))
                        kb.bar(v, v.tensor_copy(ks[:, 0:1], ks[:, 1:2]))
                        kb.bar(v, v.reciprocal(iz[:], ks[:, 0:2]))
                        ginst = v.tensor_scalar(gall[:, qt * 16:(qt + 1) * 16],
                                                num, iz[:, 0:1], 0.5,
                                                ALU.mult, ALU.mult)
                        kb.bar(v, ginst)
                        kb.inc(v.nop(), "gx")
                    kb.inc(v.nop(), "dv")
                    M2 = M + NCH + 8
                    for p in range(8):
                        kb.wait(v, "mm", M2 + 1 + p)
                        kb.inc(v.tensor_mul(
                            sb["whb"][:, (qg * 8 + p) * QG:(qg * 8 + p + 1) * QG],
                            mid2[:, qg * QG:(qg + 1) * QG], psC[:]), "dv")
            nc.all_engine_barrier()

    NOG = 16
    with ExitStack() as pc:
        bbr = pc.enter_context(nc.sbuf_tensor(kb.nm("bbr"), [1, D], BF16))
        psB2 = [pc.enter_context(nc.psum_tensor(kb.nm("f1ps"), [128, QG], F32))
                for i in range(4)]
        with nc.Block(kb.nm("f1base")) as blk:
            Lw = kb.v("lw"); M = kb.v("mm"); V = kb.v("dv")
            L = kb.v("ld"); S = kb.v("st"); LB = kb.v("lb")
            CB = kb.v("cb")

            @blk.gpsimd
            def _(g):
                kb.inc(g.dma_start(
                    out=bbr[:],
                    in_=ins["b_l1base"].rearrange("o one -> one o")), "lb", 16)
                for og in range(2, NOG):
                    kb.wait(g, "mm", M + (og - 1) * 4)
                    load_group(g, og)

            @blk.tensor
            def _(t):
                j = 0
                kb.wait(t, "lb", LB + 16)
                for og in range(NOG):
                    kb.wait(t, "lw", LwP + 80 * (og + 1))
                    for qg in range(2):
                        for mo in range(2):
                            j += 1
                            if j > 4:
                                kb.wait(t, "dv", V + j - 4)
                            for d in range(NCH):
                                t.matmul(psB2[j % 4][:],
                                         wbuf[og % 2][:].rearrange("p (d t) -> p d t", t=256)[:, d, mo * 128:(mo + 1) * 128],
                                         kb.xnv(d, qg),
                                         start=(d == 0), stop=False)
                            t.matmul(psB2[j % 4][:],
                                     bbr[0:1, og * 256 + mo * 128:
                                         og * 256 + (mo + 1) * 128],
                                     sb["ones_rowb"][:], start=False, stop=False)
                            for p in range(8):
                                inst = t.matmul(
                                    psB2[j % 4][:],
                                    bbuf[og % 2][:].rearrange("p (d t) -> p d t", t=256)[:, p, mo * 128:(mo + 1) * 128],
                                    sb["whb"][:, (qg * 8 + p) * QG:(qg * 8 + p + 1) * QG],
                                    start=False, stop=(p == 7))
                            kb.inc(inst, "mm")

            @blk.vector
            def _(v):
                j = 0
                for og in range(NOG):
                    for qg in range(2):
                        for mo in range(2):
                            j += 1
                            kb.wait(v, "mm", M + j)
                            kb.wait(v, "ld", L + j * 16)
                            if j > 2:
                                kb.wait(v, "st", S + (j - 2) * 16)
                                if xbdst is not None:
                                    kb.wait(v, "cb", CB + j - 2)
                            kb.inc(v.scalar_tensor_tensor(
                                sb["xo"][:, (j % 2) * QG:(j % 2 + 1) * QG],
                                psB2[j % 4][:], 0.5,
                                sb["xi"][:, (j % 3) * QG:(j % 3 + 1) * QG],
                                ALU.mult, ALU.add), "dv")

            @blk.scalar
            def _(sc):
                S2 = kb.v("s2")
                order = [(og, qg, mo) for og in range(NOG) for qg in range(2)
                         for mo in range(2)]
                for j in range(1, len(order) + 1):
                    og, qg, mo = order[j - 1]
                    kb.wait(sc, "dv", V + j)
                    kb.inc(sc.dma_start(
                        out=_chunked(xdst)[:, og * 2 + mo, qg * QG:(qg + 1) * QG],
                        in_=sb["xo"][:, (j % 2) * QG:((j % 2) + 1) * QG]), "st", 16)
                    if xbdst is not None:
                        if j >= 3:
                            kb.wait(sc, "s2", S2 + (j - 2) * 16)
                        kb.inc(sc.copy(
                            sb["sq"][:, (j % 2) * QG:(j % 2 + 1) * QG],
                            sb["xo"][:, (j % 2) * QG:((j % 2) + 1) * QG]), "cb")
                        kb.wait(sc, "cb", kb.v("cb"))
                        kb.inc(sc.dma_start(
                            out=_chunked(xbdst)[:, og * 2 + mo,
                                                qg * QG:(qg + 1) * QG],
                            in_=sb["sq"][:, (j % 2) * QG:(j % 2 + 1) * QG]),
                            "s2", 16)
                kb.wait(sc, "st", kb.v("st"))
                if xbdst is not None:
                    kb.wait(sc, "s2", kb.v("s2"))

            @blk.sync
            def _(sy):
                order = [(og, qg, mo) for og in range(NOG) for qg in range(2)
                         for mo in range(2)]
                for j in range(1, len(order) + 1):
                    og, qg, mo = order[j - 1]
                    if j > 3:
                        kb.wait(sy, "dv", V + j - 3)
                    kb.inc(sy.dma_start(
                        out=sb["xi"][:, (j % 3) * QG:((j % 3) + 1) * QG],
                        in_=_chunked(xsrc)[:, og * 2 + mo, qg * QG:(qg + 1) * QG]),
                        "ld", 16)
    whctx.close()
    nc.all_engine_barrier()


def build_launch_A():
    nc = bass.Bass("TRN2", target_bir_lowering=False, num_devices=NCORES)
    ins = {}
    ins["x"] = nc.dram_tensor("x", [D, TL], F32, kind="ExternalInput").ap()
    ins["w_l0ar"] = nc.dram_tensor("w_l0ar", [D, 68], F32, kind="ExternalInput").ap()
    ins["w_l0B"] = nc.dram_tensor("w_l0B", [64, D], F32, kind="ExternalInput").ap()
    ins["w_l0qkv"] = nc.dram_tensor("w_l0qkv", [D, 4352], F32, kind="ExternalInput").ap()
    ins["b_l0qkv"] = nc.dram_tensor("b_l0qkv", [4352, 1], F32, kind="ExternalInput").ap()
    x1 = nc.dram_tensor("x1", [D, TL], F32, kind="ExternalOutput").ap()
    q0 = nc.dram_tensor("q0", [D, TL], BF16, kind="ExternalOutput").ap()
    k0c = nc.dram_tensor("k0c", [128, TL], BF16, kind="ExternalOutput").ap()
    v0c = nc.dram_tensor("v0c", [TL, 128], BF16, kind="ExternalOutput").ap()
    rq0 = nc.dram_tensor("rq0", [1, TL], F32, kind="ExternalOutput").ap()
    cst = _const_tensors(nc)
    with ExitStack() as ctx:
        kb = KB(nc, ctx)
        alloc_common(kb)
        kb.alloc("sel4", [4, 64], BF16)
        kb.alloc("l0ar", [128, NCH * 68], BF16)
        kb.alloc("l0B", [64, D], BF16)
        kb.alloc("l0qkvb", [128, 34], F32)
        kb.alloc("l0vb_row", [1, 128], BF16)
        kb.alloc("arf", [68, QG], F32)
        kb.alloc("midb", [64, QG], BF16)
        kb.alloc("e16", [128, 16], F32)
        kb.alloc("nm4", [128, 4], F32)
        kb.alloc("gates4", [128, 16], BF16)
        kb.alloc("gT", [4, QG], BF16)
        kb.alloc("wmid", [64, QG], BF16)
        emit_setup(kb, cst, ins, {"l0ffn", "l0qkvb"})
        emit_ln(kb, ins["x"])
        emit_ffn0(kb, ins["x"], x1)
        emit_ln(kb, x1)
        emit_qkv(kb, 0, ins, q0, k0c, v0c)
        emit_rq_out(kb, rq0)
    return nc



def emit_attn_layer_pre(kb, layer, kg, vg):
    width = 8192 if layer == 1 else (2048, 4096)
    st = ExitStack()
    kb.sb["_attn_stack"] = st
    if layer == 1:
        for nm_ in ("kloc0", "kloc1", "vloc0", "vloc1"):
            kb.sb[nm_] = st.enter_context(
                kb.nc.sbuf_tensor(kb.nm(nm_), [128, 8192], BF16))
    else:
        kb.sb["kloc0"] = st.enter_context(kb.nc.sbuf_tensor(kb.nm("kloc0"), [128, 2048], BF16))
        kb.sb["vloc0"] = st.enter_context(kb.nc.sbuf_tensor(kb.nm("vloc0"), [128, 2048], BF16))
        kb.sb["kloc1"] = st.enter_context(kb.nc.sbuf_tensor(kb.nm("kloc1"), [128, 4096], BF16))
        kb.sb["vloc1"] = st.enter_context(kb.nc.sbuf_tensor(kb.nm("vloc1"), [128, 4096], BF16))
    emit_assemble(kb, layer, kg, vg)
    emit_k2(kb, layer)


def emit_attn_layer_post(kb):
    kb.sb.pop("_attn_stack").close()


def emit_assemble(kb, layer, kg, vg):
    nc = kb.nc
    sb = kb.sb
    u = kb.uid
    with nc.Block(kb.nm("asm")) as blk:
        if layer == 0:
            @blk.sync
            def _(sy):
                kb.wait(sy, "cc", kb.v("cc"))
                i = 0
                for s, nent in ((0, 4), (1, 8)):
                    base = 0 if s == 0 else 12
                    for e in range(nent):
                        i += 1
                        with sy.register(f"ka{u}_{i}") as r0, \
                             sy.register(f"kb{u}_{i}") as r1:
                            sy.reg_load(r0, sb["sched"][0:1, base + 3 * e: base + 3 * e + 1])
                            sy.reg_load(r1, sb["sched"][0:1, base + 3 * e + 1: base + 3 * e + 2])
                            krow = sy.snap(r0, donate=True, min_val=0, max_val=896)
                            kcol = sy.snap(r1, donate=True, min_val=0, max_val=512)
                            kb.inc(sy.dma_start(
                                out=sb[f"kloc{s}"][:, 512 * e: 512 * e + 512],
                                in_=kg[bass.ds(krow, 128), bass.ds(kcol, 512)]), "as", 16)
                kb.wait(sy, "as", kb.v("as"))

            @blk.gpsimd
            def _(g):
                kb.wait(g, "cc", kb.v("cc"))
                i = 0
                for s, nent in ((0, 4), (1, 8)):
                    base = 0 if s == 0 else 12
                    for e in range(nent):
                        i += 1
                        with g.register(f"va{u}_{i}") as r2:
                            g.reg_load(r2, sb["sched"][0:1, base + 3 * e + 2: base + 3 * e + 3])
                            vblk = g.snap(r2, donate=True, min_val=0, max_val=15)
                            kb.inc(g.dma_start(
                                out=sb[f"vloc{s}"][:, 512 * e: 512 * e + 512].rearrange(
                                    "p (one c h) -> p one c h", one=1, h=128),
                                in_=vg.rearrange("(r c p) h -> r p c h", c=4, p=128)[
                                    bass.ds(vblk, 1), :, :, :]), "av", 16)
                kb.wait(g, "av", kb.v("av"))
        else:
            @blk.scalar
            def _(sc):
                kb.wait(sc, "cc", kb.v("cc"))
                i = 0
                for s in range(2):
                    for e in range(2):
                        i += 1
                        kbase = 36 + (s * 2 + e) * 2
                        with sc.register(f"ra{u}_{i}") as r0, \
                             sc.register(f"rb{u}_{i}") as r1, \
                             sc.register(f"rc{u}_{i}") as r2:
                            sc.reg_load(r0, sb["sched"][0:1, kbase:kbase + 1])
                            sc.reg_load(r1, sb["sched"][0:1, kbase + 1:kbase + 2])
                            rr = sc.snap(r0, donate=True, min_val=0, max_val=7)
                            cb = sc.snap(r1, donate=True, min_val=0, max_val=1)
                            src = kg.rearrange("(r h p) (cb c) -> cb r p h c",
                                               h=8, p=128, c=512)[
                                bass.ds(cb, 1), bass.ds(rr, 1), :, :, :]
                            dst = sb[f"kloc{s}"][:].rearrange(
                                "p (h ec c) -> p h ec c", h=8, ec=2, c=512)[:, :, e:e + 1, :]
                            kb.inc(sc.dma_start(out=dst, in_=src), "as", 16)
                            sc.reg_load(r2, sb["sched"][0:1, 100 + s * 2 + e: 101 + s * 2 + e])
                            vblk = sc.snap(r2, donate=True, min_val=0, max_val=15)
                            dst = sb[f"vloc{s}"][:].rearrange(
                                "p (ec c h hd) -> p ec c h hd", ec=2, c=4, h=8)[:, e, :, :, :]
                            kb.inc(sc.dma_start(
                                out=dst,
                                in_=vg.rearrange("(r c p) (h hd) -> r p c h hd",
                                                 c=4, p=128, h=8)[
                                    bass.ds(vblk, 1), :, :, :, :]), "as", 16)
                kb.wait(sc, "as", kb.v("as"))
    nc.all_engine_barrier()


def emit_k2(kb, layer):
    nc = kb.nc
    sb = kb.sb
    slices = []
    if layer == 0:
        for j in range(8):
            slices.append(sb["kloc1"][:, 512 * j:512 * (j + 1)])
    else:
        for s in range(2):
            for j in range(16):
                slices.append(sb[f"kloc{s}"][:, 512 * j:512 * (j + 1)])
    with ExitStack() as pc:
        psk = [pc.enter_context(nc.psum_tensor(kb.nm("psk"), [1, 512], F32))
               for i in range(2)]
        k2b = pc.enter_context(nc.sbuf_tensor(kb.nm("k2b"), [128, 512], BF16))
        with nc.Block(kb.nm("k2")) as blk:
            M = kb.v("mm"); V = kb.v("dv")

            @blk.vector
            def _(v):
                kb.bar(v, v.memset(sb["row_m"][:], NEG))
                for j, sl in enumerate(slices):
                    if j >= 1:
                        kb.wait(v, "mm", M + j)
                    kb.inc(v.tensor_mul(k2b[:], sl, sl), "dv")
                    if j >= 1:
                        v.tensor_tensor(sb["row_m"][:], sb["row_m"][:],
                                        psk[(j - 1) % 2][:], ALU.max)
                kb.wait(v, "mm", M + len(slices))
                kb.bar(v, v.tensor_tensor(sb["row_m"][:], sb["row_m"][:],
                                          psk[(len(slices) - 1) % 2][:], ALU.max))
                kb.bar(v, v.reduce_max(sb["k2m"][:, 0:1], sb["row_m"][:],
                                       axis=AX.X))
                kb.bar(v, v.tensor_copy(sb["k2m"][:, 1:2], sb["k2m"][:, 0:1]))
                kb.inc(v.nop(), "dv")

            @blk.tensor
            def _(t):
                for j in range(len(slices)):
                    kb.wait(t, "dv", V + j + 1)
                    kb.inc(t.matmul(psk[j % 2][:], sb["ones_colb"][:], k2b[:],
                                    start=True, stop=True), "mm")
    nc.all_engine_barrier()


def emit_attn(kb, layer, qsrc):
    nc = kb.nc
    sb = kb.sb
    nkc = L1_NKC if layer == 1 else L0_NKC
    nmask = 8 if layer == 1 else 4
    padcol = (48, 56) if layer == 1 else (0, 16)
    with ExitStack() as pc:
        psS = [pc.enter_context(nc.psum_tensor(kb.nm("psS"), [128, QG], F32))
               for i in range(3)]
        psO = [pc.enter_context(nc.psum_tensor(kb.nm("psO"), [128, QG], F32))
               for i in range(3)]
        psZ = [pc.enter_context(nc.psum_tensor(kb.nm("psZ"), [1, QG], F32))]
        psB = pc.enter_context(nc.psum_tensor(kb.nm("psB"), [128, QG], F32))
        psU = psB
        qh = pc.enter_context(nc.sbuf_tensor(kb.nm("qh"), [128, 3 * QG], BF16))
        pt = pc.enter_context(nc.sbuf_tensor(kb.nm("pt"), [128, 3 * QG], BF16))
        zacc = sb["xi"][:].bitcast(BF16)
        izb = sb["xo"][:].bitcast(BF16)
        lnb = sb["xo"][:, QG:2 * QG]
        bcb = pc.enter_context(nc.sbuf_tensor(kb.nm("bcb"), [128, QG], BF16))
        bmat = pc.enter_context(nc.sbuf_tensor(kb.nm("bmat"), [128, 64], F32))
        kb.sb["_dbg_bmat"] = bmat
        ub = pc.enter_context(nc.sbuf_tensor(kb.nm("ub"), [128, 1], F32))
        uacc = pc.enter_context(nc.sbuf_tensor(kb.nm("uacc"), [1, 4], F32))
        kb.sb["_dbg_uacc"] = uacc

        with nc.Block(kb.nm("attn")) as blk:
            SM = kb.v("sm"); EX = kb.v("ex"); PT = kb.v("pt"); ZZ = kb.v("zz")
            IV = kb.v("iv"); BE = kb.v("be"); AO = kb.v("ao"); BZ = kb.v("bz")
            QR = kb.v("qr"); LQ = kb.v("lq"); PU = kb.v("pu"); FU = kb.v("fu")
            PV = kb.v("pv"); ZA = kb.v("za"); ZR = kb.v("zr")
            PF = kb.v("pf")

            iters = [(qg, h) for qg in range(2) for h in range(32)]
            csm, cex, cpt, czz, cz4 = [0], [0], [0], [0], [0]
            sm_at, ex_at, pt_at, zz_at, z4_at = [], [], [], [], []
            for (qg, h) in iters:
                sm_at.append(csm[0]); ex_at.append(cex[0])
                pt_at.append(cpt[0]); zz_at.append(czz[0])
                z4_at.append(cz4[0])
                csm[0] += nkc[qg]; cex[0] += nkc[qg]
                cpt[0] += min(nmask, nkc[qg]); czz[0] += nkc[qg]
                cz4[0] += nkc[qg] // 4

            @blk.sync
            def _(sy):
                for it, (qg, h) in enumerate(iters):
                    if it >= 3:
                        kb.wait(sy, "qr", QR + it - 2)
                    kb.inc(sy.dma_start(
                        out=qh[:, (it % 3) * QG:(it % 3 + 1) * QG],
                        in_=_chunked(qsrc)[:, h, qg * QG:(qg + 1) * QG]), "lq", 16)

            @blk.vector
            def _(v):
                for qg in range(2):
                    kb.bar(v, v.reduce_max(uacc[:, 0:1],
                                           sb["runq2"][:, qg * QG:(qg + 1) * QG],
                                           axis=AX.X))
                    kb.bar(v, v.tensor_copy(uacc[:, 1:2], uacc[:, 0:1]))
                    kb.bar(v, v.tensor_mul(uacc[:, 0:2], uacc[:, 0:2], sb["k2m"][:]))
                    kb.inc(v.nop(), "pu")
                    kb.wait(v, "fu", FU + qg + 1)
                    kb.bar(v, v.tensor_scalar(uacc[:, 2:4], uacc[:, 2:4], -SC, -1.0,
                                              ALU.mult, ALU.add))
                    kb.inc(v.nop(), "pu")
                    kb.wait(v, "bz", BZ + qg * 33 + 1)
                    kb.bar(v, v.tensor_scalar(bmat[:, 32 * qg:32 * qg + nkc[qg]],
                                              sb["pads"][:, padcol[qg]:padcol[qg] + nkc[qg]],
                                              psU[:, 0:1], None, ALU.add))
                    kb.inc(v.nop(), "pu")
                    for h in range(32):
                        it = qg * 32 + h
                        n = nkc[qg]
                        zsl = zacc[:, (it % 2) * QG:(it % 2 + 1) * QG]
                        for kc in range(n):
                            gz = zz_at[it] + kc
                            psl = pt[:, (gz % 3) * QG:(gz % 3 + 1) * QG]
                            kb.wait(v, "ex", EX + ex_at[it] + kc + 1)
                            if kc < min(nmask, n):
                                mvar = kc if (layer == 0 or kc >= 4) else (4 + kc)
                                if layer == 1 and kc >= 4:
                                    mvar = kc - 4
                                kb.inc(v.tensor_mul(
                                    psl, psl,
                                    sb["masks"][:, mvar * 512:(mvar + 1) * 512]),
                                    "pt")
                            if kc % 4 == 0:
                                if kc == 0:
                                    if it >= 2:
                                        kb.wait(v, "zz", ZZ + z4_at[it - 1])
                                else:
                                    kb.wait(v, "zz", ZZ + z4_at[it] + kc // 4)
                                kb.inc(v.tensor_copy(zsl, psl), "za")
                            else:
                                kb.inc(v.tensor_add(zsl, zsl, psl), "za")
                        kb.wait(v, "zz", ZZ + z4_at[it] + n // 4)
                        if it >= 2:
                            kb.wait(v, "bz", BZ + it + (1 if it >= 34 else 0))
                        kb.inc(v.tensor_copy(
                            izb[0:1, (it % 2) * QG:(it % 2 + 1) * QG],
                            psZ[0][:]), "zr")
                        eits = ([it - 1] if h >= 1 else []) + ([it] if h == 31 else [])
                        for eit in eits:
                            kb.wait(v, "be", BE + eit + 1)
                            kb.inc(v.tensor_mul(kb.xnv(eit - qg * 32, qg),
                                                psO[eit % 3][:], bcb[:]), "ao")

            @blk.scalar
            def _(s):
                for qg in range(2):
                    kb.wait(s, "pu", PU + qg * 3 + 1)
                    kb.bar(s, s.activation(uacc[:, 2:4], uacc[:, 0:2], AF.Ln), "ss")
                    kb.bar(s, s.activation(uacc[:, 2:4], uacc[:, 2:4], AF.Exp,
                                           scale=0.5), "ss")
                    kb.inc(s.nop(), "fu")
                    def bcbe(j):
                        kb.wait(s, "bz", BZ + j + 2 + (1 if j >= 32 else 0))
                        kb.inc(s.activation(lnb, psB[:], AF.Ln), "pf")
                        kb.wait(s, "ao", AO + max(0, j))
                        kb.inc(s.activation(bcb[:], lnb, AF.Exp, scale=-1.0),
                               "be")

                    for h in range(32):
                        it = qg * 32 + h
                        if it == qg * 32:
                            kb.wait(s, "pu", PU + qg * 3 + 3)
                        for kc in range(nkc[qg]):
                            gz = zz_at[it] + kc
                            kb.wait(s, "sm", SM + sm_at[it] + kc + 1)
                            if gz >= 3:
                                kb.wait(s, "pv", PV + gz - 2)
                                kb.wait(s, "za", ZA + gz - 2)
                            kb.inc(s.activation(
                                pt[:, (gz % 3) * QG:(gz % 3 + 1) * QG],
                                psS[(sm_at[it] + kc) % 3][:], AF.Exp,
                                bias=bmat[:, 32 * qg + kc:32 * qg + kc + 1],
                                scale=SC), "ex")
                        if h >= 1:
                            bcbe(it - 1)
                        if h == 31:
                            bcbe(it)

            @blk.tensor
            def _(t):
                pend_bc = [None]

                def flush_bc():
                    if pend_bc[0] is None:
                        return
                    it = pend_bc[0]
                    pend_bc[0] = None
                    kb.wait(t, "zr", ZR + it + 1)
                    if it >= 1:
                        kb.wait(t, "pf", PF + it)
                    kb.inc(t.matmul(psB[:], sb["ones_rowb"][:, 0:128],
                                    izb[0:1, (it % 2) * QG:(it % 2 + 1) * QG],
                                    start=True, stop=True), "bz")

                for qg in range(2):
                    flush_bc()
                    kb.wait(t, "pu", PU + qg * 3 + 2)
                    if qg == 1:
                        kb.wait(t, "pf", PF + 32)
                    kb.inc(t.matmul(psU[:, 0:1], sb["ones_rowf"][:], uacc[0:1, 3:4],
                                    start=True, stop=True), "bz")
                    for h in range(32):
                        it = qg * 32 + h
                        kb.wait(t, "lq", LQ + (it + 1) * 16)
                        n = nkc[qg]

                        def s_mm(kc, it=it, qg=qg, h=h):
                            g = sm_at[it] + kc
                            if ex_at[it] + kc - 2 > 0:
                                kb.wait(t, "ex", EX + ex_at[it] + kc - 2)
                            kb.inc(t.matmul(
                                psS[g % 3][:],
                                sb[f"kloc{qg}"][:, 128 * kc:128 * (kc + 1)]
                                if layer == 0 else
                                sb[f"kloc{qg}"][:, 1024 * (h // 4) + 128 * kc:
                                                1024 * (h // 4) + 128 * (kc + 1)],
                                qh[:, (it % 3) * QG:(it % 3 + 1) * QG],
                                start=True, stop=True), "sm")

                        def pv_mm(kc, it=it, qg=qg, h=h, n=n):
                            if kc < min(nmask, n):
                                kb.wait(t, "pt", PT + pt_at[it] + kc + 1)
                            else:
                                kb.wait(t, "ex", EX + ex_at[it] + kc + 1)
                            if kc == 0:
                                kb.wait(t, "ao", AO + max(0, it - 2))
                            vsl = (sb[f"vloc{qg}"][:, 128 * kc:128 * (kc + 1)]
                                   if layer == 0 else
                                   sb[f"vloc{qg}"][:, (kc * 8 + (h // 4)) * 128:
                                                   (kc * 8 + (h // 4) + 1) * 128])
                            gz = zz_at[it] + kc
                            kb.inc(t.matmul(psO[it % 3][:], vsl,
                                            pt[:, (gz % 3) * QG:(gz % 3 + 1) * QG],
                                            start=(kc == 0), stop=(kc == n - 1)), "pv")

                        def z_mm(kc, it=it, n=n):
                            kb.wait(t, "za", ZA + zz_at[it] + kc + 1)
                            if kc == 3:
                                kb.wait(t, "zr", ZR + it)
                            kb.inc(t.matmul(psZ[0][:], sb["ones_colb"][:],
                                            zacc[:, (it % 2) * QG:(it % 2 + 1) * QG],
                                            start=(kc == 3), stop=(kc == n - 1)),
                                   "zz")

                        s_mm(0)
                        s_mm(1)
                        s_mm(2)
                        for kc in range(n):
                            if kc + 3 < n:
                                s_mm(kc + 3)
                            if kc + 3 == n:
                                kb.inc(t.nop(), "qr")
                            pv_mm(kc)
                            if kc % 4 == 3:
                                z_mm(kc)
                            if kc == 3:
                                flush_bc()
                        pend_bc[0] = it
                    flush_bc() if qg == 1 else None
    nc.all_engine_barrier()


def emit_oproj(kb, layer, wsrc, xsrc, xdst, has_bias, xbdst=None):
    nc = kb.nc
    sb = kb.sb
    NOG = 16
    with ExitStack() as pc:
        wbuf = [pc.enter_context(nc.sbuf_tensor(kb.nm("wow"), [128, NCH * 256], BF16))
                for _ in range(2)]
        psO2 = [pc.enter_context(nc.psum_tensor(kb.nm("psO2"), [128, QG], F32))
                for i in range(4)]
        with nc.Block(kb.nm("oproj")) as blk:
            Lw = kb.v("lw"); M = kb.v("mm"); V = kb.v("dv")
            L = kb.v("ld"); S = kb.v("st"); CB = kb.v("cb")
            order = [(og, qg, mo) for og in range(NOG) for qg in range(2)
                     for mo in range(2)]

            @blk.gpsimd
            def _(g):
                for og in range(NOG):
                    if og >= 2:
                        kb.wait(g, "mm", M + (og - 1) * 4)
                    src = _chunked(wsrc)[:, :, og * 256:(og + 1) * 256]
                    dst = wbuf[og % 2][:].rearrange("p (d t) -> p d t", t=256)
                    for part in range(4):
                        kb.inc(g.dma_start(
                            out=dst[:, part * 8:(part + 1) * 8, :],
                            in_=src[:, part * 8:(part + 1) * 8, :]), "lw", 16)

            @blk.tensor
            def _(t):
                n = 0
                for og in range(NOG):
                    kb.wait(t, "lw", Lw + 64 * (og + 1))
                    for qg in range(2):
                        for mo in range(2):
                            n += 1
                            if n > 4:
                                kb.wait(t, "dv", V + n - 4)
                            for hch in range(NCH):
                                inst = t.matmul(
                                    psO2[n % 4][:],
                                    wbuf[og % 2][:].rearrange("p (d t) -> p d t", t=256)[:, hch, mo * 128:(mo + 1) * 128],
                                    kb.xnv(hch, qg),
                                    start=(hch == 0), stop=(hch == NCH - 1))
                            kb.inc(inst, "mm")

            @blk.vector
            def _(v):
                n = 0
                for og in range(NOG):
                    for qg in range(2):
                        for mo in range(2):
                            n += 1
                            oc = og * 2 + mo
                            kb.wait(v, "mm", M + n)
                            kb.wait(v, "ld", L + n * 16)
                            if n > 2:
                                kb.wait(v, "st", S + (n - 2) * 16)
                                if xbdst is not None:
                                    kb.wait(v, "cb", CB + n - 2)
                            bias = sb["wob"][:, oc:oc + 1] if has_bias else 0.0
                            kb.inc(v.scalar_tensor_tensor(
                                sb["xo"][:, (n % 2) * QG:(n % 2 + 1) * QG],
                                psO2[n % 4][:], bias,
                                sb["xi"][:, (n % 3) * QG:(n % 3 + 1) * QG],
                                ALU.add, ALU.add), "dv")

            @blk.scalar
            def _(sc):
                S2 = kb.v("s2")
                for j in range(1, len(order) + 1):
                    og, qg, mo = order[j - 1]
                    kb.wait(sc, "dv", V + j)
                    kb.inc(sc.dma_start(
                        out=_chunked(xdst)[:, og * 2 + mo, qg * QG:(qg + 1) * QG],
                        in_=sb["xo"][:, (j % 2) * QG:((j % 2) + 1) * QG]), "st", 16)
                    if xbdst is not None:
                        if j >= 3:
                            kb.wait(sc, "s2", S2 + (j - 2) * 16)
                        kb.inc(sc.copy(
                            sb["sq"][:, (j % 2) * QG:(j % 2 + 1) * QG],
                            sb["xo"][:, (j % 2) * QG:((j % 2) + 1) * QG]), "cb")
                        kb.wait(sc, "cb", kb.v("cb"))
                        kb.inc(sc.dma_start(
                            out=_chunked(xbdst)[:, og * 2 + mo,
                                                qg * QG:(qg + 1) * QG],
                            in_=sb["sq"][:, (j % 2) * QG:(j % 2 + 1) * QG]),
                            "s2", 16)
                kb.wait(sc, "st", kb.v("st"))
                if xbdst is not None:
                    kb.wait(sc, "s2", kb.v("s2"))

            @blk.sync
            def _(sy):
                for j in range(1, len(order) + 1):
                    og, qg, mo = order[j - 1]
                    if j > 3:
                        kb.wait(sy, "dv", V + j - 3)
                    kb.inc(sy.dma_start(
                        out=sb["xi"][:, (j % 3) * QG:((j % 3) + 1) * QG],
                        in_=_chunked(xsrc)[:, og * 2 + mo, qg * QG:(qg + 1) * QG]),
                        "ld", 16)
    nc.all_engine_barrier()


def build_launch_B(dbg=False):
    nc = bass.Bass("TRN2", target_bir_lowering=False, num_devices=NCORES)
    ins = {}
    ins["x1"] = nc.dram_tensor("x1", [D, TL], F32, kind="ExternalInput").ap()
    ins["q0"] = nc.dram_tensor("q0", [D, TL], BF16, kind="ExternalInput").ap()
    ins["k0g"] = nc.dram_tensor("k0g", [1024, TL], BF16, kind="ExternalInput").ap()
    ins["v0g"] = nc.dram_tensor("v0g", [8192, 128], BF16, kind="ExternalInput").ap()
    ins["rq_in"] = nc.dram_tensor("rq_in", [1, TL], F32, kind="ExternalInput").ap()
    ins["sched"] = nc.dram_tensor("sched", [1, 128], I32, kind="ExternalInput").ap()
    ins["pads"] = nc.dram_tensor("pads", [128, 64], F32, kind="ExternalInput").ap()
    ins["w_l0wo"] = nc.dram_tensor("w_l0wo", [D, D], F32, kind="ExternalInput").ap()
    ins["wo_b"] = nc.dram_tensor("wo_b", [D, 1], F32, kind="ExternalInput").ap()
    x2 = nc.dram_tensor("x2", [D, TL], F32, kind="ExternalOutput").ap()
    cst = _const_tensors(nc)
    with ExitStack() as ctx:
        kb = KB(nc, ctx)
        alloc_common(kb, attn=True)
        kb.alloc("wob", [128, 32], F32)
        emit_setup(kb, cst, ins, {"masks", "wob", "rq"})
        emit_attn_layer_pre(kb, 0, ins["k0g"], ins["v0g"])
        if dbg:
            dko = nc.dram_tensor("dko", [128, 6144], BF16, kind="ExternalOutput").ap()
            dvo = nc.dram_tensor("dvo", [128, 6144], BF16, kind="ExternalOutput").ap()
            dmisc = nc.dram_tensor("dmisc", [128, 72], F32, kind="ExternalOutput").ap()
        emit_attn(kb, 0, ins["q0"])
        emit_attn_layer_post(kb)
        if dbg:
            dao = nc.dram_tensor("dao", [D, TL], BF16, kind="ExternalOutput").ap()
            with nc.Block(kb.nm("dmp")) as blk:
                @blk.sync
                def _(sy):
                    kb.inc(sy.dma_start(out=dko[:, 0:2048], in_=kb.sb["kloc0"][:]), "st", 16)
                    kb.inc(sy.dma_start(out=dko[:, 2048:6144], in_=kb.sb["kloc1"][:]), "st", 16)
                    kb.inc(sy.dma_start(out=dvo[:, 0:2048], in_=kb.sb["vloc0"][:]), "st", 16)
                    kb.inc(sy.dma_start(out=dvo[:, 2048:6144], in_=kb.sb["vloc1"][:]), "st", 16)
                    kb.inc(sy.dma_start(out=dmisc[0:1, 0:2], in_=kb.sb["k2m"][:]), "st", 16)
                    kb.inc(sy.dma_start(out=dmisc[0:1, 2:4], in_=kb.sb["_dbg_uacc"][0:1, 2:4]), "st", 16)
                    kb.inc(sy.dma_start(out=dmisc[:, 4:68], in_=kb.sb["_dbg_bmat"][:]), "st", 16)
                    kb.inc(sy.dma_start(
                        out=_chunked(dao),
                        in_=kb.sb["xn"][:].rearrange("p (d t) -> p d t", t=TL)), "st", 16)
                    kb.wait(sy, "st", kb.v("st"))
        emit_oproj(kb, 0, ins["w_l0wo"], ins["x1"], x2, True)
    return nc


def _run_launch_B(w, x1s, q0s, k0g, v0g, rqs, scheds, padss, dbg=False):
    nc = build_launch_B(dbg)
    in_maps = []
    for c in range(NCORES):
        in_maps.append({
            "x1": x1s[c], "q0": q0s[c], "k0g": k0g, "v0g": v0g,
            "rq_in": rqs[c], "sched": scheds[c], "pads": padss[c],
            "w_l0wo": w["w_l0wo"], "wo_b": w["b_l0wo"],
        })
    res = run_bass_kernel_spmd(nc, in_maps, core_ids=list(range(NCORES)))
    return res.results


def build_launch_C():
    nc = bass.Bass("TRN2", target_bir_lowering=False, num_devices=NCORES)
    ins = {}
    ins["x2"] = nc.dram_tensor("x2", [D, TL], F32, kind="ExternalInput").ap()
    ins["w_l0ar"] = nc.dram_tensor("w_l0ar", [D, 68], F32, kind="ExternalInput").ap()
    ins["w_l0B"] = nc.dram_tensor("w_l0B", [64, D], F32, kind="ExternalInput").ap()
    ins["w_l1ar"] = nc.dram_tensor("w_l1ar", [D, 144], F32, kind="ExternalInput").ap()
    ins["w_l1B"] = nc.dram_tensor("w_l1B", [1024, D], F32, kind="ExternalInput").ap()
    ins["w_l1base"] = nc.dram_tensor("w_l1base", [D, D], F32, kind="ExternalInput").ap()
    ins["b_l1base"] = nc.dram_tensor("b_l1base", [D, 1], F32, kind="ExternalInput").ap()
    ins["w_l1qkv"] = nc.dram_tensor("w_l1qkv", [D, 6144], F32, kind="ExternalInput").ap()
    x3 = nc.dram_tensor("x3", [D, TL], F32, kind="ExternalOutput").ap()
    x4 = nc.dram_tensor("x4", [D, TL], F32, kind="ExternalOutput").ap()
    q1 = nc.dram_tensor("q1", [D, TL], BF16, kind="ExternalOutput").ap()
    k1c = nc.dram_tensor("k1c", [1024, TL], BF16, kind="ExternalOutput").ap()
    v1c = nc.dram_tensor("v1c", [TL, 1024], BF16, kind="ExternalOutput").ap()
    rq1 = nc.dram_tensor("rq1", [1, TL], F32, kind="ExternalOutput").ap()
    cst = _const_tensors(nc)
    with ExitStack() as ctx:
        kb = KB(nc, ctx)
        alloc_common(kb)
        kb.alloc("sel4", [4, 64], BF16)
        kb.alloc("selp", [16, 8 * 128], BF16)
        kb.alloc("l0ar", [128, NCH * 68], BF16)
        kb.alloc("l0B", [64, D], BF16)
        kb.alloc("l1ar", [128, NCH * 144], BF16)
        kb.alloc("baseb_row", [1, D], BF16)
        kb.alloc("arf", [68, QG], F32)
        kb.alloc("midb", [64, QG], BF16)
        kb.alloc("e16", [128, 16], F32)
        kb.alloc("nm4", [128, 4], F32)
        kb.alloc("gates4", [128, 16], BF16)
        kb.alloc("gT", [4, QG], BF16)
        kb.alloc("wmid", [64, QG], BF16)
        kb.alloc("scrB", [128, 32], F32)
        emit_setup(kb, cst, ins, {"l0ffn", "l1ffn"})
        emit_ln(kb, ins["x2"])
        emit_ffn0(kb, ins["x2"], x3)
        emit_ln(kb, x3)
        emit_ffn1(kb, ins, x3, x4)
        emit_ln(kb, x4)
        emit_qkv(kb, 1, ins, q1, k1c, v1c)
        emit_rq_out(kb, rq1)
    return nc


def _run_launch_C(w, x2s):
    nc = build_launch_C()
    in_maps = []
    for c in range(NCORES):
        in_maps.append({
            "x2": x2s[c],
            "w_l0ar": w["w_l0ar"], "w_l0B": w["w_l0B"],
            "w_l1ar": w["w_l1ar"], "w_l1B": w["w_l1B"],
            "w_l1base": w["w_l1base"], "b_l1base": w["b_l1base"],
            "w_l1qkv": w["w_l1qkv"],
        })
    res = run_bass_kernel_spmd(nc, in_maps, core_ids=list(range(NCORES)))
    return res.results


def build_launch_D():
    nc = bass.Bass("TRN2", target_bir_lowering=False, num_devices=NCORES)
    ins = {}
    ins["x4"] = nc.dram_tensor("x4", [D, TL], F32, kind="ExternalInput").ap()
    ins["q1"] = nc.dram_tensor("q1", [D, TL], BF16, kind="ExternalInput").ap()
    ins["k1g"] = nc.dram_tensor("k1g", [8192, TL], BF16, kind="ExternalInput").ap()
    ins["v1g"] = nc.dram_tensor("v1g", [8192, 1024], BF16, kind="ExternalInput").ap()
    ins["rq_in"] = nc.dram_tensor("rq_in", [1, TL], F32, kind="ExternalInput").ap()
    ins["sched"] = nc.dram_tensor("sched", [1, 128], I32, kind="ExternalInput").ap()
    ins["pads"] = nc.dram_tensor("pads", [128, 64], F32, kind="ExternalInput").ap()
    ins["w_l1wo"] = nc.dram_tensor("w_l1wo", [D, D], F32, kind="ExternalInput").ap()
    x5 = nc.dram_tensor("x5", [D, TL], F32, kind="ExternalOutput").ap()
    cst = _const_tensors(nc)
    with ExitStack() as ctx:
        kb = KB(nc, ctx)
        alloc_common(kb, attn=True)
        emit_setup(kb, cst, ins, {"masks", "rq"})
        emit_attn_layer_pre(kb, 1, ins["k1g"], ins["v1g"])
        emit_attn(kb, 1, ins["q1"])
        emit_attn_layer_post(kb)
        emit_oproj(kb, 1, ins["w_l1wo"], ins["x4"], x5, False)
    return nc


def _run_launch_D(w, x4s, q1s, k1g, v1g, rqs, scheds, padss):
    nc = build_launch_D()
    in_maps = []
    for c in range(NCORES):
        in_maps.append({
            "x4": x4s[c], "q1": q1s[c], "k1g": k1g, "v1g": v1g,
            "rq_in": rqs[c], "sched": scheds[c], "pads": padss[c],
            "w_l1wo": w["w_l1wo"],
        })
    res = run_bass_kernel_spmd(nc, in_maps, core_ids=list(range(NCORES)))
    return res.results


def build_launch_E():
    nc = bass.Bass("TRN2", target_bir_lowering=False, num_devices=NCORES)
    ins = {}
    ins["x5"] = nc.dram_tensor("x5", [D, TL], F32, kind="ExternalInput").ap()
    ins["w_l1ar"] = nc.dram_tensor("w_l1ar", [D, 144], F32, kind="ExternalInput").ap()
    ins["w_l1B"] = nc.dram_tensor("w_l1B", [1024, D], F32, kind="ExternalInput").ap()
    ins["w_l1base"] = nc.dram_tensor("w_l1base", [D, D], F32, kind="ExternalInput").ap()
    ins["b_l1base"] = nc.dram_tensor("b_l1base", [D, 1], F32, kind="ExternalInput").ap()
    xout = nc.dram_tensor("xout", [D, TL], F32, kind="ExternalOutput").ap()
    cst = _const_tensors(nc)
    with ExitStack() as ctx:
        kb = KB(nc, ctx)
        alloc_common(kb)
        kb.alloc("selp", [16, 8 * 128], BF16)
        kb.alloc("l1ar", [128, NCH * 144], BF16)
        kb.alloc("baseb_row", [1, D], BF16)
        kb.alloc("arf", [68, QG], F32)
        kb.alloc("scrB", [128, 32], F32)
        emit_setup(kb, cst, ins, {"l1ffn"})
        emit_ln(kb, ins["x5"])
        emit_ffn1(kb, ins, ins["x5"], xout)
    return nc


def _run_launch_E(w, x5s):
    nc = build_launch_E()
    in_maps = []
    for c in range(NCORES):
        in_maps.append({
            "x5": x5s[c],
            "w_l1ar": w["w_l1ar"], "w_l1B": w["w_l1B"],
            "w_l1base": w["w_l1base"], "b_l1base": w["b_l1base"],
        })
    res = run_bass_kernel_spmd(nc, in_maps, core_ids=list(range(NCORES)))
    return res.results


def build_dbg(stage):
    nc = bass.Bass("TRN2", target_bir_lowering=False, num_devices=NCORES)
    ins = {}
    ins["x"] = nc.dram_tensor("x", [D, TL], F32, kind="ExternalInput").ap()
    ins["w_l0ar"] = nc.dram_tensor("w_l0ar", [D, 68], F32, kind="ExternalInput").ap()
    ins["w_l0B"] = nc.dram_tensor("w_l0B", [64, D], F32, kind="ExternalInput").ap()
    ins["w_l0qkv"] = nc.dram_tensor("w_l0qkv", [D, 4352], F32, kind="ExternalInput").ap()
    ins["b_l0qkv"] = nc.dram_tensor("b_l0qkv", [4352, 1], F32, kind="ExternalInput").ap()
    xn_out = nc.dram_tensor("xn_out", [D, TL], BF16, kind="ExternalOutput").ap()
    gdbg = nc.dram_tensor("gdbg", [128, 64], F32, kind="ExternalOutput").ap()
    x1 = nc.dram_tensor("x1", [D, TL], F32, kind="ExternalOutput").ap()
    cst = _const_tensors(nc)
    with ExitStack() as ctx:
        kb = KB(nc, ctx)
        alloc_common(kb)
        kb.alloc("sel4", [4, 64], BF16)
        kb.alloc("l0ar", [128, NCH * 68], BF16)
        kb.alloc("l0B", [64, D], BF16)
        kb.alloc("l0qkvb", [128, 34], F32)
        kb.alloc("l0vb_row", [1, 128], BF16)
        kb.alloc("arf", [68, QG], F32)
        kb.alloc("midb", [64, QG], BF16)
        kb.alloc("e16", [128, 16], F32)
        kb.alloc("nm4", [128, 4], F32)
        kb.alloc("gates4", [128, 16], BF16)
        kb.alloc("gT", [4, QG], BF16)
        kb.alloc("wmid", [64, QG], BF16)
        kb.alloc("gdbg", [128, 64], F32) if False else kb.sb.update({"gdbg": ctx.enter_context(nc.sbuf_tensor("gdbg_sb", [128, 64], F32))})
        emit_setup(kb, cst, ins, {"l0ffn", "l0qkvb"})
        emit_ln(kb, ins["x"])
        if stage != "ln1":
            emit_ffn0(kb, ins["x"], x1, dbg=True)
        with nc.Block(kb.nm("dump")) as blk:
            @blk.sync
            def _(sy):
                kb.inc(sy.dma_start(
                    out=_chunked(xn_out),
                    in_=kb.sb["xn"][:].rearrange("p (d t) -> p d t", t=TL)), "st", 16)
                kb.inc(sy.dma_start(out=gdbg, in_=kb.sb["gdbg"][:]), "st", 16)
                kb.wait(sy, "st", kb.v("st"))
    return nc


def _run_dbg(w, per_core_x, stage):
    nc = build_dbg(stage)
    in_maps = []
    for c in range(NCORES):
        in_maps.append({
            "x": per_core_x[c],
            "w_l0ar": w["w_l0ar"], "w_l0B": w["w_l0B"],
            "w_l0qkv": w["w_l0qkv"], "b_l0qkv": w["b_l0qkv"],
        })
    res = run_bass_kernel_spmd(nc, in_maps, core_ids=list(range(NCORES)))
    return res.results


def _run_launch_A(w, per_core_x):
    nc = build_launch_A()
    in_maps = []
    for c in range(NCORES):
        in_maps.append({
            "x": per_core_x[c],
            "w_l0ar": w["w_l0ar"], "w_l0B": w["w_l0B"],
            "w_l0qkv": w["w_l0qkv"], "b_l0qkv": w["b_l0qkv"],
        })
    res = run_bass_kernel_spmd(nc, in_maps, core_ids=list(range(NCORES)))
    return res.results


def emit_gather(kb, layer, ccki, ccko, ccvi, ccvo, wait=True, which="both"):
    nc = kb.nc
    with nc.Block(kb.nm("gather")) as blk:
        @blk.gpsimd
        def _(g):
            if which in ("both", "k"):
                kb.inc(g.collective_compute(
                    "AllGather", mybir.AluOpType.bypass,
                    replica_groups=[list(range(NCORES))],
                    ins=[ccki], outs=[ccko]), "cc")
            if which in ("both", "v"):
                kb.inc(g.collective_compute(
                    "AllGather", mybir.AluOpType.bypass,
                    replica_groups=[list(range(NCORES))],
                    ins=[ccvi], outs=[ccvo]), "cc")
            if wait:
                kb.wait(g, "cc", kb.v("cc"))
    if wait:
        nc.all_engine_barrier()


def build_fused(upto=99):
    nc = bass.Bass("TRN2", target_bir_lowering=False, num_devices=NCORES)
    ins = {}
    ins["x"] = nc.dram_tensor("x", [D, TL], F32, kind="ExternalInput").ap()
    for name, shape in (("w_l0ar", [D, 68]), ("w_l0B", [64, D]),
                        ("w_l0qkv", [D, 4352]), ("b_l0qkv", [4352, 1]),
                        ("w_l0wo", [D, D]), ("wo_b", [D, 1]),
                        ("w_l1qkv", [D, 6144]), ("w_l1wo", [D, D]),
                        ("w_l1base", [D, D]), ("b_l1base", [D, 1]),
                        ("w_l1ar", [D, 144]), ("w_l1B", [1024, D]),
                        ("sched", [1, 128]), ("pads", [128, 64])):
        dt = I32 if name == "sched" else F32
        ins[name] = nc.dram_tensor(name, shape, dt, kind="ExternalInput").ap()
    xout = nc.dram_tensor("xout", [D, TL], F32, kind="ExternalOutput").ap()
    xscr = nc.dram_tensor("xscr", [D, TL], F32).ap()
    xbscr = nc.dram_tensor("xbscr", [D, TL], BF16).ap()
    q_scr = nc.dram_tensor("q_scr", [D, TL], BF16).ap()
    ck0i = nc.dram_tensor("ck0i", [128, TL], BF16).ap()
    cv0i = nc.dram_tensor("cv0i", [TL, 128], BF16).ap()
    ck0o = nc.dram_tensor("ck0o", [1024, TL], BF16).ap()
    cv0o = nc.dram_tensor("cv0o", [8192, 128], BF16).ap()
    ck1i = nc.dram_tensor("ck1i", [1024, TL], BF16).ap()
    cv1i = nc.dram_tensor("cv1i", [TL, 1024], BF16).ap()
    ck1o = nc.dram_tensor("ck1o", [8192, TL], BF16).ap()
    cv1o = nc.dram_tensor("cv1o", [8192, 1024], BF16).ap()
    cst = _const_tensors(nc)
    with ExitStack() as ctx:
        kb = KB(nc, ctx)
        alloc_common(kb, attn=True)
        kb.alloc("selp", [16, 8 * 128], BF16)
        kb.alloc("l1ar", [128, NCH * 144], BF16)
        kb.alloc("arf", [68, QG], F32)
        kb.alloc("midb", [64, QG], BF16)
        kb.alloc("e16", [128, 16], F32)
        kb.alloc("nm4", [128, 4], F32)
        kb.alloc("gates4", [128, 16], BF16)
        kb.alloc("gT", [4, QG], BF16)
        kb.alloc("wmid", [64, QG], BF16)
        kb.alloc("scrB", [128, 32], F32)
        l0ctx = ExitStack()
        for nm_, shp, dt_ in (("sel4", [4, 64], BF16),
                              ("l0ar", [128, NCH * 68], BF16),
                              ("l0B", [64, D], BF16),
                              ("l0qkvb", [128, 34], F32),
                              ("l0vb_row", [1, 128], BF16),
                              ("wob", [128, 32], F32)):
            kb.sb[nm_] = l0ctx.enter_context(kb.nc.sbuf_tensor(kb.nm(nm_), shp, dt_))
        emit_setup(kb, cst, ins, {"masks", "l0ffn", "l1ffn", "l0qkvb", "wob"})
        done = False

        def fin(stage):
            nonlocal done
            if done or upto > stage:
                return upto <= stage and done
            with nc.Block(kb.nm("fin")) as blk:
                @blk.sync
                def _(sy):
                    kb.inc(sy.dma_start(out=xout, in_=xscr), "st", 16)
                    kb.wait(sy, "st", kb.v("st"))
            done = True
            return True
        emit_ln(kb, ins["x"])
        emit_ffn0(kb, ins["x"], xscr, xbdst=xbscr)
        if not fin(2):
            emit_ln(kb, xbscr, pre=True)
            emit_qkv(kb, 0, ins, q_scr, ck0i, cv0i, part="k")
            emit_gather(kb, 0, ck0i, ck0o, cv0i, cv0o, wait=False, which="k")
            emit_qkv(kb, 0, ins, q_scr, ck0i, cv0i, part="v")
        if not fin(3):
            emit_gather(kb, 0, ck0i, ck0o, cv0i, cv0o, wait=False, which="v")
            emit_qkv(kb, 0, ins, q_scr, ck0i, cv0i, part="q")
        if not fin(4):
            emit_attn_layer_pre(kb, 0, ck0o, cv0o)
            emit_attn(kb, 0, q_scr)
            emit_attn_layer_post(kb)
        if not fin(5):
            emit_oproj(kb, 0, ins["w_l0wo"], xscr, xscr, True, xbdst=xbscr)
        if not fin(6):
            emit_ln(kb, xbscr, pre=True)
            emit_ffn0(kb, xscr, xscr, xbdst=xbscr)
            l0ctx.close()
        if not fin(7):
            emit_ln(kb, xbscr, pre=True)
            emit_ffn1(kb, ins, xscr, xscr, xbdst=xbscr)
        if not fin(8):
            emit_ln(kb, xbscr, pre=True)
            emit_qkv(kb, 1, ins, q_scr, ck1i, cv1i, part="k")
            emit_gather(kb, 1, ck1i, ck1o, cv1i, cv1o, wait=False, which="k")
            emit_qkv(kb, 1, ins, q_scr, ck1i, cv1i, part="v")
        if not fin(9):
            emit_gather(kb, 1, ck1i, ck1o, cv1i, cv1o, wait=False, which="v")
            emit_qkv(kb, 1, ins, q_scr, ck1i, cv1i, part="q")
        if not fin(10):
            emit_attn_layer_pre(kb, 1, ck1o, cv1o)
            emit_attn(kb, 1, q_scr)
            emit_attn_layer_post(kb)
        if not fin(11):
            emit_oproj(kb, 1, ins["w_l1wo"], xscr, xscr, False, xbdst=xbscr)
        if not fin(12):
            emit_ln(kb, xbscr, pre=True)
            emit_ffn1(kb, ins, xscr, xout)
        if upto <= 6:
            l0ctx.close()
    return nc


def kernel_fused(upto=99, **inputs):
    w, pcx, scheds, padss = _host_prep(inputs)
    nc = build_fused(upto)
    in_maps = []
    for c in range(NCORES):
        m = {"x": pcx[c], "sched": scheds[c], "pads": padss[c]}
        for k in ("w_l0ar", "w_l0B", "w_l0qkv", "b_l0qkv", "w_l0wo",
                  "w_l1qkv", "w_l1wo", "w_l1base", "b_l1base", "w_l1ar", "w_l1B"):
            m[k] = w[k]
        m["wo_b"] = w["b_l0wo"]
        in_maps.append(m)
    res = run_bass_kernel_spmd(nc, in_maps, core_ids=list(range(NCORES)))
    return _reassemble([r["xout"] for r in res.results])


def _reassemble(x6s):
    y = np.zeros((B, T, D), np.float32)
    for c in range(NCORES):
        b, cc, m0, m1 = _core_meta(c)
        xc = np.asarray(x6s[c], np.float32)
        y[b, 512 * m0:512 * m0 + 512, :] = xc[:, 0:512].T
        y[b, 512 * m1:512 * m1 + 512, :] = xc[:, 512:1024].T
    return y


def kernel(**inputs):
    return kernel_fused(**inputs)


def _kernel_unfused(**inputs):
    w, pcx, scheds, padss = _host_prep(inputs)
    asc = np.ascontiguousarray
    resA = _run_launch_A(w, pcx)
    x1s = [asc(np.asarray(r["x1"], np.float32)) for r in resA]
    q0s = [asc(np.asarray(r["q0"])) for r in resA]
    rq0 = [asc(np.asarray(r["rq0"], np.float32)) for r in resA]
    k0g = asc(np.concatenate([np.asarray(r["k0c"]) for r in resA], axis=0))
    v0g = asc(np.concatenate([np.asarray(r["v0c"]) for r in resA], axis=0))
    resB = _run_launch_B(w, x1s, q0s, k0g, v0g, rq0, scheds, padss)
    x2s = [asc(np.asarray(r["x2"], np.float32)) for r in resB]
    resC = _run_launch_C(w, x2s)
    x4s = [asc(np.asarray(r["x4"], np.float32)) for r in resC]
    q1s = [asc(np.asarray(r["q1"])) for r in resC]
    rq1 = [asc(np.asarray(r["rq1"], np.float32)) for r in resC]
    k1g = asc(np.concatenate([np.asarray(r["k1c"]) for r in resC], axis=0))
    v1g = asc(np.concatenate([np.asarray(r["v1c"]) for r in resC], axis=0))
    resD = _run_launch_D(w, x4s, q1s, k1g, v1g, rq1, scheds, padss)
    x5s = [asc(np.asarray(r["x5"], np.float32)) for r in resD]
    resE = _run_launch_E(w, x5s)
    return _reassemble([r["xout"] for r in resE])
